# version 12
# baseline (speedup 1.0000x reference)
import numpy as np
import ml_dtypes
import concourse.bass as bass
import concourse.mybir as mybir
from concourse.bass_utils import run_bass_kernel_spmd
from concourse.bass_core_rust import IndirectOffsetOnAxis

F32 = mybir.dt.float32
BF16 = mybir.dt.bfloat16
AF = mybir.ActivationFunctionType
ALU = mybir.AluOpType
AX = mybir.AxisListType

S, D = 2048, 2048
NQ, NH = 4, 8
NE = 17
STILE = 512
NSt = S // STILE
CEXP = 384
I32 = mybir.dt.int32
EPS = 1e-5
NEG_BIG = -1e30
ENGINES = ("te", "ve", "se", "sy", "gp")
_LAST_PROG = [None]


class Ring:
    def __init__(self, slots):
        self.slots = [{"o": s, "free": None} for s in slots]
        self.i = 0

    def next(self):
        s = self.slots[self.i % len(self.slots)]
        self.i += 1
        return s


class Prog:
    def __init__(self):
        self.ops = {e: [] for e in ENGINES}
        self.ctr = {"te": 0, "ve": 0, "se": 0, "ph": 0}

    def emit(self, eng, fn, waits=(), inc=None, inc_n=1):
        flat = []
        for t in waits:
            if t is None:
                continue
            if isinstance(t, list):
                flat.extend(x for x in t if x is not None)
            else:
                flat.append(t)
        w = tuple(flat)
        tok = None
        if inc is not None:
            self.ctr.setdefault(inc, 0)
            self.ctr[inc] += inc_n
            tok = (inc, self.ctr[inc])
        self.ops[eng].append((fn, w, (inc, inc_n) if inc else None))
        return tok


def MM(out_ap, l_ap, r_ap, start, stop, tp=None):
    if tp is None:
        return lambda t: t.matmul(out_ap, l_ap, r_ap, start=start, stop=stop)
    return lambda t: t.matmul(out_ap, l_ap, r_ap, start=start, stop=stop,
                              tile_position=tp)


def TR(out_ap, in_ap, ident_ap):
    return lambda t: t.transpose(out_ap, in_ap, ident_ap)


def CP(out_ap, in_ap):
    return lambda v: v.tensor_copy(out_ap, in_ap)


def TT_(out_ap, a_ap, b_ap, op):
    return lambda v: v.tensor_tensor(out_ap, a_ap, b_ap, op)


def TS(out_ap, in_ap, s1, s2, op0, op1=None):
    if op1 is None:
        return lambda v: v.tensor_scalar(out_ap, in_ap, s1, None, op0)
    return lambda v: v.tensor_scalar(out_ap, in_ap, s1, s2, op0, op1)


def STT(out_ap, in0_ap, sc, in1_ap, op0, op1):
    return lambda v: v.scalar_tensor_tensor(out_ap, in0_ap, sc, in1_ap, op0, op1)


def ACT(out_ap, in_ap, func, bias=0.0, scale=1.0):
    return lambda s: s.activation(out_ap, in_ap, func, bias=bias, scale=scale)


def DMA(out_ap, in_ap):
    return lambda e: e.dma_start(out=out_ap, in_=in_ap)


def RED(out_ap, in_ap, op):
    return lambda v: v.tensor_reduce(out_ap, in_ap, AX.X, op)


def RECIP(out_ap, in_ap):
    return lambda v: v.reciprocal(out_ap, in_ap)


def MEMSET(ap, val):
    return lambda v: v.memset(ap, val)


def NOP():
    return lambda e: e.nop()


def VNOP():
    return lambda v: v.engine_nop()


def _lhsT_blocks(W):
    dout, din = W.shape
    Ki, nm = din // 128, dout // 128
    A = np.ascontiguousarray(W.T).reshape(Ki, 128, nm, 128)
    A = np.ascontiguousarray(A.transpose(2, 1, 0, 3))
    return A.reshape(nm, 128, Ki * 128)


def _rhs_layout(W):
    dout, din = W.shape
    Ki = din // 128
    B = np.ascontiguousarray(W.T).reshape(Ki, 128, dout).transpose(1, 0, 2)
    return np.ascontiguousarray(B).reshape(128, Ki * dout)


def build_nc(attn_only=False, dbg=None):
    nc = bass.Bass("TRN2", target_bir_lowering=False)
    P = Prog()
    _LAST_PROG[0] = P

    x_d = nc.dram_tensor("x", [S, D], F32, kind="ExternalInput")
    wq_d = nc.dram_tensor("wq", [16, 128, 2048], F32, kind="ExternalInput")
    wk_d = nc.dram_tensor("wk", [4, 128, 2048], F32, kind="ExternalInput")
    wv_d = nc.dram_tensor("wv", [128, 8192], F32, kind="ExternalInput")
    mem_d = nc.dram_tensor("mem", [4, 128, 512], F32, kind="ExternalInput")
    memn_d = nc.dram_tensor("memn", [128, 4], F32, kind="ExternalInput")
    mw_d = nc.dram_tensor("mw", [128, 1], F32, kind="ExternalInput")
    ebias_d = nc.dram_tensor("ebias", [128, 16], F32, kind="ExternalInput")
    rw_d = nc.dram_tensor("rw", [128, 256], F32, kind="ExternalInput")
    gu_d = nc.dram_tensor("gu", [NE, 12, 128, 2048], F32, kind="ExternalInput")
    dw_d = nc.dram_tensor("dw", [NE, 128, 12288], F32, kind="ExternalInput")
    idn16_d = nc.dram_tensor("idn16", [128, 128], BF16, kind="ExternalInput")
    idn32_d = nc.dram_tensor("idn32", [128, 128], F32, kind="ExternalInput")
    x1_d = nc.dram_tensor("x1buf", [S, D], F32, kind="Internal")
    xn2n_d = nc.dram_tensor("xn2nat", [S, D], BF16, kind="Internal")
    idxcw_d = nc.dram_tensor("idxcw", [16 * CEXP, 2], I32, kind="Internal")
    ll_d = nc.dram_tensor("ll", [128, 128], BF16, kind="ExternalInput")
    ones1_d = nc.dram_tensor("ones1", [1, 128], F32, kind="ExternalInput")
    tid_d = nc.dram_tensor("tid", [128, 8], I32, kind="ExternalInput")
    eoff_d = nc.dram_tensor("eoff", [128, 16], F32, kind="ExternalInput")
    y_d = nc.dram_tensor("y", [S, D], F32, kind="ExternalOutput")
    attn_out_d = y_d if attn_only else x1_d

    stp0 = nc.alloc_psum_tensor("stp0", [128, 1024], F32)
    stp1 = nc.alloc_psum_tensor("stp1", [128, 1024], F32)
    pvp0 = nc.alloc_psum_tensor("pvp0", [128, 512], F32)
    pvp1 = nc.alloc_psum_tensor("pvp1", [128, 512], F32)
    pool = Ring([stp0[:, 0:512], stp0[:, 512:1024],
                 stp1[:, 0:512], stp1[:, 512:1024],
                 pvp0[:, :], pvp1[:, :]])
    ptb = Ring([nc.alloc_psum_tensor(f"pt{i}", [128, 1024], BF16) for i in range(2)])

    ARENA = 212736
    arena = nc.alloc_sbuf_tensor("arena", [128, ARENA], mybir.dt.uint8)
    base = nc.lookup_mloc(arena).addr
    cursor = [0]

    def place(name, shape, dtype, align=128):
        nbytes = int(np.prod(shape[1:])) * mybir.dt.size(dtype)
        off = cursor[0]
        cursor[0] = (off + nbytes + align - 1) & ~(align - 1)
        return nc.alloc_sbuf_tensor_at(name, list(shape), dtype,
                                       offset=base + off)

    idn16 = place("idn16s", [128, 128], BF16)
    idn32 = place("idn32s", [128, 128], F32)
    sq = place("sq", [128, 2048], F32)
    tiny = place("tiny", [128, 16], F32)
    small = place("small", [128, 16], F32)
    wsig = place("wsig", [128, 2], F32)
    epsc = place("epsc", [128, 1], F32)
    region0 = cursor[0]

    wbuf = place("wbuf", [128, 16384], BF16)
    memB = place("memB", [128, 2048], BF16)
    memn4 = place("memn4", [128, 4], BF16)
    kT = place("kT", [128, 4 * 2048], BF16)
    v_ext = place("v_ext", [128, 16 * 520 + 128], BF16)
    xn1T = place("xn1T", [128, 16 * 512], BF16)
    qT = place("qT", [128, 16 * 512], BF16)
    sgT = place("sgT", [128, 16 * 512], BF16)
    acc = place("acc", [128, 4 * 2048], F32)
    xn1c = place("xn1c", [128, 2048], BF16)
    oT_sb = place("oT_sb", [128, 1024], F32)
    amT_sb = place("amT_sb", [128, 4 * 512], F32)
    amden = place("amden", [1, 512], F32)
    lnm = place("lnm", [128, 16], F32)
    lnr = place("lnr", [128, 16], F32)
    ptc = Ring([place(f"ptc{i}", [128, 1024], BF16) for i in range(3)])
    xring = Ring([place(f"xr{i}", [128, 2048], F32) for i in range(2)])
    attn_end = cursor[0]
    assert attn_end <= ARENA, f"attn SBUF over: {attn_end}"

    cursor[0] = region0
    y_acc = place("y_acc", [128, 8 * 2048], F32)
    xn2T_off = cursor[0]
    xn2T = place("xn2T", [128, 16 * 1024], BF16)
    xe_nat = nc.alloc_sbuf_tensor_at("xe_nat", [128, 3 * 2048], BF16,
                                     offset=base + xn2T_off)
    xeT = nc.alloc_sbuf_tensor_at("xeT", [128, 16 * CEXP], BF16,
                                  offset=base + xn2T_off + 3 * 2048 * 2)
    actB = place("actB", [128, 6 * 1024], BF16)
    y_sp = place("y_sp", [128, 2 * 2048], F32)
    m_bf = place("m_bf", [128, 128], BF16)
    Pn = place("Pn", [128, 128], F32)
    trow0 = place("trow0", [1, 128], F32)
    trow1 = place("trow1", [1, 128], F32)
    onesc = place("onesc", [128, 1], BF16)
    posf = place("posf", [128, 4], F32)
    cwf = place("cwf", [128, 4], F32)
    posi = place("posi", [128, 4], I32)
    tk2 = place("tk2", [128, 8], I32)
    tksb = [place(f"tksb{i}", [128, 6], I32) for i in range(2)]
    ixc = [[place(f"ixc{i}_{j}", [128, 1], I32, align=32) for j in range(3)]
           for i in range(2)]
    Ll = place("Lls", [128, 128], BF16)
    ones1 = place("ones1s", [1, 128], F32)
    tid = place("tids", [128, 8], I32)
    eoffC = place("eoffs", [128, 16], F32)
    wdr = Ring([place(f"wd{i}", [128, 6 * 2048], BF16) for i in range(2)])
    gur = Ring([place(f"gur{i}", [128, 2048], BF16) for i in range(3)])
    xn2c = place("xn2c", [128, 2048], BF16)
    sgbuf = place("sgbuf", [128, 1024], F32)
    scores = place("scores", [128, 8 * 16], F32)
    combine = place("combine", [128, 8 * 16], F32)
    ft = place("ft", [128, 96], F32)
    ebias_s = place("ebias_s", [128, 16], F32)
    rwT = place("rwT", [128, 256], BF16)
    m2 = place("m2", [128, 8], F32)
    r2 = place("r2", [128, 8], F32)
    ffn_end = cursor[0]
    assert ffn_end <= ARENA, f"ffn SBUF over: {ffn_end}"

    elu1, elu2, elu3 = sq[:, 0:512], sq[:, 512:1024], sq[:, 1024:1536]

    ln_prev = {0: None, 1: None}

    def ln_stats(xc_ap, mcol_ap, rcol_ap, deps, par=0):
        B = 8 * par
        t0 = P.emit("ve", RED(tiny[:, B:B + 1], xc_ap, ALU.add),
                    waits=list(deps) + [ln_prev[par]], inc="ve")
        t1 = P.emit("se", (lambda o=sq[:, :], a=xc_ap,
                           acc=tiny[:, B + 1:B + 2]:
                           (lambda s_: s_.activation(
                               o, a, AF.Square, accum_out=acc)))(),
                    waits=list(deps) + [ln_prev[par]], inc="se")
        t2 = P.emit("ve", TS(mcol_ap, tiny[:, B:B + 1], 1.0 / D, None,
                             ALU.mult), waits=[t0], inc="ve")
        t1b = P.emit("ve", TS(tiny[:, B + 1:B + 2], tiny[:, B + 1:B + 2],
                              1.0 / D, None, ALU.mult), waits=[t1], inc="ve")
        t4 = P.emit("ve", TT_(tiny[:, B + 2:B + 3], mcol_ap, mcol_ap,
                              ALU.mult), waits=[t2], inc="ve")
        tv = P.emit("ve", TT_(tiny[:, B + 1:B + 2], tiny[:, B + 1:B + 2],
                              tiny[:, B + 2:B + 3], ALU.subtract),
                    waits=[t1b, t4], inc="ve")
        tl = P.emit("se", ACT(tiny[:, B + 3:B + 4], tiny[:, B + 1:B + 2],
                              AF.Ln, bias=epsc[:, 0:1]), waits=[tv], inc="se")
        ts = P.emit("se", ACT(rcol_ap, tiny[:, B + 3:B + 4], AF.Exp,
                              scale=-0.5), waits=[tl], inc="se")
        ln_prev[par] = ts
        return ts

    def transpose_to_T(src_chunk, dstT, dst_col, tok_w, deps):
        last = None
        for half in range(2):
            b = ptb.next()
            tok = None
            for j in range(8):
                k = half * 8 + j
                w = (list(deps) + [b["free"]]) if j == 0 else []
                tok = P.emit("te", TR(b["o"][:, j * 128:(j + 1) * 128],
                                      src_chunk[:, k * 128:(k + 1) * 128],
                                      idn16[:, :]),
                             waits=w, inc="te" if j == 7 else None)
            src = b["o"][:, 0:1024].rearrange("p (k t) -> p k t", t=128)
            dst = dstT[:, :].rearrange("p (k t) -> p k t", t=tok_w)
            last = P.emit("ve", CP(dst[:, half * 8:half * 8 + 8,
                                       dst_col:dst_col + 128], src),
                          waits=[tok], inc="ve")
            b["free"] = last
        return last

    def barrier():
        base = P.ctr["ph"]
        qtot = [(k, v) for k, v in P.ctr.items() if k.startswith("q")]
        P.emit("sy", NOP(), waits=qtot, inc="ph")
        P.emit("gp", NOP(), waits=qtot, inc="ph")
        for eng in ("te", "ve", "se"):
            P.emit(eng, NOP(), inc="ph")
        for eng in ENGINES:
            P.emit(eng, NOP(), waits=[("ph", base + 5)])

    P.emit("sy", DMA(idn16[:, :], idn16_d.ap()), inc="qcs", inc_n=16)
    P.emit("sy", DMA(idn32[:, :], idn32_d.ap()), inc="qcs", inc_n=16)
    P.emit("sy", DMA(small[:, 0:1], mw_d.ap()), inc="qcs", inc_n=16)
    for m in range(4):
        P.emit("gp", DMA(wbuf[:, m * 2048:(m + 1) * 2048], wk_d.ap()[m]),
               inc="qcg", inc_n=16)
    P.emit("gp", DMA(wbuf[:, 8192:16384], wv_d.ap()), inc="qcg", inc_n=16)
    for m in range(4):
        P.emit("gp", DMA(memB[:, m * 512:(m + 1) * 512], mem_d.ap()[m]),
               inc="qcg", inc_n=16)
    P.emit("gp", DMA(memn4[:, :], memn_d.ap()), inc="qcg", inc_n=16)
    consts_done = [("qcs", P.ctr["qcs"]), ("qcg", P.ctr["qcg"])]
    t_idn16 = t_idn32 = t_mw = t_wk = t_wv = t_mem = t_memn = consts_done

    P.emit("ve", MEMSET(epsc[:, 0:1], EPS))
    t1 = P.emit("se", ACT(small[:, 1:2], small[:, 0:1], AF.Exp, scale=-1.0),
                waits=[t_mw], inc="se")
    t2 = P.emit("ve", TS(small[:, 2:3], small[:, 1:2], 1.0, None, ALU.add),
                waits=[t1], inc="ve")
    t3 = P.emit("ve", RECIP(wsig[:, 0:1], small[:, 2:3]), waits=[t2], inc="ve")
    P.emit("ve", TS(wsig[:, 1:2], wsig[:, 0:1], -1.0, 1.0, ALU.mult, ALU.add),
           waits=[t3], inc="ve")

    P.emit("ve", MEMSET(v_ext[:, 16 * 520:16 * 520 + 128], 0.0))
    ve_view = v_ext[:, 0:16 * 520].rearrange("p (t c) -> p t c", c=520)
    for h in range(NH):
        P.emit("ve", MEMSET(ve_view[:, :, h * 65 + 64:h * 65 + 65], 1.0))
    t_ones = P.emit("ve", VNOP(), inc="ve")

    def dbg_finish(pairs):
        for r0, nr, src in pairs:
            c = nr // 128
            dst = y_d.ap()[r0:r0 + nr, :].rearrange("(c p) f -> p c f", c=c)
            sview = src.rearrange("p (c f) -> p c f", c=c)
            P.emit("gp", DMA(dst, sview), waits=[("te", P.ctr["te"]),
                                                 ("ve", P.ctr["ve"]),
                                                 ("se", P.ctr["se"])],
                   inc="qdbg", inc_n=16)
        P.emit("gp", NOP(), waits=[("qdbg", P.ctr.get("qdbg", 0))])
        _realize(nc, P)
        return nc

    if dbg == "const":
        return dbg_finish([])

    for tile in range(4):
        drain_toks = []
        for lc in range(4):
            ci = tile * 4 + lc
            xs = xring.next()
            xsem = f"qx{ci % 2}"
            t_x = P.emit("sy", DMA(xs["o"][:, :],
                                   x_d.ap()[ci * 128:(ci + 1) * 128, :]),
                         waits=[xs["free"]], inc=xsem, inc_n=16)
            ts = ln_stats(xs["o"][:, :], lnm[:, ci:ci + 1], lnr[:, ci:ci + 1],
                          [t_x], par=ci % 2)
            t_n = P.emit("ve", TS(xn1c[:, :], xs["o"][:, :], lnm[:, ci:ci + 1],
                                  lnr[:, ci:ci + 1], ALU.subtract, ALU.mult),
                         waits=[ts], inc="ve")
            xs["free"] = t_n
            drain_toks.append(
                transpose_to_T(xn1c[:, :], xn1T, lc * 128, 512, [t_n]))
        for m in range(4):
            b = pool.next()
            tok = None
            for ki in range(16):
                w = ([t_wk, b["free"]] + drain_toks) if ki == 0 else []
                tok = P.emit("te", MM(
                    b["o"][:, :],
                    wbuf[:, m * 2048 + ki * 128:m * 2048 + ki * 128 + 128],
                    xn1T[:, ki * 512:(ki + 1) * 512], ki == 0, ki == 15),
                    waits=w, inc="te" if ki == 15 else None)
            b["free"] = P.emit("ve", CP(
                kT[:, m * 2048 + tile * 512:m * 2048 + tile * 512 + 512],
                b["o"][:, :]), waits=[tok], inc="ve")
        for mt in range(4):
            b = pool.next()
            tc = tile * 4 + mt
            tok = None
            for ki in range(16):
                w = ([t_wv, b["free"]] + drain_toks) if ki == 0 else []
                tok = P.emit("te", MM(
                    b["o"][:, :],
                    xn1T[:, ki * 512 + mt * 128:ki * 512 + mt * 128 + 128],
                    wbuf[:, 8192 + ki * 512:8192 + (ki + 1) * 512],
                    ki == 0, ki == 15),
                    waits=w, inc="te" if ki == 15 else None)
            lastd = None
            for h in range(NH):
                lastd = P.emit("ve", CP(
                    v_ext[:, tc * 520 + h * 65:tc * 520 + h * 65 + 64],
                    b["o"][:, h * 64:(h + 1) * 64]),
                    waits=[tok, t_ones] if h == 0 else [],
                    inc="ve" if h == NH - 1 else None)
            b["free"] = lastd

    barrier()

    if dbg == "pass1":
        for src, r0, c0, w_ in ((lnm, 1024, 0, 16), (lnr, 1024, 16, 16)):
            P.emit("gp", DMA(y_d.ap()[r0:r0 + 128, c0:c0 + w_], src[:, 0:w_]),
                   waits=[("te", P.ctr["te"]), ("ve", P.ctr["ve"]),
                          ("se", P.ctr["se"])], inc="qdbg", inc_n=16)
        P.emit("gp", DMA(y_d.ap()[1152:1280, :], xn1c[:, :]),
               inc="qdbg", inc_n=16)
        return dbg_finish([(0, 512, kT[:, :]), (512, 512, xn1T[:, :])])

    wq_ring = Ring([i * 2048 for i in range(4)])
    prev_te_end = None
    acc_war = None
    for st in range(NSt):
        acc_ready = []
        for lc in range(4):
            ci = st * 4 + lc
            acc_ready.append(P.emit("sy", DMA(
                acc[:, lc * 2048:(lc + 1) * 2048],
                x_d.ap()[ci * 128:(ci + 1) * 128, :]),
                waits=[acc_war], inc=f"qa{lc}", inc_n=16))
        xn1_toks = []
        for lc in range(4):
            ci = st * 4 + lc
            t_n = P.emit("ve", TS(xn1c[:, :], acc[:, lc * 2048:(lc + 1) * 2048],
                                  lnm[:, ci:ci + 1], lnr[:, ci:ci + 1],
                                  ALU.subtract, ALU.mult),
                         waits=[acc_ready[lc], prev_te_end], inc="ve")
            xn1_toks.append(transpose_to_T(xn1c[:, :], xn1T, lc * 128, 512, [t_n]))
        prev_elu_se = None
        prev_elu_ve = None
        for m in range(16):
            slot = wq_ring.next()
            col = slot["o"]
            t_w = P.emit("gp", DMA(wbuf[:, col:col + 2048], wq_d.ap()[m]),
                         waits=[slot["free"]], inc=f"qw{m % 4}", inc_n=16)
            b = pool.next()
            tok = None
            for ki in range(16):
                w = ([t_w, b["free"]] + xn1_toks) if ki == 0 else []
                tok = P.emit("te", MM(
                    b["o"][:, :],
                    wbuf[:, col + ki * 128:col + ki * 128 + 128],
                    xn1T[:, ki * 512:(ki + 1) * 512], ki == 0, ki == 15),
                    waits=w, inc="te" if ki == 15 else None)
            slot["free"] = tok
            P.emit("ve", CP(qT[:, m * 512:(m + 1) * 512], b["o"][:, :]),
                   waits=[tok])
            t_mn = P.emit("ve", TS(elu1, b["o"][:, :], 0.0, None, ALU.min),
                          waits=[prev_elu_se], inc="ve")
            t_e = P.emit("se", ACT(elu2, elu1, AF.Exp),
                         waits=[t_mn, prev_elu_ve], inc="se")
            prev_elu_se = t_e
            t_mx = P.emit("ve", TS(elu3, b["o"][:, :], 0.0, None, ALU.max),
                          waits=[prev_elu_ve], inc="ve")
            t_s = P.emit("ve", STT(sgT[:, m * 512:(m + 1) * 512], elu2, -1.0,
                                   elu3, ALU.add, ALU.add),
                         waits=[t_e, t_mx], inc="ve")
            prev_elu_ve = t_s
            b["free"] = t_s
        q_done = P.emit("ve", VNOP(), inc="ve")

        stpt = [{"o": stp0, "free": [pool.slots[0]["free"], pool.slots[1]["free"]]},
                {"o": stp1, "free": [pool.slots[2]["free"], pool.slots[3]["free"]]}]
        pvt = [{"o": pvp0, "free": [pool.slots[4]["free"]]},
               {"o": pvp1, "free": [pool.slots[5]["free"]]}]
        first_heads = [q_done]
        for qg in range(NQ):
            for hp in range(4):
                h0, h1 = 2 * hp, 2 * hp + 1
                mcol = (qg * 4 + hp) * 512
                pend = None

                def emit_pv(pd, h0=h0, h1=h1):
                    j, pc, tex = pd
                    t0_ = P.emit("te", MM(
                        pvt[0]["o"][:, :],
                        v_ext[:, j * 520 + h0 * 65:j * 520 + h0 * 65 + 128],
                        pc["o"][:, 0:512], j == 0, j == 15),
                        waits=[tex] + (pvt[0]["free"] if j == 0 else []),
                        inc="te" if j == 15 else None)
                    t1_ = P.emit("te", MM(
                        pvt[1]["o"][:, :],
                        v_ext[:, j * 520 + h1 * 65:j * 520 + h1 * 65 + 128],
                        pc["o"][:, 512:1024], j == 0, j == 15),
                        waits=(pvt[1]["free"] if j == 0 else []),
                        inc="te" if j == 15 else None)
                    pc["free"] = t1_ if j == 15 else t0_
                    return t0_, t1_

                for i in range(16):
                    sp = stpt[i % 2]
                    w = [sp["free"]]
                    if qg == 0 and hp == 0 and i < 2:
                        w = w + first_heads
                    P.emit("te", MM(
                        sp["o"][:, 0:512],
                        kT[0:64, hp * 2048 + i * 128:hp * 2048 + i * 128 + 128],
                        qT[0:64, mcol:mcol + 512], True, True, tp=(0, 0)),
                        waits=w)
                    tqk1 = P.emit("te", MM(
                        sp["o"][:, 512:1024],
                        kT[64:128, hp * 2048 + i * 128:hp * 2048 + i * 128 + 128],
                        qT[64:128, mcol:mcol + 512], True, True, tp=(64, 0)),
                        inc="te")
                    pc = ptc.next()
                    tex = P.emit("se", ACT(pc["o"][:, 0:1024], sp["o"][:, 0:1024],
                                           AF.Exp, scale=0.125),
                                 waits=[tqk1, pc["free"]], inc="se")
                    sp["free"] = [tex]
                    if pend is not None:
                        emit_pv(pend)
                    pend = (i, pc, tex)
                tpv0, tpv1 = emit_pv(pend)

                td0 = P.emit("ve", CP(oT_sb[0:65, 0:512], pvt[0]["o"][0:65, :]),
                             waits=[tpv0], inc="ve")
                td1 = P.emit("ve", CP(oT_sb[0:65, 512:1024], pvt[1]["o"][0:65, :]),
                             waits=[tpv1], inc="ve")
                ta0 = ta1 = None
                for sc in range(4):
                    tt0 = P.emit("te", TR(pvt[0]["o"][:, 0:65],
                                          oT_sb[0:65, sc * 128:sc * 128 + 128],
                                          idn32[0:65, 0:65]),
                                 waits=[td0, t_idn32] + ([ta0] if ta0 else []),
                                 inc="te")
                    tt1 = P.emit("te", TR(
                        pvt[1]["o"][:, 0:65],
                        oT_sb[0:65, 512 + sc * 128:512 + sc * 128 + 128],
                        idn32[0:65, 0:65]),
                        waits=[td1] + ([ta1] if ta1 else []), inc="te")
                    tr = P.emit("ve", RECIP(small[:, 4:5], pvt[0]["o"][:, 64:65]),
                                waits=[tt0], inc="ve")
                    tw_ = P.emit("ve", TT_(small[:, 5:6], small[:, 4:5],
                                           wsig[:, 1:2], ALU.mult),
                                 waits=[tr], inc="ve")
                    c0 = sc * 2048 + qg * 512 + h0 * 64
                    ta0 = P.emit("ve", STT(acc[:, c0:c0 + 64],
                                           pvt[0]["o"][:, 0:64],
                                           small[:, 5:6], acc[:, c0:c0 + 64],
                                           ALU.mult, ALU.add),
                                 waits=[tw_], inc="ve")
                    tr1 = P.emit("ve", RECIP(small[:, 6:7], pvt[1]["o"][:, 64:65]),
                                 waits=[tt1], inc="ve")
                    tw1 = P.emit("ve", TT_(small[:, 7:8], small[:, 6:7],
                                           wsig[:, 1:2], ALU.mult),
                                 waits=[tr1], inc="ve")
                    c1 = sc * 2048 + qg * 512 + h1 * 64
                    ta1 = P.emit("ve", STT(acc[:, c1:c1 + 64],
                                           pvt[1]["o"][:, 0:64],
                                           small[:, 7:8], acc[:, c1:c1 + 64],
                                           ALU.mult, ALU.add),
                                 waits=[tw1], inc="ve")
                pvt[0]["free"] = [ta0]
                pvt[1]["free"] = [ta1]

        pool.slots[0]["free"] = stpt[0]["free"][0]
        pool.slots[1]["free"] = stpt[0]["free"][0]
        pool.slots[2]["free"] = stpt[1]["free"][0]
        pool.slots[3]["free"] = stpt[1]["free"][0]
        pool.slots[4]["free"] = pvt[0]["free"][0]
        pool.slots[5]["free"] = pvt[1]["free"][0]
        pool.i = 0

        for qg in range(NQ):
            bD = pool.next()
            tokD = None
            for ii in range(4):
                w = [q_done, t_memn, bD["free"]] if ii == 0 else []
                tokD = P.emit("te", MM(
                    bD["o"][0:1, :], memn4[:, ii:ii + 1],
                    sgT[:, (qg * 4 + ii) * 512:(qg * 4 + ii + 1) * 512],
                    ii == 0, ii == 3),
                    waits=w, inc="te" if ii == 3 else None)
            jtoks = []
            for jj in range(4):
                b = pool.next()
                tok = None
                for ii in range(4):
                    w = [t_mem, b["free"]] if ii == 0 else []
                    tok = P.emit("te", MM(
                        b["o"][:, :],
                        memB[:, jj * 512 + ii * 128:jj * 512 + ii * 128 + 128],
                        sgT[:, (qg * 4 + ii) * 512:(qg * 4 + ii + 1) * 512],
                        ii == 0, ii == 3),
                        waits=w, inc="te" if ii == 3 else None)
                t_am = P.emit("ve", CP(amT_sb[:, jj * 512:(jj + 1) * 512],
                                       b["o"][:, :]), waits=[tok], inc="ve")
                b["free"] = t_am
                jtoks.append(t_am)
            t_dn = P.emit("ve", CP(amden[0:1, :], bD["o"][0:1, :]),
                          waits=[tokD], inc="ve")
            bD["free"] = t_dn
            bDn = pool.next()
            prev_dr = None
            for sc in range(4):
                w = [t_dn] + ([bDn["free"]] if sc == 0 else [prev_dr])
                tsp = P.emit("te", TR(bDn["o"][:, 0:1],
                                      amden[0:1, sc * 128:(sc + 1) * 128],
                                      idn32[0:1, 0:1]),
                             waits=w, inc="te")
                prev_dr = P.emit("ve", CP(tiny[:, 10 + sc:11 + sc],
                                          bDn["o"][:, 0:1]),
                                 waits=[tsp], inc="ve")
            tdn = prev_dr
            last_sc = None
            for sc in range(4):
                trc = P.emit("ve", RECIP(tiny[:, 8:9], tiny[:, 10 + sc:11 + sc]),
                             waits=[tdn], inc="ve")
                tww = P.emit("ve", TT_(tiny[:, 9:10], tiny[:, 8:9], wsig[:, 0:1],
                                       ALU.mult), waits=[trc], inc="ve")
                for jj in range(4):
                    bA2 = pool.next()
                    ttj = P.emit("te", TR(
                        bA2["o"][:, 0:128],
                        amT_sb[:, jj * 512 + sc * 128:jj * 512 + sc * 128 + 128],
                        idn32[:, :]),
                        waits=[jtoks[jj], bA2["free"]], inc="te")
                    cc = sc * 2048 + qg * 512 + jj * 128
                    last_sc = P.emit("ve", STT(
                        acc[:, cc:cc + 128], bA2["o"][:, 0:128], tiny[:, 9:10],
                        acc[:, cc:cc + 128], ALU.mult, ALU.add),
                        waits=[ttj, tww], inc="ve")
                    bA2["free"] = last_sc
            bDn["free"] = tdn

        t_all = P.emit("ve", VNOP(), inc="ve")
        for lc in range(4):
            ci = st * 4 + lc
            P.emit("sy", DMA(attn_out_d.ap()[ci * 128:(ci + 1) * 128, :],
                             acc[:, lc * 2048:(lc + 1) * 2048]),
                   waits=[t_all], inc="qs", inc_n=16)
        acc_war = ("qs", P.ctr["qs"])
        prev_te_end = P.emit("te", NOP(), inc="te")

    attn_store_ctr = P.ctr["qs"]

    if attn_only:
        P.emit("sy", NOP(), waits=[("qs", P.ctr["qs"])])
        _realize(nc, P)
        return nc

    barrier()

    P.emit("sy", DMA(ebias_s[:, :], ebias_d.ap()), inc="qc2s", inc_n=16)
    P.emit("gp", DMA(rwT[:, :], rw_d.ap()), inc="qc2g", inc_n=16)
    P.emit("sy", DMA(Ll[:, :], ll_d.ap()), inc="qc2s", inc_n=16)
    P.emit("sy", DMA(ones1[0:1, :], ones1_d.ap()), inc="qc2s", inc_n=16)
    P.emit("sy", DMA(tid[:, :], tid_d.ap()), inc="qc2s", inc_n=16)
    P.emit("sy", DMA(eoffC[:, :], eoff_d.ap()), inc="qc2s", inc_n=16)
    t_eb = t_rw = [("qc2s", P.ctr["qc2s"]), ("qc2g", P.ctr["qc2g"])]
    t_onc = P.emit("ve", MEMSET(onesc[:, :], 1.0), inc="ve")
    oob_view = y_sp[:, 0:96].bitcast(I32)

    prev_y_store = ("qs", attn_store_ctr)
    prev_te_end2 = None
    prev_idx_load = None
    prev_act_te = None
    prev_sg = [None, None]
    tk_ring = Ring([0, 1, 2, 3])
    tkb_war = [None, None]
    ysp_ring = Ring([0, 1])

    dense_state = {"last_gu": None}

    def dense_branch(e, xn2_toks):
        nonlocal prev_act_te
        wi = wdr.i % 2
        wd_slot = wdr.next()
        t_wd = P.emit("gp", DMA(wd_slot["o"][:, :], dw_d.ap()[e]),
                      waits=[wd_slot["free"]], inc=f"qd{wi}", inc_n=16)
        act_toks = []
        for i in range(6):
            pair = []
            for m in (i, i + 6):
                gi = gur.i % 3
                slot = gur.next()
                t_w = P.emit("gp", DMA(slot["o"][:, :], gu_d.ap()[e, m]),
                             waits=[slot["free"]], inc=f"qgu{gi}", inc_n=16)
                bh = [pool.next(), pool.next()]
                tok = None
                for ki in range(16):
                    for half in range(2):
                        w = []
                        if ki == 0:
                            w = [t_w, bh[half]["free"]]
                            if half == 0:
                                w += xn2_toks
                        tok = P.emit("te", MM(
                            bh[half]["o"][:, :],
                            slot["o"][:, ki * 128:(ki + 1) * 128],
                            xn2T[:, ki * 1024 + half * 512:ki * 1024 + half * 512 + 512],
                            ki == 0, ki == 15),
                            waits=w,
                            inc="te" if (ki == 15 and half == 1) else None)
                slot["free"] = tok
                pair.append((bh, tok))
            dense_state["last_gu"] = pair[1][1]
            (bg, tokg), (bu, toku) = pair
            for half in range(2):
                t_sg = P.emit("se", ACT(sgbuf[:, half * 512:(half + 1) * 512],
                                        bg[half]["o"][:, :], AF.Silu),
                              waits=[tokg, prev_sg[half]], inc="se")
                bg[half]["free"] = t_sg
                t_act = P.emit("ve", TT_(
                    actB[:, i * 1024 + half * 512:i * 1024 + half * 512 + 512],
                    sgbuf[:, half * 512:(half + 1) * 512],
                    bu[half]["o"][:, :], ALU.mult),
                    waits=[t_sg, toku, prev_act_te], inc="ve")
                bu[half]["free"] = t_act
                prev_sg[half] = t_act
                act_toks.append(t_act)
        tok_dn = None
        for tc in range(8):
            for ncol in range(4):
                b = pool.next()
                for ki in range(6):
                    w = ([t_wd, b["free"]] + act_toks) if ki == 0 else []
                    tok_dn = P.emit("te", MM(
                        b["o"][:, :],
                        actB[:, ki * 1024 + tc * 128:ki * 1024 + tc * 128 + 128],
                        wd_slot["o"][:, ki * 2048 + ncol * 512:ki * 2048 + ncol * 512 + 512],
                        ki == 0, ki == 5),
                        waits=w, inc="te" if ki == 5 else None)
                dst = y_acc[:, tc * 2048 + ncol * 512:tc * 2048 + ncol * 512 + 512]
                t_dr = P.emit("ve", STT(dst, b["o"][:, :], 1.0, dst,
                                        ALU.mult, ALU.add),
                              waits=[tok_dn], inc="ve")
                b["free"] = t_dr
        wd_slot["free"] = tok_dn
        prev_act_te = tok_dn

    prev_ys = []
    for p in range(2):
        t_ms = P.emit("ve", MEMSET(oob_view, 1 << 20),
                      waits=prev_ys, inc="ve")
        t_fill = P.emit("sy", DMA(
            idxcw_d.ap().rearrange("(p f) j -> p (f j)", p=128),
            oob_view), waits=[prev_idx_load, t_ms], inc="qif", inc_n=16)
        fill_tok = ("qif", P.ctr["qif"])
        for _sl in ysp_ring.slots:
            _sl["free"] = fill_tok

        xn2_toks = []
        for c in range(8):
            ci = p * 8 + c
            t_x = P.emit("sy", DMA(y_acc[:, c * 2048:(c + 1) * 2048],
                                   x1_d.ap()[ci * 128:(ci + 1) * 128, :]),
                         waits=[prev_y_store] if c == 0 else [],
                         inc=f"qy{c}", inc_n=16)
            ts2 = ln_stats(y_acc[:, c * 2048:(c + 1) * 2048],
                           m2[:, c:c + 1], r2[:, c:c + 1], [t_x], par=c % 2)
            t_n = P.emit("ve", TS(xn2c[:, :], y_acc[:, c * 2048:(c + 1) * 2048],
                                  m2[:, c:c + 1], r2[:, c:c + 1],
                                  ALU.subtract, ALU.mult),
                         waits=[ts2], inc="ve")
            P.emit("sy", DMA(
                xn2n_d.ap()[p * 1024 + c * 128:p * 1024 + (c + 1) * 128, :],
                xn2c[:, :]), waits=[t_n], inc="qxn", inc_n=16)
            xn2_toks.append(transpose_to_T(xn2c[:, :], xn2T, c * 128, 1024,
                                           [t_n, prev_te_end2]))
        xn_ready = ("qxn", P.ctr["qxn"])

        rt_toks = []
        for c in range(8):
            b = pool.next()
            tok = None
            for ki in range(16):
                w = ([t_rw, b["free"]] + xn2_toks) if ki == 0 else []
                tok = P.emit("te", MM(
                    b["o"][:, 0:16],
                    xn2T[:, ki * 1024 + c * 128:ki * 1024 + c * 128 + 128],
                    rwT[:, ki * 16:(ki + 1) * 16], ki == 0, ki == 15),
                    waits=w, inc="te" if ki == 15 else None)
            rt_toks.append((b, tok))

        topk_toks = []
        prev_ft_read = None
        for c in range(8):
            b, tok = rt_toks[c]
            t_ex = P.emit("se", ACT(ft[:, 0:16], b["o"][:, 0:16], AF.Exp,
                                    scale=-1.0), waits=[tok, prev_ft_read],
                          inc="se")
            b["free"] = t_ex
            u = P.emit("ve", TS(ft[:, 16:32], ft[:, 0:16], 1.0, None, ALU.add),
                       waits=[t_ex], inc="ve")
            prev_ft_read = u
            u = P.emit("ve", RECIP(scores[:, c * 16:(c + 1) * 16],
                                   ft[:, 16:32]), waits=[u], inc="ve")
            u = P.emit("ve", TT_(ft[:, 16:32], scores[:, c * 16:(c + 1) * 16],
                                 ebias_s[:, :], ALU.add), waits=[t_eb, u],
                       inc="ve")
            u2 = P.emit("ve", MEMSET(combine[:, c * 16:(c + 1) * 16], 0.0),
                        inc="ve")
            for k in range(4):
                u = P.emit("ve", RED(ft[:, 32:33], ft[:, 16:32], ALU.max),
                           waits=[u], inc="ve")
                u = P.emit("ve", TS(ft[:, 48:64], ft[:, 16:32], ft[:, 32:33],
                                    None, ALU.is_equal), waits=[u], inc="ve")
                u = P.emit("ve", TT_(ft[:, 64:80], ft[:, 48:64],
                                     scores[:, c * 16:(c + 1) * 16], ALU.mult),
                           waits=[u], inc="ve")
                u = P.emit("ve", TT_(combine[:, c * 16:(c + 1) * 16],
                                     combine[:, c * 16:(c + 1) * 16],
                                     ft[:, 64:80], ALU.add),
                           waits=[u, u2], inc="ve")
                u = P.emit("ve", STT(ft[:, 16:32], ft[:, 48:64], NEG_BIG,
                                     ft[:, 16:32], ALU.mult, ALU.add),
                           waits=[u], inc="ve")
            u = P.emit("ve", RED(ft[:, 33:34], combine[:, c * 16:(c + 1) * 16],
                                 ALU.add), waits=[u], inc="ve")
            u = P.emit("ve", TS(ft[:, 34:35], ft[:, 33:34], 1e-20, None,
                                ALU.add), waits=[u], inc="ve")
            u = P.emit("ve", RECIP(ft[:, 35:36], ft[:, 34:35]), waits=[u],
                       inc="ve")
            tk = P.emit("ve", TS(combine[:, c * 16:(c + 1) * 16],
                                 combine[:, c * 16:(c + 1) * 16],
                                 ft[:, 35:36], None, ALU.mult), waits=[u],
                        inc="ve")
            topk_toks.append(tk)

        dense_branch(16, xn2_toks)
        t_shared_done = P.emit("ve", VNOP(), inc="ve")

        for c in range(8):
            ci = p * 8 + c
            P.emit("sy", DMA(y_d.ap()[ci * 128:(ci + 1) * 128, :],
                             y_acc[:, c * 2048:(c + 1) * 2048]),
                   waits=[t_shared_done], inc="qs", inc_n=16)
        y_base_tok = ("qs", P.ctr["qs"])
        prev_y_store = y_base_tok
        t_m = P.emit("ve", TS(m_bf[:, :], combine[:, :], 0.0, None, ALU.is_gt),
                     waits=topk_toks + [t_onc], inc="ve")
        bS = pool.next()
        tS = P.emit("te", MM(bS["o"][0:1, 0:128], onesc[:, 0:1], m_bf[:, :],
                             True, True), waits=[t_m, bS["free"]], inc="te")
        t_r1 = P.emit("ve", CP(trow0[0:1, :], bS["o"][0:1, 0:128]),
                      waits=[tS], inc="ve")
        bS["free"] = t_r1
        u = P.emit("ve", MEMSET(trow1[0:1, :], 0.0), waits=[t_r1], inc="ve")
        u = P.emit("ve", CP(trow1[0:1, 16:128], trow0[0:1, 0:112]),
                   waits=[u], inc="ve")
        bufs = [trow1, trow0]
        for step, sh in enumerate((16, 32, 64)):
            src = bufs[step % 2]
            dst = bufs[(step + 1) % 2]
            u = P.emit("ve", CP(dst[0:1, 0:sh], src[0:1, 0:sh]), waits=[u],
                       inc="ve")
            u = P.emit("ve", TT_(dst[0:1, sh:128], src[0:1, sh:128],
                                 src[0:1, 0:128 - sh], ALU.add), waits=[u],
                       inc="ve")
        pref_buf = trow0
        t_pref = u
        bP = pool.next()
        P.emit("te", MM(bP["o"][:, 0:128], Ll[:, :], m_bf[:, :], True, False),
               waits=[t_m, bP["free"], t_eb], inc=None)
        tP = P.emit("te", MM(bP["o"][:, 0:128], ones1[0:1, :],
                             pref_buf[0:1, :], False, True),
                    waits=[t_pref], inc="te")
        t_Pn = P.emit("ve", CP(Pn[:, :], bP["o"][:, 0:128]), waits=[tP],
                      inc="ve")
        bP["free"] = t_Pn

        for c in range(8):
            cs = slice(c * 16, (c + 1) * 16)
            u = P.emit("ve", TT_(ft[:, 0:16], eoffC[:, :], Pn[:, cs],
                                 ALU.add), waits=[t_Pn], inc="ve")
            u = P.emit("ve", TS(ft[:, 80:96], Pn[:, cs], CEXP - 0.5, None,
                                ALU.is_ge), waits=[u], inc="ve")
            u = P.emit("ve", STT(ft[:, 0:16], ft[:, 80:96], 1.0e6, ft[:, 0:16],
                                 ALU.mult, ALU.add), waits=[u], inc="ve")
            u = P.emit("ve", TT_(ft[:, 16:32], scores[:, cs],
                                 ebias_s[:, :], ALU.add), waits=[u], inc="ve")
            for k in range(4):
                u = P.emit("ve", RED(ft[:, 32:33], ft[:, 16:32], ALU.max),
                           waits=[u], inc="ve")
                u = P.emit("ve", TS(ft[:, 48:64], ft[:, 16:32], ft[:, 32:33],
                                    None, ALU.is_equal), waits=[u], inc="ve")
                si = tk_ring.next()
                sidx = si["o"]
                u = P.emit("ve", TT_(ft[:, 64:80], ft[:, 48:64], ft[:, 0:16],
                                     ALU.mult), waits=[u, si["free"]],
                           inc="ve")
                u = P.emit("ve", RED(posf[:, sidx:sidx + 1], ft[:, 64:80],
                                     ALU.add), waits=[u], inc="ve")
                u = P.emit("ve", TT_(ft[:, 64:80], ft[:, 48:64],
                                     combine[:, cs], ALU.mult),
                           waits=[u], inc="ve")
                u = P.emit("ve", RED(cwf[:, sidx:sidx + 1], ft[:, 64:80],
                                     ALU.add), waits=[u], inc="ve")
                u = P.emit("ve", CP(posi[:, sidx:sidx + 1],
                                    posf[:, sidx:sidx + 1]), waits=[u],
                           inc="ve")
                u = P.emit("ve", TS(tk2[:, 2 * sidx:2 * sidx + 1],
                                    tid[:, c:c + 1], float(p * 1024), None,
                                    ALU.add), waits=[u], inc="ve")
                u = P.emit("ve", CP(
                    tk2[:, 2 * sidx + 1:2 * sidx + 2],
                    cwf[:, sidx:sidx + 1].bitcast(I32)), waits=[u], inc="ve")
                o_ap = idxcw_d.ap()
                off_ap = posi[:, sidx:sidx + 1]
                in_ap = tk2[:, 2 * sidx:2 * sidx + 2]
                tsc = P.emit("gp", (lambda oa=o_ap, fa=off_ap, ia=in_ap:
                             (lambda g: g.indirect_dma_start(
                                 oa, IndirectOffsetOnAxis(fa, 0), ia, None)))(),
                             waits=[u, fill_tok], inc=f"qsc{sidx}", inc_n=16)
                si["free"] = tsc
                u = P.emit("ve", STT(ft[:, 16:32], ft[:, 48:64], NEG_BIG,
                                     ft[:, 16:32], ALU.mult, ALU.add),
                           waits=[u], inc="ve")
        scatters_done = [(f"qsc{i}", P.ctr.get(f"qsc{i}", 0))
                         for i in range(4)]

        prev_gu_read = None
        gstate = {}

        def emit_gather(e):
            par = e % 2
            tkb_cur = tksb[par]
            for cc3 in range(3):
                P.emit("sy", DMA(
                    tkb_cur[:, 2 * cc3:2 * cc3 + 2],
                    idxcw_d.ap()[e * CEXP + cc3 * 128:e * CEXP + (cc3 + 1) * 128, :]),
                    waits=(scatters_done + [tkb_war[par]]) if cc3 == 0 else [],
                    inc=f"qi{par}", inc_n=16)
            tkb_tok = (f"qi{par}", P.ctr[f"qi{par}"])
            tcl = None
            for sc in range(3):
                tcl = P.emit("ve", TS(ixc[par][sc][:, 0:1],
                                      tkb_cur[:, 2 * sc:2 * sc + 1],
                                      float(p * 1024 + 1023), None, ALU.min),
                             waits=[tkb_tok] if sc == 0 else [], inc="ve")
            g_toks = []
            for sc in range(3):
                o_ap = xe_nat[:, sc * 2048:(sc + 1) * 2048]
                in_ap = xn2n_d.ap()
                off_ap = ixc[par][sc][:, 0:1]
                P.emit("gp", (lambda oa=o_ap, ia=in_ap, fa=off_ap:
                       (lambda g: g.indirect_dma_start(
                           oa, None, ia, IndirectOffsetOnAxis(fa, 0))))(),
                       waits=[tcl, xn_ready, prev_gu_read,
                              dense_state["last_gu"] if e == 0 else None],
                       inc=f"qga{sc}", inc_n=16)
                g_toks.append((f"qga{sc}", P.ctr[f"qga{sc}"]))
            gstate[e] = (tkb_cur, tkb_tok, g_toks)

        emit_gather(0)
        for e in range(16):
            par = e % 2
            tkb_cur, tkb_tok, g_toks = gstate.pop(e)
            tr_last = None
            for sc in range(3):
                for kk in range(16):
                    bt = ptb.next()
                    tr_ = P.emit("te", TR(
                        bt["o"][:, 0:128],
                        xe_nat[:, sc * 2048 + kk * 128:sc * 2048 + kk * 128 + 128],
                        idn16[:, :]),
                        waits=[g_toks[sc], bt["free"]], inc="te")
                    dcol = kk * CEXP + sc * 128
                    o_ap = xeT[:, dcol:dcol + 128]
                    i_ap = bt["o"][:, 0:128]
                    tr_last = P.emit("se", (lambda oa=o_ap, ia=i_ap:
                                     (lambda s_: s_.copy(oa, ia)))(),
                                     waits=[tr_], inc="se")
                    bt["free"] = tr_last
            xeT_ready = tr_last
            prev_gu_read = tr_

            act_toks = []
            tok = None
            for i in range(6):
                pair = []
                for m in (i, i + 6):
                    gi = gur.i % 3
                    slot = gur.next()
                    t_w = P.emit("gp", DMA(slot["o"][:, :], gu_d.ap()[e, m]),
                                 waits=[slot["free"]], inc=f"qgu{gi}",
                                 inc_n=16)
                    b = pool.next()
                    for ki in range(16):
                        w = ([t_w, b["free"], xeT_ready] if ki == 0 else [])
                        tok = P.emit("te", MM(
                            b["o"][:, 0:CEXP],
                            slot["o"][:, ki * 128:(ki + 1) * 128],
                            xeT[:, ki * CEXP:(ki + 1) * CEXP],
                            ki == 0, ki == 15),
                            waits=w, inc="te" if ki == 15 else None)
                    slot["free"] = tok
                    pair.append((b, tok))
                (bg, tokg), (bu, toku) = pair
                t_sg = P.emit("se", ACT(sgbuf[:, 0:CEXP], bg["o"][:, 0:CEXP],
                                        AF.Silu),
                              waits=[tokg, prev_sg[0]], inc="se")
                bg["free"] = t_sg
                t_act = P.emit("ve", TT_(
                    actB[:, i * CEXP:(i + 1) * CEXP], sgbuf[:, 0:CEXP],
                    bu["o"][:, 0:CEXP], ALU.mult),
                    waits=[t_sg, toku, prev_act_te], inc="ve")
                bu["free"] = t_act
                prev_sg[0] = t_act
                act_toks.append(t_act)
            if e + 1 < 16:
                emit_gather(e + 1)
            wi = wdr.i % 2
            wd_slot = wdr.next()
            t_wd = P.emit("gp", DMA(wd_slot["o"][:, :], dw_d.ap()[e]),
                          waits=[wd_slot["free"]], inc=f"qd{wi}", inc_n=16)
            tok_dn = None
            last_ys = None
            for sc in range(3):
                ysl = ysp_ring.next()
                ypar = ysl["o"]
                drains = []
                for ncol in range(4):
                    b = pool.next()
                    for ki in range(6):
                        w = ([t_wd, b["free"]] + act_toks) if ki == 0 else []
                        tok_dn = P.emit("te", MM(
                            b["o"][:, :],
                            actB[:, ki * CEXP + sc * 128:ki * CEXP + sc * 128 + 128],
                            wd_slot["o"][:, ki * 2048 + ncol * 512:ki * 2048 + ncol * 512 + 512],
                            ki == 0, ki == 5),
                            waits=w, inc="te" if ki == 5 else None)
                    t_dr = P.emit("ve", TS(
                        y_sp[:, ypar * 2048 + ncol * 512:ypar * 2048 + ncol * 512 + 512],
                        b["o"][:, :],
                        tkb_cur[:, 2 * sc + 1:2 * sc + 2].bitcast(F32),
                        None, ALU.mult),
                        waits=[tok_dn, ysl["free"]], inc="ve")
                    drains.append(t_dr)
                    b["free"] = t_dr
                o_ap = y_d.ap()
                off_ap = tkb_cur[:, 2 * sc:2 * sc + 1]
                i_ap = y_sp[:, ypar * 2048:(ypar + 1) * 2048]
                last_ys = P.emit("gp", (lambda oa=o_ap, fa=off_ap, ia=i_ap:
                                 (lambda g: g.indirect_dma_start(
                                     oa, IndirectOffsetOnAxis(fa, 0), ia, None,
                                     compute_op=ALU.add)))(),
                                 waits=drains + [y_base_tok],
                                 inc=f"qys{ypar}", inc_n=16)
                ysl["free"] = last_ys
            wd_slot["free"] = tok_dn
            prev_act_te = tok_dn
            tkb_war[par] = last_ys

        prev_idx_load = [(f"qi{i}", P.ctr.get(f"qi{i}", 0)) for i in range(2)]
        prev_ys = [(f"qys{i}", P.ctr.get(f"qys{i}", 0)) for i in range(2)]
        prev_te_end2 = prev_act_te

    fin = [(k, v) for k, v in P.ctr.items() if k.startswith("q")]
    P.emit("sy", NOP(), waits=fin)
    P.emit("gp", NOP(), waits=[t for t in fin if t[0].startswith("qys")])
    _realize(nc, P)
    return nc


def _realize(nc, P):
    import contextlib
    with contextlib.ExitStack() as ctx:
        sems = {s: ctx.enter_context(nc.semaphore(s))
                for s in sorted(P.ctr.keys())}
        block = ctx.enter_context(nc.Block())

        def make(eng_name):
            ops = P.ops[eng_name]
            seen = {}

            def body(eng):
                for fn, waits, inc in ops:
                    for s, v in waits:
                        if seen.get(s, -1) < v:
                            eng.wait_ge(sems[s], v)
                            seen[s] = v
                    ins = fn(eng)
                    if inc is not None:
                        ins.then_inc(sems[inc[0]], inc[1])
            return body

        for eng_name, attr in (("te", "tensor"), ("ve", "vector"),
                               ("se", "scalar"), ("sy", "sync"),
                               ("gp", "gpsimd")):
            getattr(block, attr)(make(eng_name))


def prep_inputs(L0_attn_Wq, L0_attn_Wk, L0_attn_Wv, L0_attn_memory,
                L0_attn_memory_norm, L0_attn_memory_weight, L0_ffn_router_w,
                L0_ffn_e_bias, L0_ffn_gate_up_proj, L0_ffn_down_proj,
                L0_ffn_shared_gate_w, L0_ffn_shared_up_w, L0_ffn_shared_down_w):
    f = np.float32
    wq = _lhsT_blocks(np.asarray(L0_attn_Wq, f))
    wk = _lhsT_blocks(np.asarray(L0_attn_Wk, f))
    wv = _rhs_layout(np.asarray(L0_attn_Wv, f))
    mem = _lhsT_blocks(np.asarray(L0_attn_memory, f).T)
    memn = np.ascontiguousarray(
        np.asarray(L0_attn_memory_norm, f).reshape(4, 128, 1)
        .transpose(1, 0, 2).reshape(128, 4))
    mw = np.ascontiguousarray(
        np.broadcast_to(np.asarray(L0_attn_memory_weight, f).reshape(1, 1),
                        (128, 1)))
    ebias = np.ascontiguousarray(
        np.broadcast_to(np.asarray(L0_ffn_e_bias, f).reshape(1, 16), (128, 16)))
    rw = _rhs_layout(np.asarray(L0_ffn_router_w, f))
    gup = np.asarray(L0_ffn_gate_up_proj, f)
    dp = np.asarray(L0_ffn_down_proj, f)
    sh_gu = np.concatenate([np.asarray(L0_ffn_shared_gate_w, f),
                            np.asarray(L0_ffn_shared_up_w, f)], axis=0)
    gu = np.stack([_lhsT_blocks(gup[e]) for e in range(16)]
                  + [_lhsT_blocks(sh_gu)])
    dw = np.stack([_rhs_layout(dp[e]) for e in range(16)]
                  + [_rhs_layout(np.asarray(L0_ffn_shared_down_w, f))])
    idn16 = np.eye(128, dtype=ml_dtypes.bfloat16)
    idn32 = np.eye(128, dtype=np.float32)
    ll = np.triu(np.ones((128, 128), np.float32), 1).astype(ml_dtypes.bfloat16)
    ones1 = np.ones((1, 128), np.float32)
    tid = np.ascontiguousarray(
        (np.arange(8)[None, :] * 128 + np.arange(128)[:, None]).astype(np.int32))
    eoff = np.ascontiguousarray(np.broadcast_to(
        (np.arange(16) * CEXP).astype(np.float32)[None, :], (128, 16)))
    return {"wq": wq, "wk": wk, "wv": wv, "mem": mem, "memn": memn, "mw": mw,
            "ebias": ebias, "rw": rw, "gu": gu, "dw": dw,
            "idn16": idn16, "idn32": idn32, "ll": ll, "ones1": ones1,
            "tid": tid, "eoff": eoff}


def kernel(L0_attn_x, **w):
    common = prep_inputs(**w)
    x = np.ascontiguousarray(np.asarray(L0_attn_x, np.float32))
    nc = build_nc(attn_only=False)
    in_maps = [dict(common, x=np.ascontiguousarray(x[c])) for c in range(8)]
    res = run_bass_kernel_spmd(nc, in_maps, core_ids=list(range(8)))
    return np.stack([np.asarray(r["y"], np.float32) for r in res.results])


# version 13
# speedup vs baseline: 1.0287x; 1.0287x over previous
import numpy as np
import ml_dtypes
import concourse.bass as bass
import concourse.mybir as mybir
from concourse.bass_utils import run_bass_kernel_spmd
from concourse.bass_core_rust import IndirectOffsetOnAxis

F32 = mybir.dt.float32
BF16 = mybir.dt.bfloat16
AF = mybir.ActivationFunctionType
ALU = mybir.AluOpType
AX = mybir.AxisListType

S, D = 2048, 2048
NQ, NH = 4, 8
NE = 17
STILE = 512
NSt = S // STILE
CEXP = 384
I32 = mybir.dt.int32
EPS = 1e-5
NEG_BIG = -1e30
ENGINES = ("te", "ve", "se", "sy", "gp")
_LAST_PROG = [None]


class Ring:
    def __init__(self, slots):
        self.slots = [{"o": s, "free": None} for s in slots]
        self.i = 0

    def next(self):
        s = self.slots[self.i % len(self.slots)]
        self.i += 1
        return s


class Prog:
    def __init__(self):
        self.ops = {e: [] for e in ENGINES}
        self.ctr = {"te": 0, "ve": 0, "se": 0, "ph": 0}

    def emit(self, eng, fn, waits=(), inc=None, inc_n=1):
        flat = []
        for t in waits:
            if t is None:
                continue
            if isinstance(t, list):
                flat.extend(x for x in t if x is not None)
            else:
                flat.append(t)
        w = tuple(flat)
        tok = None
        if inc is not None:
            self.ctr.setdefault(inc, 0)
            self.ctr[inc] += inc_n
            tok = (inc, self.ctr[inc])
        self.ops[eng].append((fn, w, (inc, inc_n) if inc else None))
        return tok


def MM(out_ap, l_ap, r_ap, start, stop, tp=None):
    if tp is None:
        return lambda t: t.matmul(out_ap, l_ap, r_ap, start=start, stop=stop)
    return lambda t: t.matmul(out_ap, l_ap, r_ap, start=start, stop=stop,
                              tile_position=tp)


def TR(out_ap, in_ap, ident_ap):
    return lambda t: t.transpose(out_ap, in_ap, ident_ap)


def CP(out_ap, in_ap):
    return lambda v: v.tensor_copy(out_ap, in_ap)


def TT_(out_ap, a_ap, b_ap, op):
    return lambda v: v.tensor_tensor(out_ap, a_ap, b_ap, op)


def TS(out_ap, in_ap, s1, s2, op0, op1=None):
    if op1 is None:
        return lambda v: v.tensor_scalar(out_ap, in_ap, s1, None, op0)
    return lambda v: v.tensor_scalar(out_ap, in_ap, s1, s2, op0, op1)


def STT(out_ap, in0_ap, sc, in1_ap, op0, op1):
    return lambda v: v.scalar_tensor_tensor(out_ap, in0_ap, sc, in1_ap, op0, op1)


def ACT(out_ap, in_ap, func, bias=0.0, scale=1.0):
    return lambda s: s.activation(out_ap, in_ap, func, bias=bias, scale=scale)


def DMA(out_ap, in_ap):
    return lambda e: e.dma_start(out=out_ap, in_=in_ap)


def RED(out_ap, in_ap, op):
    return lambda v: v.tensor_reduce(out_ap, in_ap, AX.X, op)


def RECIP(out_ap, in_ap):
    return lambda v: v.reciprocal(out_ap, in_ap)


def MEMSET(ap, val):
    return lambda v: v.memset(ap, val)


def NOP():
    return lambda e: e.nop()


def VNOP():
    return lambda v: v.engine_nop()


def _lhsT_blocks(W):
    dout, din = W.shape
    Ki, nm = din // 128, dout // 128
    A = np.ascontiguousarray(W.T).reshape(Ki, 128, nm, 128)
    A = np.ascontiguousarray(A.transpose(2, 1, 0, 3))
    return A.reshape(nm, 128, Ki * 128)


def _rhs_layout(W):
    dout, din = W.shape
    Ki = din // 128
    B = np.ascontiguousarray(W.T).reshape(Ki, 128, dout).transpose(1, 0, 2)
    return np.ascontiguousarray(B).reshape(128, Ki * dout)


def build_nc(attn_only=False, dbg=None):
    nc = bass.Bass("TRN2", target_bir_lowering=False)
    P = Prog()
    _LAST_PROG[0] = P

    x_d = nc.dram_tensor("x", [S, D], F32, kind="ExternalInput")
    wq_d = nc.dram_tensor("wq", [16, 128, 2048], F32, kind="ExternalInput")
    wk_d = nc.dram_tensor("wk", [4, 128, 2048], F32, kind="ExternalInput")
    wv_d = nc.dram_tensor("wv", [128, 8192], F32, kind="ExternalInput")
    mem_d = nc.dram_tensor("mem", [4, 128, 512], F32, kind="ExternalInput")
    memn_d = nc.dram_tensor("memn", [128, 4], F32, kind="ExternalInput")
    mw_d = nc.dram_tensor("mw", [128, 1], F32, kind="ExternalInput")
    ebias_d = nc.dram_tensor("ebias", [128, 16], F32, kind="ExternalInput")
    rw_d = nc.dram_tensor("rw", [128, 256], F32, kind="ExternalInput")
    gu_d = nc.dram_tensor("gu", [NE, 12, 128, 2048], F32, kind="ExternalInput")
    dw_d = nc.dram_tensor("dw", [NE, 128, 12288], F32, kind="ExternalInput")
    idn16_d = nc.dram_tensor("idn16", [128, 128], BF16, kind="ExternalInput")
    idn32_d = nc.dram_tensor("idn32", [128, 128], F32, kind="ExternalInput")
    x1_d = nc.dram_tensor("x1buf", [S, D], F32, kind="Internal")
    xn2n_d = nc.dram_tensor("xn2nat", [S, D], BF16, kind="Internal")
    idxcw_d = nc.dram_tensor("idxcw", [16 * CEXP, 2], I32, kind="Internal")
    ll_d = nc.dram_tensor("ll", [128, 128], BF16, kind="ExternalInput")
    ones1_d = nc.dram_tensor("ones1", [1, 128], F32, kind="ExternalInput")
    tid_d = nc.dram_tensor("tid", [128, 8], I32, kind="ExternalInput")
    eoff_d = nc.dram_tensor("eoff", [128, 16], F32, kind="ExternalInput")
    y_d = nc.dram_tensor("y", [S, D], F32, kind="ExternalOutput")
    attn_out_d = y_d if attn_only else x1_d

    stp0 = nc.alloc_psum_tensor("stp0", [128, 1024], F32)
    stp1 = nc.alloc_psum_tensor("stp1", [128, 1024], F32)
    pvp0 = nc.alloc_psum_tensor("pvp0", [128, 512], F32)
    pvp1 = nc.alloc_psum_tensor("pvp1", [128, 512], F32)
    pool = Ring([stp0[:, 0:512], stp0[:, 512:1024],
                 stp1[:, 0:512], stp1[:, 512:1024],
                 pvp0[:, :], pvp1[:, :]])
    ptb = Ring([nc.alloc_psum_tensor(f"pt{i}", [128, 1024], BF16) for i in range(2)])

    ARENA = 212736
    arena = nc.alloc_sbuf_tensor("arena", [128, ARENA], mybir.dt.uint8)
    base = nc.lookup_mloc(arena).addr
    cursor = [0]

    def place(name, shape, dtype, align=128):
        nbytes = int(np.prod(shape[1:])) * mybir.dt.size(dtype)
        off = cursor[0]
        cursor[0] = (off + nbytes + align - 1) & ~(align - 1)
        return nc.alloc_sbuf_tensor_at(name, list(shape), dtype,
                                       offset=base + off)

    idn16 = place("idn16s", [128, 128], BF16)
    idn32 = place("idn32s", [128, 128], F32)
    sq = place("sq", [128, 2048], F32)
    tiny = place("tiny", [128, 16], F32)
    small = place("small", [128, 16], F32)
    wsig = place("wsig", [128, 2], F32)
    epsc = place("epsc", [128, 1], F32)
    region0 = cursor[0]

    wbuf = place("wbuf", [128, 16384], BF16)
    memB = place("memB", [128, 2048], BF16)
    memn4 = place("memn4", [128, 4], BF16)
    kT = place("kT", [128, 4 * 2048], BF16)
    v_ext = place("v_ext", [128, 16 * 520 + 128], BF16)
    xn1T = place("xn1T", [128, 16 * 512], BF16)
    qT = place("qT", [128, 16 * 512], BF16)
    sgT = place("sgT", [128, 16 * 512], BF16)
    acc = place("acc", [128, 4 * 2048], F32)
    xn1c = place("xn1c", [128, 2048], BF16)
    oT_sb = place("oT_sb", [128, 1024], F32)
    amT_sb = place("amT_sb", [128, 4 * 512], F32)
    amden = place("amden", [1, 512], F32)
    lnm = place("lnm", [128, 16], F32)
    lnr = place("lnr", [128, 16], F32)
    ptc = Ring([place(f"ptc{i}", [128, 1024], BF16) for i in range(3)])
    xring = Ring([place(f"xr{i}", [128, 2048], F32) for i in range(2)])
    attn_end = cursor[0]
    assert attn_end <= ARENA, f"attn SBUF over: {attn_end}"

    cursor[0] = region0
    y_acc = place("y_acc", [128, 8 * 2048], F32)
    xn2T_off = cursor[0]
    xn2T = place("xn2T", [128, 16 * 1024], BF16)
    xe_nat = nc.alloc_sbuf_tensor_at("xe_nat", [128, 3 * 2048], BF16,
                                     offset=base + xn2T_off)
    xeT = nc.alloc_sbuf_tensor_at("xeT", [128, 16 * CEXP], BF16,
                                  offset=base + xn2T_off + 3 * 2048 * 2)
    actB = place("actB", [128, 6 * 1024], BF16)
    y_sp = place("y_sp", [128, 2 * 2048], F32)
    m_bf = place("m_bf", [128, 128], BF16)
    Pn = place("Pn", [128, 128], F32)
    trow0 = place("trow0", [1, 128], F32)
    trow1 = place("trow1", [1, 128], F32)
    onesc = place("onesc", [128, 1], BF16)
    posf = place("posf", [128, 4], F32)
    cwf = place("cwf", [128, 4], F32)
    posi = place("posi", [128, 4], I32)
    tk2 = place("tk2", [128, 8], I32)
    tksb = [place(f"tksb{i}", [128, 6], I32) for i in range(2)]
    ixc = [[place(f"ixc{i}_{j}", [128, 1], I32, align=32) for j in range(3)]
           for i in range(2)]
    Ll = place("Lls", [128, 128], BF16)
    ones1 = place("ones1s", [1, 128], F32)
    tid = place("tids", [128, 8], I32)
    eoffC = place("eoffs", [128, 16], F32)
    wdr = Ring([place(f"wd{i}", [128, 6 * 2048], BF16) for i in range(2)])
    gur = Ring([place(f"gur{i}", [128, 2048], BF16) for i in range(3)])
    xn2c = place("xn2c", [128, 2048], BF16)
    sgbuf = place("sgbuf", [128, 1024], F32)
    scores = place("scores", [128, 8 * 16], F32)
    combine = place("combine", [128, 8 * 16], F32)
    ft = place("ft", [128, 96], F32)
    ebias_s = place("ebias_s", [128, 16], F32)
    rwT = place("rwT", [128, 256], BF16)
    m2 = place("m2", [128, 8], F32)
    r2 = place("r2", [128, 8], F32)
    ffn_end = cursor[0]
    assert ffn_end <= ARENA, f"ffn SBUF over: {ffn_end}"

    elu1, elu2, elu3 = sq[:, 0:512], sq[:, 512:1024], sq[:, 1024:1536]

    ln_prev = {0: None, 1: None}

    def ln_stats(xc_ap, mcol_ap, rcol_ap, deps, par=0):
        B = 8 * par
        t0 = P.emit("ve", RED(tiny[:, B:B + 1], xc_ap, ALU.add),
                    waits=list(deps) + [ln_prev[par]], inc="ve")
        t1 = P.emit("se", (lambda o=sq[:, :], a=xc_ap,
                           acc=tiny[:, B + 1:B + 2]:
                           (lambda s_: s_.activation(
                               o, a, AF.Square, accum_out=acc)))(),
                    waits=list(deps) + [ln_prev[par]], inc="se")
        t2 = P.emit("ve", TS(mcol_ap, tiny[:, B:B + 1], 1.0 / D, None,
                             ALU.mult), waits=[t0], inc="ve")
        t1b = P.emit("ve", TS(tiny[:, B + 1:B + 2], tiny[:, B + 1:B + 2],
                              1.0 / D, None, ALU.mult), waits=[t1], inc="ve")
        t4 = P.emit("ve", TT_(tiny[:, B + 2:B + 3], mcol_ap, mcol_ap,
                              ALU.mult), waits=[t2], inc="ve")
        tv = P.emit("ve", TT_(tiny[:, B + 1:B + 2], tiny[:, B + 1:B + 2],
                              tiny[:, B + 2:B + 3], ALU.subtract),
                    waits=[t1b, t4], inc="ve")
        tl = P.emit("se", ACT(tiny[:, B + 3:B + 4], tiny[:, B + 1:B + 2],
                              AF.Ln, bias=epsc[:, 0:1]), waits=[tv], inc="se")
        ts = P.emit("se", ACT(rcol_ap, tiny[:, B + 3:B + 4], AF.Exp,
                              scale=-0.5), waits=[tl], inc="se")
        ln_prev[par] = ts
        return ts

    def transpose_to_T(src_chunk, dstT, dst_col, tok_w, deps):
        last = None
        for half in range(2):
            b = ptb.next()
            tok = None
            for j in range(8):
                k = half * 8 + j
                w = (list(deps) + [b["free"]]) if j == 0 else []
                tok = P.emit("te", TR(b["o"][:, j * 128:(j + 1) * 128],
                                      src_chunk[:, k * 128:(k + 1) * 128],
                                      idn16[:, :]),
                             waits=w, inc="te" if j == 7 else None)
            src = b["o"][:, 0:1024].rearrange("p (k t) -> p k t", t=128)
            dst = dstT[:, :].rearrange("p (k t) -> p k t", t=tok_w)
            last = P.emit("ve", CP(dst[:, half * 8:half * 8 + 8,
                                       dst_col:dst_col + 128], src),
                          waits=[tok], inc="ve")
            b["free"] = last
        return last

    def barrier():
        base = P.ctr["ph"]
        qtot = [(k, v) for k, v in P.ctr.items() if k.startswith("q")]
        P.emit("sy", NOP(), waits=qtot, inc="ph")
        P.emit("gp", NOP(), waits=qtot, inc="ph")
        for eng in ("te", "ve", "se"):
            P.emit(eng, NOP(), inc="ph")
        for eng in ENGINES:
            P.emit(eng, NOP(), waits=[("ph", base + 5)])

    P.emit("sy", DMA(idn16[:, :], idn16_d.ap()), inc="qcs", inc_n=16)
    P.emit("sy", DMA(idn32[:, :], idn32_d.ap()), inc="qcs", inc_n=16)
    P.emit("sy", DMA(small[:, 0:1], mw_d.ap()), inc="qcs", inc_n=16)
    for m in range(4):
        P.emit("gp", DMA(wbuf[:, m * 2048:(m + 1) * 2048], wk_d.ap()[m]),
               inc="qcg", inc_n=16)
    P.emit("gp", DMA(wbuf[:, 8192:16384], wv_d.ap()), inc="qcg", inc_n=16)
    for m in range(4):
        P.emit("gp", DMA(memB[:, m * 512:(m + 1) * 512], mem_d.ap()[m]),
               inc="qcg", inc_n=16)
    P.emit("gp", DMA(memn4[:, :], memn_d.ap()), inc="qcg", inc_n=16)
    consts_done = [("qcs", P.ctr["qcs"]), ("qcg", P.ctr["qcg"])]
    t_idn16 = t_idn32 = t_mw = t_wk = t_wv = t_mem = t_memn = consts_done

    P.emit("ve", MEMSET(epsc[:, 0:1], EPS))
    t1 = P.emit("se", ACT(small[:, 1:2], small[:, 0:1], AF.Exp, scale=-1.0),
                waits=[t_mw], inc="se")
    t2 = P.emit("ve", TS(small[:, 2:3], small[:, 1:2], 1.0, None, ALU.add),
                waits=[t1], inc="ve")
    t3 = P.emit("ve", RECIP(wsig[:, 0:1], small[:, 2:3]), waits=[t2], inc="ve")
    P.emit("ve", TS(wsig[:, 1:2], wsig[:, 0:1], -1.0, 1.0, ALU.mult, ALU.add),
           waits=[t3], inc="ve")

    P.emit("ve", MEMSET(v_ext[:, 16 * 520:16 * 520 + 128], 0.0))
    ve_view = v_ext[:, 0:16 * 520].rearrange("p (t c) -> p t c", c=520)
    for h in range(NH):
        P.emit("ve", MEMSET(ve_view[:, :, h * 65 + 64:h * 65 + 65], 1.0))
    t_ones = P.emit("ve", VNOP(), inc="ve")

    def dbg_finish(pairs):
        for r0, nr, src in pairs:
            c = nr // 128
            dst = y_d.ap()[r0:r0 + nr, :].rearrange("(c p) f -> p c f", c=c)
            sview = src.rearrange("p (c f) -> p c f", c=c)
            P.emit("gp", DMA(dst, sview), waits=[("te", P.ctr["te"]),
                                                 ("ve", P.ctr["ve"]),
                                                 ("se", P.ctr["se"])],
                   inc="qdbg", inc_n=16)
        P.emit("gp", NOP(), waits=[("qdbg", P.ctr.get("qdbg", 0))])
        _realize(nc, P)
        return nc

    if dbg == "const":
        return dbg_finish([])

    for tile in range(4):
        drain_toks = []
        for lc in range(4):
            ci = tile * 4 + lc
            xs = xring.next()
            xsem = f"qx{ci % 2}"
            t_x = P.emit("sy", DMA(xs["o"][:, :],
                                   x_d.ap()[ci * 128:(ci + 1) * 128, :]),
                         waits=[xs["free"]], inc=xsem, inc_n=16)
            ts = ln_stats(xs["o"][:, :], lnm[:, ci:ci + 1], lnr[:, ci:ci + 1],
                          [t_x], par=ci % 2)
            t_n = P.emit("ve", TS(xn1c[:, :], xs["o"][:, :], lnm[:, ci:ci + 1],
                                  lnr[:, ci:ci + 1], ALU.subtract, ALU.mult),
                         waits=[ts], inc="ve")
            xs["free"] = t_n
            drain_toks.append(
                transpose_to_T(xn1c[:, :], xn1T, lc * 128, 512, [t_n]))
        for m in range(4):
            b = pool.next()
            tok = None
            for ki in range(16):
                w = ([t_wk, b["free"]] + drain_toks) if ki == 0 else []
                tok = P.emit("te", MM(
                    b["o"][:, :],
                    wbuf[:, m * 2048 + ki * 128:m * 2048 + ki * 128 + 128],
                    xn1T[:, ki * 512:(ki + 1) * 512], ki == 0, ki == 15),
                    waits=w, inc="te" if ki == 15 else None)
            b["free"] = P.emit("ve", CP(
                kT[:, m * 2048 + tile * 512:m * 2048 + tile * 512 + 512],
                b["o"][:, :]), waits=[tok], inc="ve")
        for mt in range(4):
            b = pool.next()
            tc = tile * 4 + mt
            tok = None
            for ki in range(16):
                w = ([t_wv, b["free"]] + drain_toks) if ki == 0 else []
                tok = P.emit("te", MM(
                    b["o"][:, :],
                    xn1T[:, ki * 512 + mt * 128:ki * 512 + mt * 128 + 128],
                    wbuf[:, 8192 + ki * 512:8192 + (ki + 1) * 512],
                    ki == 0, ki == 15),
                    waits=w, inc="te" if ki == 15 else None)
            lastd = None
            for h in range(NH):
                lastd = P.emit("ve", CP(
                    v_ext[:, tc * 520 + h * 65:tc * 520 + h * 65 + 64],
                    b["o"][:, h * 64:(h + 1) * 64]),
                    waits=[tok, t_ones] if h == 0 else [],
                    inc="ve" if h == NH - 1 else None)
            b["free"] = lastd

    barrier()

    if dbg == "pass1":
        for src, r0, c0, w_ in ((lnm, 1024, 0, 16), (lnr, 1024, 16, 16)):
            P.emit("gp", DMA(y_d.ap()[r0:r0 + 128, c0:c0 + w_], src[:, 0:w_]),
                   waits=[("te", P.ctr["te"]), ("ve", P.ctr["ve"]),
                          ("se", P.ctr["se"])], inc="qdbg", inc_n=16)
        P.emit("gp", DMA(y_d.ap()[1152:1280, :], xn1c[:, :]),
               inc="qdbg", inc_n=16)
        return dbg_finish([(0, 512, kT[:, :]), (512, 512, xn1T[:, :])])

    wq_ring = Ring([i * 2048 for i in range(4)])
    prev_te_end = None
    acc_war = None
    for st in range(NSt):
        acc_ready = []
        for lc in range(4):
            ci = st * 4 + lc
            acc_ready.append(P.emit("sy", DMA(
                acc[:, lc * 2048:(lc + 1) * 2048],
                x_d.ap()[ci * 128:(ci + 1) * 128, :]),
                waits=[acc_war], inc=f"qa{lc}", inc_n=16))
        xn1_toks = []
        for lc in range(4):
            ci = st * 4 + lc
            t_n = P.emit("ve", TS(xn1c[:, :], acc[:, lc * 2048:(lc + 1) * 2048],
                                  lnm[:, ci:ci + 1], lnr[:, ci:ci + 1],
                                  ALU.subtract, ALU.mult),
                         waits=[acc_ready[lc], prev_te_end], inc="ve")
            xn1_toks.append(transpose_to_T(xn1c[:, :], xn1T, lc * 128, 512, [t_n]))
        prev_elu_se = None
        prev_elu_ve = None
        for m in range(16):
            slot = wq_ring.next()
            col = slot["o"]
            t_w = P.emit("gp", DMA(wbuf[:, col:col + 2048], wq_d.ap()[m]),
                         waits=[slot["free"]], inc=f"qw{m % 4}", inc_n=16)
            b = pool.next()
            tok = None
            for ki in range(16):
                w = ([t_w, b["free"]] + xn1_toks) if ki == 0 else []
                tok = P.emit("te", MM(
                    b["o"][:, :],
                    wbuf[:, col + ki * 128:col + ki * 128 + 128],
                    xn1T[:, ki * 512:(ki + 1) * 512], ki == 0, ki == 15),
                    waits=w, inc="te" if ki == 15 else None)
            slot["free"] = tok
            P.emit("ve", CP(qT[:, m * 512:(m + 1) * 512], b["o"][:, :]),
                   waits=[tok])
            t_mn = P.emit("ve", TS(elu1, b["o"][:, :], 0.0, None, ALU.min),
                          waits=[prev_elu_se], inc="ve")
            t_e = P.emit("se", ACT(elu2, elu1, AF.Exp),
                         waits=[t_mn, prev_elu_ve], inc="se")
            prev_elu_se = t_e
            t_mx = P.emit("ve", TS(elu3, b["o"][:, :], 0.0, None, ALU.max),
                          waits=[prev_elu_ve], inc="ve")
            t_s = P.emit("ve", STT(sgT[:, m * 512:(m + 1) * 512], elu2, -1.0,
                                   elu3, ALU.add, ALU.add),
                         waits=[t_e, t_mx], inc="ve")
            prev_elu_ve = t_s
            b["free"] = t_s
        q_done = P.emit("ve", VNOP(), inc="ve")

        stpt = [{"o": stp0, "free": [pool.slots[0]["free"], pool.slots[1]["free"]]},
                {"o": stp1, "free": [pool.slots[2]["free"], pool.slots[3]["free"]]}]
        pvt = [{"o": pvp0, "free": [pool.slots[4]["free"]]},
               {"o": pvp1, "free": [pool.slots[5]["free"]]}]
        first_heads = [q_done]
        for qg in range(NQ):
            for hp in range(4):
                h0, h1 = 2 * hp, 2 * hp + 1
                mcol = (qg * 4 + hp) * 512
                pend = None

                def emit_pv(pd, h0=h0, h1=h1):
                    j, pc, tex = pd
                    t0_ = P.emit("te", MM(
                        pvt[0]["o"][:, :],
                        v_ext[:, j * 520 + h0 * 65:j * 520 + h0 * 65 + 128],
                        pc["o"][:, 0:512], j == 0, j == 15),
                        waits=[tex] + (pvt[0]["free"] if j == 0 else []),
                        inc="te" if j == 15 else None)
                    t1_ = P.emit("te", MM(
                        pvt[1]["o"][:, :],
                        v_ext[:, j * 520 + h1 * 65:j * 520 + h1 * 65 + 128],
                        pc["o"][:, 512:1024], j == 0, j == 15),
                        waits=(pvt[1]["free"] if j == 0 else []),
                        inc="te" if j == 15 else None)
                    pc["free"] = t1_ if j == 15 else t0_
                    return t0_, t1_

                for i in range(16):
                    sp = stpt[i % 2]
                    w = [sp["free"]]
                    if qg == 0 and hp == 0 and i < 2:
                        w = w + first_heads
                    P.emit("te", MM(
                        sp["o"][:, 0:512],
                        kT[0:64, hp * 2048 + i * 128:hp * 2048 + i * 128 + 128],
                        qT[0:64, mcol:mcol + 512], True, True, tp=(0, 0)),
                        waits=w)
                    tqk1 = P.emit("te", MM(
                        sp["o"][:, 512:1024],
                        kT[64:128, hp * 2048 + i * 128:hp * 2048 + i * 128 + 128],
                        qT[64:128, mcol:mcol + 512], True, True, tp=(64, 0)),
                        inc="te")
                    pc = ptc.next()
                    tex = P.emit("se", ACT(pc["o"][:, 0:1024], sp["o"][:, 0:1024],
                                           AF.Exp, scale=0.125),
                                 waits=[tqk1, pc["free"]], inc="se")
                    sp["free"] = [tex]
                    if pend is not None:
                        emit_pv(pend)
                    pend = (i, pc, tex)
                tpv0, tpv1 = emit_pv(pend)

                td0 = P.emit("ve", CP(oT_sb[0:65, 0:512], pvt[0]["o"][0:65, :]),
                             waits=[tpv0], inc="ve")
                td1 = P.emit("ve", CP(oT_sb[0:65, 512:1024], pvt[1]["o"][0:65, :]),
                             waits=[tpv1], inc="ve")
                ta0 = ta1 = None
                for sc in range(4):
                    tt0 = P.emit("te", TR(pvt[0]["o"][:, 0:65],
                                          oT_sb[0:65, sc * 128:sc * 128 + 128],
                                          idn32[0:65, 0:65]),
                                 waits=[td0, t_idn32] + ([ta0] if ta0 else []),
                                 inc="te")
                    tt1 = P.emit("te", TR(
                        pvt[1]["o"][:, 0:65],
                        oT_sb[0:65, 512 + sc * 128:512 + sc * 128 + 128],
                        idn32[0:65, 0:65]),
                        waits=[td1] + ([ta1] if ta1 else []), inc="te")
                    tr = P.emit("ve", RECIP(small[:, 4:5], pvt[0]["o"][:, 64:65]),
                                waits=[tt0], inc="ve")
                    tw_ = P.emit("ve", TT_(small[:, 5:6], small[:, 4:5],
                                           wsig[:, 1:2], ALU.mult),
                                 waits=[tr], inc="ve")
                    c0 = sc * 2048 + qg * 512 + h0 * 64
                    ta0 = P.emit("ve", STT(acc[:, c0:c0 + 64],
                                           pvt[0]["o"][:, 0:64],
                                           small[:, 5:6], acc[:, c0:c0 + 64],
                                           ALU.mult, ALU.add),
                                 waits=[tw_], inc="ve")
                    tr1 = P.emit("ve", RECIP(small[:, 6:7], pvt[1]["o"][:, 64:65]),
                                 waits=[tt1], inc="ve")
                    tw1 = P.emit("ve", TT_(small[:, 7:8], small[:, 6:7],
                                           wsig[:, 1:2], ALU.mult),
                                 waits=[tr1], inc="ve")
                    c1 = sc * 2048 + qg * 512 + h1 * 64
                    ta1 = P.emit("ve", STT(acc[:, c1:c1 + 64],
                                           pvt[1]["o"][:, 0:64],
                                           small[:, 7:8], acc[:, c1:c1 + 64],
                                           ALU.mult, ALU.add),
                                 waits=[tw1], inc="ve")
                pvt[0]["free"] = [ta0]
                pvt[1]["free"] = [ta1]

        pool.slots[0]["free"] = stpt[0]["free"][0]
        pool.slots[1]["free"] = stpt[0]["free"][0]
        pool.slots[2]["free"] = stpt[1]["free"][0]
        pool.slots[3]["free"] = stpt[1]["free"][0]
        pool.slots[4]["free"] = pvt[0]["free"][0]
        pool.slots[5]["free"] = pvt[1]["free"][0]
        pool.i = 0

        for qg in range(NQ):
            bD = pool.next()
            tokD = None
            for ii in range(4):
                w = [q_done, t_memn, bD["free"]] if ii == 0 else []
                tokD = P.emit("te", MM(
                    bD["o"][0:1, :], memn4[:, ii:ii + 1],
                    sgT[:, (qg * 4 + ii) * 512:(qg * 4 + ii + 1) * 512],
                    ii == 0, ii == 3),
                    waits=w, inc="te" if ii == 3 else None)
            jtoks = []
            for jj in range(4):
                b = pool.next()
                tok = None
                for ii in range(4):
                    w = [t_mem, b["free"]] if ii == 0 else []
                    tok = P.emit("te", MM(
                        b["o"][:, :],
                        memB[:, jj * 512 + ii * 128:jj * 512 + ii * 128 + 128],
                        sgT[:, (qg * 4 + ii) * 512:(qg * 4 + ii + 1) * 512],
                        ii == 0, ii == 3),
                        waits=w, inc="te" if ii == 3 else None)
                t_am = P.emit("ve", CP(amT_sb[:, jj * 512:(jj + 1) * 512],
                                       b["o"][:, :]), waits=[tok], inc="ve")
                b["free"] = t_am
                jtoks.append(t_am)
            t_dn = P.emit("ve", CP(amden[0:1, :], bD["o"][0:1, :]),
                          waits=[tokD], inc="ve")
            bD["free"] = t_dn
            bDn = pool.next()
            prev_dr = None
            for sc in range(4):
                w = [t_dn] + ([bDn["free"]] if sc == 0 else [prev_dr])
                tsp = P.emit("te", TR(bDn["o"][:, 0:1],
                                      amden[0:1, sc * 128:(sc + 1) * 128],
                                      idn32[0:1, 0:1]),
                             waits=w, inc="te")
                prev_dr = P.emit("ve", CP(tiny[:, 10 + sc:11 + sc],
                                          bDn["o"][:, 0:1]),
                                 waits=[tsp], inc="ve")
            tdn = prev_dr
            last_sc = None
            for sc in range(4):
                trc = P.emit("ve", RECIP(tiny[:, 8:9], tiny[:, 10 + sc:11 + sc]),
                             waits=[tdn], inc="ve")
                tww = P.emit("ve", TT_(tiny[:, 9:10], tiny[:, 8:9], wsig[:, 0:1],
                                       ALU.mult), waits=[trc], inc="ve")
                for jj in range(4):
                    bA2 = pool.next()
                    ttj = P.emit("te", TR(
                        bA2["o"][:, 0:128],
                        amT_sb[:, jj * 512 + sc * 128:jj * 512 + sc * 128 + 128],
                        idn32[:, :]),
                        waits=[jtoks[jj], bA2["free"]], inc="te")
                    cc = sc * 2048 + qg * 512 + jj * 128
                    last_sc = P.emit("ve", STT(
                        acc[:, cc:cc + 128], bA2["o"][:, 0:128], tiny[:, 9:10],
                        acc[:, cc:cc + 128], ALU.mult, ALU.add),
                        waits=[ttj, tww], inc="ve")
                    bA2["free"] = last_sc
            bDn["free"] = tdn

        t_all = P.emit("ve", VNOP(), inc="ve")
        for lc in range(4):
            ci = st * 4 + lc
            P.emit("sy", DMA(attn_out_d.ap()[ci * 128:(ci + 1) * 128, :],
                             acc[:, lc * 2048:(lc + 1) * 2048]),
                   waits=[t_all], inc="qs", inc_n=16)
        acc_war = ("qs", P.ctr["qs"])
        prev_te_end = P.emit("te", NOP(), inc="te")

    attn_store_ctr = P.ctr["qs"]

    if attn_only:
        P.emit("sy", NOP(), waits=[("qs", P.ctr["qs"])])
        _realize(nc, P)
        return nc

    barrier()

    P.emit("sy", DMA(ebias_s[:, :], ebias_d.ap()), inc="qc2s", inc_n=16)
    P.emit("gp", DMA(rwT[:, :], rw_d.ap()), inc="qc2g", inc_n=16)
    P.emit("sy", DMA(Ll[:, :], ll_d.ap()), inc="qc2s", inc_n=16)
    P.emit("sy", DMA(ones1[0:1, :], ones1_d.ap()), inc="qc2s", inc_n=16)
    P.emit("sy", DMA(tid[:, :], tid_d.ap()), inc="qc2s", inc_n=16)
    P.emit("sy", DMA(eoffC[:, :], eoff_d.ap()), inc="qc2s", inc_n=16)
    t_eb = t_rw = [("qc2s", P.ctr["qc2s"]), ("qc2g", P.ctr["qc2g"])]
    t_onc = P.emit("ve", MEMSET(onesc[:, :], 1.0), inc="ve")
    oob_view = y_sp[:, 0:96].bitcast(I32)

    prev_y_store = ("qs", attn_store_ctr)
    prev_te_end2 = None
    prev_idx_load = None
    prev_act_te = None
    prev_sg = [None, None]
    tk_ring = Ring([0, 1, 2, 3])
    tkb_war = [None, None]
    ysp_ring = Ring([0, 1])

    dense_state = {"last_gu": None}

    def dense_branch(e, xn2_toks):
        nonlocal prev_act_te
        wi = wdr.i % 2
        wd_slot = wdr.next()
        t_wd = P.emit("gp", DMA(wd_slot["o"][:, :], dw_d.ap()[e]),
                      waits=[wd_slot["free"]], inc=f"qd{wi}", inc_n=16)
        act_toks = []
        for i in range(6):
            pair = []
            for m in (i, i + 6):
                gi = gur.i % 3
                slot = gur.next()
                t_w = P.emit("gp", DMA(slot["o"][:, :], gu_d.ap()[e, m]),
                             waits=[slot["free"]], inc=f"qgu{gi}", inc_n=16)
                bh = [pool.next(), pool.next()]
                tok = None
                for ki in range(16):
                    for half in range(2):
                        w = []
                        if ki == 0:
                            w = [t_w, bh[half]["free"]]
                            if half == 0:
                                w += xn2_toks
                        tok = P.emit("te", MM(
                            bh[half]["o"][:, :],
                            slot["o"][:, ki * 128:(ki + 1) * 128],
                            xn2T[:, ki * 1024 + half * 512:ki * 1024 + half * 512 + 512],
                            ki == 0, ki == 15),
                            waits=w,
                            inc="te" if (ki == 15 and half == 1) else None)
                slot["free"] = tok
                pair.append((bh, tok))
            dense_state["last_gu"] = pair[1][1]
            (bg, tokg), (bu, toku) = pair
            for half in range(2):
                t_sg = P.emit("se", ACT(sgbuf[:, half * 512:(half + 1) * 512],
                                        bg[half]["o"][:, :], AF.Silu),
                              waits=[tokg, prev_sg[half]], inc="se")
                bg[half]["free"] = t_sg
                t_act = P.emit("ve", TT_(
                    actB[:, i * 1024 + half * 512:i * 1024 + half * 512 + 512],
                    sgbuf[:, half * 512:(half + 1) * 512],
                    bu[half]["o"][:, :], ALU.mult),
                    waits=[t_sg, toku, prev_act_te], inc="ve")
                bu[half]["free"] = t_act
                prev_sg[half] = t_act
                act_toks.append(t_act)
        tok_dn = None
        for tc in range(8):
            for ncol in range(4):
                b = pool.next()
                for ki in range(6):
                    w = ([t_wd, b["free"]] + act_toks) if ki == 0 else []
                    tok_dn = P.emit("te", MM(
                        b["o"][:, :],
                        actB[:, ki * 1024 + tc * 128:ki * 1024 + tc * 128 + 128],
                        wd_slot["o"][:, ki * 2048 + ncol * 512:ki * 2048 + ncol * 512 + 512],
                        ki == 0, ki == 5),
                        waits=w, inc="te" if ki == 5 else None)
                dst = y_acc[:, tc * 2048 + ncol * 512:tc * 2048 + ncol * 512 + 512]
                t_dr = P.emit("ve", STT(dst, b["o"][:, :], 1.0, dst,
                                        ALU.mult, ALU.add),
                              waits=[tok_dn], inc="ve")
                b["free"] = t_dr
        wd_slot["free"] = tok_dn
        prev_act_te = tok_dn

    prev_ys = []
    for p in range(2):
        t_ms = P.emit("ve", MEMSET(oob_view, 1 << 20),
                      waits=prev_ys, inc="ve")
        t_fill = P.emit("sy", DMA(
            idxcw_d.ap().rearrange("(p f) j -> p (f j)", p=128),
            oob_view), waits=[prev_idx_load, t_ms], inc="qif", inc_n=16)
        fill_tok = ("qif", P.ctr["qif"])
        for _sl in ysp_ring.slots:
            _sl["free"] = fill_tok

        xn2_toks = []
        for c in range(8):
            ci = p * 8 + c
            t_x = P.emit("sy", DMA(y_acc[:, c * 2048:(c + 1) * 2048],
                                   x1_d.ap()[ci * 128:(ci + 1) * 128, :]),
                         waits=[prev_y_store] if c == 0 else [],
                         inc=f"qy{c}", inc_n=16)
            ts2 = ln_stats(y_acc[:, c * 2048:(c + 1) * 2048],
                           m2[:, c:c + 1], r2[:, c:c + 1], [t_x], par=c % 2)
            t_n = P.emit("ve", TS(xn2c[:, :], y_acc[:, c * 2048:(c + 1) * 2048],
                                  m2[:, c:c + 1], r2[:, c:c + 1],
                                  ALU.subtract, ALU.mult),
                         waits=[ts2], inc="ve")
            P.emit("sy", DMA(
                xn2n_d.ap()[p * 1024 + c * 128:p * 1024 + (c + 1) * 128, :],
                xn2c[:, :]), waits=[t_n], inc="qxn", inc_n=16)
            xn2_toks.append(transpose_to_T(xn2c[:, :], xn2T, c * 128, 1024,
                                           [t_n, prev_te_end2]))
        xn_ready = ("qxn", P.ctr["qxn"])

        rt_toks = []
        for c in range(8):
            b = pool.next()
            tok = None
            for ki in range(16):
                w = ([t_rw, b["free"]] + xn2_toks) if ki == 0 else []
                tok = P.emit("te", MM(
                    b["o"][:, 0:16],
                    xn2T[:, ki * 1024 + c * 128:ki * 1024 + c * 128 + 128],
                    rwT[:, ki * 16:(ki + 1) * 16], ki == 0, ki == 15),
                    waits=w, inc="te" if ki == 15 else None)
            rt_toks.append((b, tok))

        topk_toks = []
        prev_ft_read = None
        for c in range(8):
            b, tok = rt_toks[c]
            t_ex = P.emit("se", ACT(ft[:, 0:16], b["o"][:, 0:16], AF.Exp,
                                    scale=-1.0), waits=[tok, prev_ft_read],
                          inc="se")
            b["free"] = t_ex
            u = P.emit("ve", TS(ft[:, 16:32], ft[:, 0:16], 1.0, None, ALU.add),
                       waits=[t_ex], inc="ve")
            prev_ft_read = u
            u = P.emit("ve", RECIP(scores[:, c * 16:(c + 1) * 16],
                                   ft[:, 16:32]), waits=[u], inc="ve")
            u = P.emit("ve", TT_(ft[:, 16:32], scores[:, c * 16:(c + 1) * 16],
                                 ebias_s[:, :], ALU.add), waits=[t_eb, u],
                       inc="ve")
            u2 = P.emit("ve", MEMSET(combine[:, c * 16:(c + 1) * 16], 0.0),
                        inc="ve")
            for k in range(4):
                u = P.emit("ve", RED(ft[:, 32:33], ft[:, 16:32], ALU.max),
                           waits=[u], inc="ve")
                u = P.emit("ve", TS(ft[:, 48:64], ft[:, 16:32], ft[:, 32:33],
                                    None, ALU.is_equal), waits=[u], inc="ve")
                u = P.emit("ve", TT_(ft[:, 64:80], ft[:, 48:64],
                                     scores[:, c * 16:(c + 1) * 16], ALU.mult),
                           waits=[u], inc="ve")
                u = P.emit("ve", TT_(combine[:, c * 16:(c + 1) * 16],
                                     combine[:, c * 16:(c + 1) * 16],
                                     ft[:, 64:80], ALU.add),
                           waits=[u, u2], inc="ve")
                u = P.emit("ve", STT(ft[:, 16:32], ft[:, 48:64], NEG_BIG,
                                     ft[:, 16:32], ALU.mult, ALU.add),
                           waits=[u], inc="ve")
            u = P.emit("ve", RED(ft[:, 33:34], combine[:, c * 16:(c + 1) * 16],
                                 ALU.add), waits=[u], inc="ve")
            u = P.emit("ve", TS(ft[:, 34:35], ft[:, 33:34], 1e-20, None,
                                ALU.add), waits=[u], inc="ve")
            u = P.emit("ve", RECIP(ft[:, 35:36], ft[:, 34:35]), waits=[u],
                       inc="ve")
            tk = P.emit("ve", TS(combine[:, c * 16:(c + 1) * 16],
                                 combine[:, c * 16:(c + 1) * 16],
                                 ft[:, 35:36], None, ALU.mult), waits=[u],
                        inc="ve")
            topk_toks.append(tk)

        dense_branch(16, xn2_toks)
        t_shared_done = P.emit("ve", VNOP(), inc="ve")

        for c in range(8):
            ci = p * 8 + c
            P.emit("sy", DMA(y_d.ap()[ci * 128:(ci + 1) * 128, :],
                             y_acc[:, c * 2048:(c + 1) * 2048]),
                   waits=[t_shared_done], inc="qs", inc_n=16)
        y_base_tok = ("qs", P.ctr["qs"])
        prev_y_store = y_base_tok
        t_m = P.emit("ve", TS(m_bf[:, :], combine[:, :], 0.0, None, ALU.is_gt),
                     waits=topk_toks + [t_onc], inc="ve")
        bS = pool.next()
        tS = P.emit("te", MM(bS["o"][0:1, 0:128], onesc[:, 0:1], m_bf[:, :],
                             True, True), waits=[t_m, bS["free"]], inc="te")
        t_r1 = P.emit("ve", CP(trow0[0:1, :], bS["o"][0:1, 0:128]),
                      waits=[tS], inc="ve")
        bS["free"] = t_r1
        u = P.emit("ve", MEMSET(trow1[0:1, :], 0.0), waits=[t_r1], inc="ve")
        u = P.emit("ve", CP(trow1[0:1, 16:128], trow0[0:1, 0:112]),
                   waits=[u], inc="ve")
        bufs = [trow1, trow0]
        for step, sh in enumerate((16, 32, 64)):
            src = bufs[step % 2]
            dst = bufs[(step + 1) % 2]
            u = P.emit("ve", CP(dst[0:1, 0:sh], src[0:1, 0:sh]), waits=[u],
                       inc="ve")
            u = P.emit("ve", TT_(dst[0:1, sh:128], src[0:1, sh:128],
                                 src[0:1, 0:128 - sh], ALU.add), waits=[u],
                       inc="ve")
        pref_buf = trow0
        t_pref = u
        bP = pool.next()
        P.emit("te", MM(bP["o"][:, 0:128], Ll[:, :], m_bf[:, :], True, False),
               waits=[t_m, bP["free"], t_eb], inc=None)
        tP = P.emit("te", MM(bP["o"][:, 0:128], ones1[0:1, :],
                             pref_buf[0:1, :], False, True),
                    waits=[t_pref], inc="te")
        t_Pn = P.emit("ve", CP(Pn[:, :], bP["o"][:, 0:128]), waits=[tP],
                      inc="ve")
        bP["free"] = t_Pn

        for c in range(8):
            cs = slice(c * 16, (c + 1) * 16)
            u = P.emit("ve", TT_(ft[:, 0:16], eoffC[:, :], Pn[:, cs],
                                 ALU.add), waits=[t_Pn], inc="ve")
            u = P.emit("ve", TS(ft[:, 80:96], Pn[:, cs], CEXP - 0.5, None,
                                ALU.is_ge), waits=[u], inc="ve")
            u = P.emit("ve", STT(ft[:, 0:16], ft[:, 80:96], 1.0e6, ft[:, 0:16],
                                 ALU.mult, ALU.add), waits=[u], inc="ve")
            u = P.emit("ve", TT_(ft[:, 16:32], scores[:, cs],
                                 ebias_s[:, :], ALU.add), waits=[u], inc="ve")
            for k in range(4):
                u = P.emit("ve", RED(ft[:, 32:33], ft[:, 16:32], ALU.max),
                           waits=[u], inc="ve")
                u = P.emit("ve", TS(ft[:, 48:64], ft[:, 16:32], ft[:, 32:33],
                                    None, ALU.is_equal), waits=[u], inc="ve")
                si = tk_ring.next()
                sidx = si["o"]
                u = P.emit("ve", TT_(ft[:, 64:80], ft[:, 48:64], ft[:, 0:16],
                                     ALU.mult), waits=[u, si["free"]],
                           inc="ve")
                u = P.emit("ve", RED(posf[:, sidx:sidx + 1], ft[:, 64:80],
                                     ALU.add), waits=[u], inc="ve")
                u = P.emit("ve", TT_(ft[:, 64:80], ft[:, 48:64],
                                     combine[:, cs], ALU.mult),
                           waits=[u], inc="ve")
                u = P.emit("ve", RED(cwf[:, sidx:sidx + 1], ft[:, 64:80],
                                     ALU.add), waits=[u], inc="ve")
                u = P.emit("ve", CP(posi[:, sidx:sidx + 1],
                                    posf[:, sidx:sidx + 1]), waits=[u],
                           inc="ve")
                u = P.emit("ve", TS(tk2[:, 2 * sidx:2 * sidx + 1],
                                    tid[:, c:c + 1], float(p * 1024), None,
                                    ALU.add), waits=[u], inc="ve")
                u = P.emit("ve", CP(
                    tk2[:, 2 * sidx + 1:2 * sidx + 2],
                    cwf[:, sidx:sidx + 1].bitcast(I32)), waits=[u], inc="ve")
                o_ap = idxcw_d.ap()
                off_ap = posi[:, sidx:sidx + 1]
                in_ap = tk2[:, 2 * sidx:2 * sidx + 2]
                tsc = P.emit("gp", (lambda oa=o_ap, fa=off_ap, ia=in_ap:
                             (lambda g: g.indirect_dma_start(
                                 oa, IndirectOffsetOnAxis(fa, 0), ia, None)))(),
                             waits=[u, fill_tok], inc=f"qsc{sidx}", inc_n=16)
                si["free"] = tsc
                u = P.emit("ve", STT(ft[:, 16:32], ft[:, 48:64], NEG_BIG,
                                     ft[:, 16:32], ALU.mult, ALU.add),
                           waits=[u], inc="ve")
        scatters_done = [(f"qsc{i}", P.ctr.get(f"qsc{i}", 0))
                         for i in range(4)]

        prev_gu_read = None
        gstate = {}

        def emit_gather(e):
            par = e % 2
            tkb_cur = tksb[par]
            for cc3 in range(3):
                P.emit("sy", DMA(
                    tkb_cur[:, 2 * cc3:2 * cc3 + 2],
                    idxcw_d.ap()[e * CEXP + cc3 * 128:e * CEXP + (cc3 + 1) * 128, :]),
                    waits=(scatters_done + [tkb_war[par]]) if cc3 == 0 else [],
                    inc=f"qi{par}", inc_n=16)
            tkb_tok = (f"qi{par}", P.ctr[f"qi{par}"])
            tcl = None
            for sc in range(3):
                tcl = P.emit("ve", TS(ixc[par][sc][:, 0:1],
                                      tkb_cur[:, 2 * sc:2 * sc + 1],
                                      float(p * 1024 + 1023), None, ALU.min),
                             waits=[tkb_tok] if sc == 0 else [], inc="ve")
            g_toks = []
            for sc in range(3):
                o_ap = xe_nat[:, sc * 2048:(sc + 1) * 2048]
                in_ap = xn2n_d.ap()
                off_ap = ixc[par][sc][:, 0:1]
                P.emit("gp", (lambda oa=o_ap, ia=in_ap, fa=off_ap:
                       (lambda g: g.indirect_dma_start(
                           oa, None, ia, IndirectOffsetOnAxis(fa, 0))))(),
                       waits=[tcl, xn_ready, prev_gu_read,
                              dense_state["last_gu"] if e == 0 else None],
                       inc=f"qga{sc}", inc_n=16)
                g_toks.append((f"qga{sc}", P.ctr[f"qga{sc}"]))
            gstate[e] = (tkb_cur, tkb_tok, g_toks)

        emit_gather(0)
        for e in range(16):
            par = e % 2
            tkb_cur, tkb_tok, g_toks = gstate.pop(e)
            gi = gur.i % 3
            slot0 = gur.next()
            t_w0 = P.emit("gp", DMA(slot0["o"][:, :], gu_d.ap()[e, 0]),
                          waits=[slot0["free"]], inc=f"qgu{gi}", inc_n=16)
            gi = gur.i % 3
            slot6 = gur.next()
            t_w6 = P.emit("gp", DMA(slot6["o"][:, :], gu_d.ap()[e, 6]),
                          waits=[slot6["free"]], inc=f"qgu{gi}", inc_n=16)
            b0 = pool.next()
            b6 = pool.next()
            tok0 = tok6 = None
            tr_ = None
            for kk in range(16):
                tr_last = None
                for sc in range(3):
                    bt = ptb.next()
                    tr_ = P.emit("te", TR(
                        bt["o"][:, 0:128],
                        xe_nat[:, sc * 2048 + kk * 128:sc * 2048 + kk * 128 + 128],
                        idn16[:, :]),
                        waits=[g_toks[sc], bt["free"]], inc="te")
                    dcol = kk * CEXP + sc * 128
                    o_ap = xeT[:, dcol:dcol + 128]
                    i_ap = bt["o"][:, 0:128]
                    tr_last = P.emit("se", (lambda oa=o_ap, ia=i_ap:
                                     (lambda s_: s_.copy(oa, ia)))(),
                                     waits=[tr_], inc="se")
                    bt["free"] = tr_last
                w0 = [tr_last] + ([t_w0, b0["free"]] if kk == 0 else [])
                tok0 = P.emit("te", MM(
                    b0["o"][:, 0:CEXP], slot0["o"][:, kk * 128:(kk + 1) * 128],
                    xeT[:, kk * CEXP:(kk + 1) * CEXP], kk == 0, kk == 15),
                    waits=w0, inc="te" if kk == 15 else None)
                w6 = [t_w6, b6["free"]] if kk == 0 else []
                tok6 = P.emit("te", MM(
                    b6["o"][:, 0:CEXP], slot6["o"][:, kk * 128:(kk + 1) * 128],
                    xeT[:, kk * CEXP:(kk + 1) * CEXP], kk == 0, kk == 15),
                    waits=w6, inc="te" if kk == 15 else None)
            slot0["free"] = tok0
            slot6["free"] = tok6
            xeT_ready = None
            prev_gu_read = tr_

            act_toks = []
            tok = None
            t_sg = P.emit("se", ACT(sgbuf[:, 0:CEXP], b0["o"][:, 0:CEXP],
                                    AF.Silu), waits=[tok0, prev_sg[0]],
                          inc="se")
            b0["free"] = t_sg
            t_act = P.emit("ve", TT_(actB[:, 0:CEXP], sgbuf[:, 0:CEXP],
                                     b6["o"][:, 0:CEXP], ALU.mult),
                           waits=[t_sg, tok6, prev_act_te], inc="ve")
            b6["free"] = t_act
            prev_sg[0] = t_act
            act_toks.append(t_act)
            for i in range(1, 6):
                pair = []
                for m in (i, i + 6):
                    gi = gur.i % 3
                    slot = gur.next()
                    t_w = P.emit("gp", DMA(slot["o"][:, :], gu_d.ap()[e, m]),
                                 waits=[slot["free"]], inc=f"qgu{gi}",
                                 inc_n=16)
                    b = pool.next()
                    for ki in range(16):
                        w = ([t_w, b["free"]] if ki == 0 else [])
                        tok = P.emit("te", MM(
                            b["o"][:, 0:CEXP],
                            slot["o"][:, ki * 128:(ki + 1) * 128],
                            xeT[:, ki * CEXP:(ki + 1) * CEXP],
                            ki == 0, ki == 15),
                            waits=w, inc="te" if ki == 15 else None)
                    slot["free"] = tok
                    pair.append((b, tok))
                (bg, tokg), (bu, toku) = pair
                t_sg = P.emit("se", ACT(sgbuf[:, 0:CEXP], bg["o"][:, 0:CEXP],
                                        AF.Silu),
                              waits=[tokg, prev_sg[0]], inc="se")
                bg["free"] = t_sg
                t_act = P.emit("ve", TT_(
                    actB[:, i * CEXP:(i + 1) * CEXP], sgbuf[:, 0:CEXP],
                    bu["o"][:, 0:CEXP], ALU.mult),
                    waits=[t_sg, toku, prev_act_te], inc="ve")
                bu["free"] = t_act
                prev_sg[0] = t_act
                act_toks.append(t_act)
            if e + 1 < 16:
                emit_gather(e + 1)
            wi = wdr.i % 2
            wd_slot = wdr.next()
            t_wd = P.emit("gp", DMA(wd_slot["o"][:, :], dw_d.ap()[e]),
                          waits=[wd_slot["free"]], inc=f"qd{wi}", inc_n=16)
            tok_dn = None
            last_ys = None
            for sc in range(3):
                ysl = ysp_ring.next()
                ypar = ysl["o"]
                drains = []
                for ncol in range(4):
                    b = pool.next()
                    for ki in range(6):
                        w = ([t_wd, b["free"]] + act_toks) if ki == 0 else []
                        tok_dn = P.emit("te", MM(
                            b["o"][:, :],
                            actB[:, ki * CEXP + sc * 128:ki * CEXP + sc * 128 + 128],
                            wd_slot["o"][:, ki * 2048 + ncol * 512:ki * 2048 + ncol * 512 + 512],
                            ki == 0, ki == 5),
                            waits=w, inc="te" if ki == 5 else None)
                    t_dr = P.emit("ve", TS(
                        y_sp[:, ypar * 2048 + ncol * 512:ypar * 2048 + ncol * 512 + 512],
                        b["o"][:, :],
                        tkb_cur[:, 2 * sc + 1:2 * sc + 2].bitcast(F32),
                        None, ALU.mult),
                        waits=[tok_dn, ysl["free"]], inc="ve")
                    drains.append(t_dr)
                    b["free"] = t_dr
                o_ap = y_d.ap()
                off_ap = tkb_cur[:, 2 * sc:2 * sc + 1]
                i_ap = y_sp[:, ypar * 2048:(ypar + 1) * 2048]
                last_ys = P.emit("gp", (lambda oa=o_ap, fa=off_ap, ia=i_ap:
                                 (lambda g: g.indirect_dma_start(
                                     oa, IndirectOffsetOnAxis(fa, 0), ia, None,
                                     compute_op=ALU.add)))(),
                                 waits=drains + [y_base_tok],
                                 inc=f"qys{ypar}", inc_n=16)
                ysl["free"] = last_ys
            wd_slot["free"] = tok_dn
            prev_act_te = tok_dn
            tkb_war[par] = last_ys

        prev_idx_load = [(f"qi{i}", P.ctr.get(f"qi{i}", 0)) for i in range(2)]
        prev_ys = [(f"qys{i}", P.ctr.get(f"qys{i}", 0)) for i in range(2)]
        prev_te_end2 = prev_act_te

    fin = [(k, v) for k, v in P.ctr.items() if k.startswith("q")]
    P.emit("sy", NOP(), waits=fin)
    P.emit("gp", NOP(), waits=[t for t in fin if t[0].startswith("qys")])
    _realize(nc, P)
    return nc


def _realize(nc, P):
    import contextlib
    with contextlib.ExitStack() as ctx:
        sems = {s: ctx.enter_context(nc.semaphore(s))
                for s in sorted(P.ctr.keys())}
        block = ctx.enter_context(nc.Block())

        def make(eng_name):
            ops = P.ops[eng_name]
            seen = {}

            def body(eng):
                for fn, waits, inc in ops:
                    for s, v in waits:
                        if seen.get(s, -1) < v:
                            eng.wait_ge(sems[s], v)
                            seen[s] = v
                    ins = fn(eng)
                    if inc is not None:
                        ins.then_inc(sems[inc[0]], inc[1])
            return body

        for eng_name, attr in (("te", "tensor"), ("ve", "vector"),
                               ("se", "scalar"), ("sy", "sync"),
                               ("gp", "gpsimd")):
            getattr(block, attr)(make(eng_name))


def prep_inputs(L0_attn_Wq, L0_attn_Wk, L0_attn_Wv, L0_attn_memory,
                L0_attn_memory_norm, L0_attn_memory_weight, L0_ffn_router_w,
                L0_ffn_e_bias, L0_ffn_gate_up_proj, L0_ffn_down_proj,
                L0_ffn_shared_gate_w, L0_ffn_shared_up_w, L0_ffn_shared_down_w):
    f = np.float32
    wq = _lhsT_blocks(np.asarray(L0_attn_Wq, f))
    wk = _lhsT_blocks(np.asarray(L0_attn_Wk, f))
    wv = _rhs_layout(np.asarray(L0_attn_Wv, f))
    mem = _lhsT_blocks(np.asarray(L0_attn_memory, f).T)
    memn = np.ascontiguousarray(
        np.asarray(L0_attn_memory_norm, f).reshape(4, 128, 1)
        .transpose(1, 0, 2).reshape(128, 4))
    mw = np.ascontiguousarray(
        np.broadcast_to(np.asarray(L0_attn_memory_weight, f).reshape(1, 1),
                        (128, 1)))
    ebias = np.ascontiguousarray(
        np.broadcast_to(np.asarray(L0_ffn_e_bias, f).reshape(1, 16), (128, 16)))
    rw = _rhs_layout(np.asarray(L0_ffn_router_w, f))
    gup = np.asarray(L0_ffn_gate_up_proj, f)
    dp = np.asarray(L0_ffn_down_proj, f)
    sh_gu = np.concatenate([np.asarray(L0_ffn_shared_gate_w, f),
                            np.asarray(L0_ffn_shared_up_w, f)], axis=0)
    gu = np.stack([_lhsT_blocks(gup[e]) for e in range(16)]
                  + [_lhsT_blocks(sh_gu)])
    dw = np.stack([_rhs_layout(dp[e]) for e in range(16)]
                  + [_rhs_layout(np.asarray(L0_ffn_shared_down_w, f))])
    idn16 = np.eye(128, dtype=ml_dtypes.bfloat16)
    idn32 = np.eye(128, dtype=np.float32)
    ll = np.triu(np.ones((128, 128), np.float32), 1).astype(ml_dtypes.bfloat16)
    ones1 = np.ones((1, 128), np.float32)
    tid = np.ascontiguousarray(
        (np.arange(8)[None, :] * 128 + np.arange(128)[:, None]).astype(np.int32))
    eoff = np.ascontiguousarray(np.broadcast_to(
        (np.arange(16) * CEXP).astype(np.float32)[None, :], (128, 16)))
    return {"wq": wq, "wk": wk, "wv": wv, "mem": mem, "memn": memn, "mw": mw,
            "ebias": ebias, "rw": rw, "gu": gu, "dw": dw,
            "idn16": idn16, "idn32": idn32, "ll": ll, "ones1": ones1,
            "tid": tid, "eoff": eoff}


def kernel(L0_attn_x, **w):
    common = prep_inputs(**w)
    x = np.ascontiguousarray(np.asarray(L0_attn_x, np.float32))
    nc = build_nc(attn_only=False)
    in_maps = [dict(common, x=np.ascontiguousarray(x[c])) for c in range(8)]
    res = run_bass_kernel_spmd(nc, in_maps, core_ids=list(range(8)))
    return np.stack([np.asarray(r["y"], np.float32) for r in res.results])


# version 14
# speedup vs baseline: 1.1793x; 1.1464x over previous
import numpy as np
import ml_dtypes
import concourse.bass as bass
import concourse.mybir as mybir
from concourse.bass_utils import run_bass_kernel_spmd
from concourse.bass_core_rust import IndirectOffsetOnAxis

F32 = mybir.dt.float32
BF16 = mybir.dt.bfloat16
AF = mybir.ActivationFunctionType
ALU = mybir.AluOpType
AX = mybir.AxisListType

S, D = 2048, 2048
NQ, NH = 4, 8
NE = 17
STILE = 512
NSt = S // STILE
CEXP = 384
I32 = mybir.dt.int32
EPS = 1e-5
NEG_BIG = -1e30
ENGINES = ("te", "ve", "se", "sy", "gp")
_LAST_PROG = [None]


class Ring:
    def __init__(self, slots):
        self.slots = [{"o": s, "free": None} for s in slots]
        self.i = 0

    def next(self):
        s = self.slots[self.i % len(self.slots)]
        self.i += 1
        return s


class Prog:
    def __init__(self):
        self.ops = {e: [] for e in ENGINES}
        self.ctr = {"te": 0, "ve": 0, "se": 0, "ph": 0}

    def emit(self, eng, fn, waits=(), inc=None, inc_n=1):
        flat = []
        for t in waits:
            if t is None:
                continue
            if isinstance(t, list):
                flat.extend(x for x in t if x is not None)
            else:
                flat.append(t)
        w = tuple(flat)
        tok = None
        if inc is not None:
            self.ctr.setdefault(inc, 0)
            self.ctr[inc] += inc_n
            tok = (inc, self.ctr[inc])
        self.ops[eng].append((fn, w, (inc, inc_n) if inc else None))
        return tok


def MM(out_ap, l_ap, r_ap, start, stop, tp=None):
    if tp is None:
        return lambda t: t.matmul(out_ap, l_ap, r_ap, start=start, stop=stop)
    return lambda t: t.matmul(out_ap, l_ap, r_ap, start=start, stop=stop,
                              tile_position=tp)


def TR(out_ap, in_ap, ident_ap):
    return lambda t: t.transpose(out_ap, in_ap, ident_ap)


def CP(out_ap, in_ap):
    return lambda v: v.tensor_copy(out_ap, in_ap)


def TT_(out_ap, a_ap, b_ap, op):
    return lambda v: v.tensor_tensor(out_ap, a_ap, b_ap, op)


def TS(out_ap, in_ap, s1, s2, op0, op1=None):
    if op1 is None:
        return lambda v: v.tensor_scalar(out_ap, in_ap, s1, None, op0)
    return lambda v: v.tensor_scalar(out_ap, in_ap, s1, s2, op0, op1)


def STT(out_ap, in0_ap, sc, in1_ap, op0, op1):
    return lambda v: v.scalar_tensor_tensor(out_ap, in0_ap, sc, in1_ap, op0, op1)


def ACT(out_ap, in_ap, func, bias=0.0, scale=1.0):
    return lambda s: s.activation(out_ap, in_ap, func, bias=bias, scale=scale)


def DMA(out_ap, in_ap):
    return lambda e: e.dma_start(out=out_ap, in_=in_ap)


def RED(out_ap, in_ap, op):
    return lambda v: v.tensor_reduce(out_ap, in_ap, AX.X, op)


def RECIP(out_ap, in_ap):
    return lambda v: v.reciprocal(out_ap, in_ap)


def MEMSET(ap, val):
    return lambda v: v.memset(ap, val)


def NOP():
    return lambda e: e.nop()


def VNOP():
    return lambda v: v.engine_nop()


def _lhsT_blocks(W):
    dout, din = W.shape
    Ki, nm = din // 128, dout // 128
    A = np.ascontiguousarray(W.T).reshape(Ki, 128, nm, 128)
    A = np.ascontiguousarray(A.transpose(2, 1, 0, 3))
    return A.reshape(nm, 128, Ki * 128)


def _rhs_layout(W):
    dout, din = W.shape
    Ki = din // 128
    B = np.ascontiguousarray(W.T).reshape(Ki, 128, dout).transpose(1, 0, 2)
    return np.ascontiguousarray(B).reshape(128, Ki * dout)


def build_nc(attn_only=False, dbg=None):
    nc = bass.Bass("TRN2", target_bir_lowering=False)
    P = Prog()
    _LAST_PROG[0] = P

    x_d = nc.dram_tensor("x", [S, D], F32, kind="ExternalInput")
    wq_d = nc.dram_tensor("wq", [16, 128, 2048], F32, kind="ExternalInput")
    wk_d = nc.dram_tensor("wk", [4, 128, 2048], F32, kind="ExternalInput")
    wv_d = nc.dram_tensor("wv", [128, 8192], F32, kind="ExternalInput")
    mem_d = nc.dram_tensor("mem", [4, 128, 512], F32, kind="ExternalInput")
    memn_d = nc.dram_tensor("memn", [128, 4], F32, kind="ExternalInput")
    mw_d = nc.dram_tensor("mw", [128, 1], F32, kind="ExternalInput")
    ebias_d = nc.dram_tensor("ebias", [128, 16], F32, kind="ExternalInput")
    rw_d = nc.dram_tensor("rw", [128, 256], F32, kind="ExternalInput")
    gu_d = nc.dram_tensor("gu", [NE, 12, 128, 2048], F32, kind="ExternalInput")
    dw_d = nc.dram_tensor("dw", [NE, 128, 12288], F32, kind="ExternalInput")
    idn16_d = nc.dram_tensor("idn16", [128, 128], BF16, kind="ExternalInput")
    idn32_d = nc.dram_tensor("idn32", [128, 128], F32, kind="ExternalInput")
    x1_d = nc.dram_tensor("x1buf", [S, D], F32, kind="Internal")
    xn2n_d = nc.dram_tensor("xn2nat", [S, D], BF16, kind="Internal")
    idxcw_d = nc.dram_tensor("idxcw", [16 * CEXP, 2], I32, kind="Internal")
    ll_d = nc.dram_tensor("ll", [128, 128], BF16, kind="ExternalInput")
    ones1_d = nc.dram_tensor("ones1", [1, 128], F32, kind="ExternalInput")
    tid_d = nc.dram_tensor("tid", [128, 8], I32, kind="ExternalInput")
    eoff_d = nc.dram_tensor("eoff", [128, 16], F32, kind="ExternalInput")
    y_d = nc.dram_tensor("y", [S, D], F32, kind="ExternalOutput")
    attn_out_d = y_d if attn_only else x1_d

    stp0 = nc.alloc_psum_tensor("stp0", [128, 1024], F32)
    stp1 = nc.alloc_psum_tensor("stp1", [128, 1024], F32)
    pvp0 = nc.alloc_psum_tensor("pvp0", [128, 512], F32)
    pvp1 = nc.alloc_psum_tensor("pvp1", [128, 512], F32)
    pool = Ring([stp0[:, 0:512], stp0[:, 512:1024],
                 stp1[:, 0:512], stp1[:, 512:1024],
                 pvp0[:, :], pvp1[:, :]])
    ptb = Ring([nc.alloc_psum_tensor(f"pt{i}", [128, 1024], BF16) for i in range(2)])

    ARENA = 212736
    arena = nc.alloc_sbuf_tensor("arena", [128, ARENA], mybir.dt.uint8)
    base = nc.lookup_mloc(arena).addr
    cursor = [0]

    def place(name, shape, dtype, align=128):
        nbytes = int(np.prod(shape[1:])) * mybir.dt.size(dtype)
        off = cursor[0]
        cursor[0] = (off + nbytes + align - 1) & ~(align - 1)
        return nc.alloc_sbuf_tensor_at(name, list(shape), dtype,
                                       offset=base + off)

    idn16 = place("idn16s", [128, 128], BF16)
    idn32 = place("idn32s", [128, 128], F32)
    sq = place("sq", [128, 2048], F32)
    tiny = place("tiny", [128, 16], F32)
    small = place("small", [128, 16], F32)
    wsig = place("wsig", [128, 2], F32)
    epsc = place("epsc", [128, 1], F32)
    region0 = cursor[0]

    wbuf = place("wbuf", [128, 16384], BF16)
    memB = place("memB", [128, 2048], BF16)
    memn4 = place("memn4", [128, 4], BF16)
    kT = place("kT", [128, 4 * 2048], BF16)
    v_ext = place("v_ext", [128, 16 * 520 + 128], BF16)
    xn1T = place("xn1T", [128, 16 * 512], BF16)
    qT = place("qT", [128, 16 * 512], BF16)
    sgT = place("sgT", [128, 16 * 512], BF16)
    acc = place("acc", [128, 4 * 2048], F32)
    xn1c = place("xn1c", [128, 2048], BF16)
    oT_sb = place("oT_sb", [128, 1024], BF16)
    amT_sb = place("amT_sb", [128, 4 * 512], F32)
    amden = place("amden", [1, 512], F32)
    lnm = place("lnm", [128, 16], F32)
    lnr = place("lnr", [128, 16], F32)
    ptc = Ring([place(f"ptc{i}", [128, 1024], BF16) for i in range(3)])
    xring = Ring([place(f"xr{i}", [128, 2048], F32) for i in range(2)])
    attn_end = cursor[0]
    assert attn_end <= ARENA, f"attn SBUF over: {attn_end}"

    cursor[0] = region0
    y_acc = place("y_acc", [128, 8 * 2048], F32)
    xn2T_off = cursor[0]
    xn2T = place("xn2T", [128, 16 * 1024], BF16)
    xe_nat = nc.alloc_sbuf_tensor_at("xe_nat", [128, 3 * 2048], BF16,
                                     offset=base + xn2T_off)
    xeT = nc.alloc_sbuf_tensor_at("xeT", [128, 16 * CEXP], BF16,
                                  offset=base + xn2T_off + 3 * 2048 * 2)
    actB = place("actB", [128, 6 * 1024], BF16)
    y_sp = place("y_sp", [128, 2 * 2048], F32)
    m_bf = place("m_bf", [128, 128], BF16)
    Pn = place("Pn", [128, 128], F32)
    trow0 = place("trow0", [1, 128], F32)
    trow1 = place("trow1", [1, 128], F32)
    onesc = place("onesc", [128, 1], BF16)
    posf = place("posf", [128, 4], F32)
    cwf = place("cwf", [128, 4], F32)
    posi = place("posi", [128, 4], I32)
    tk2 = place("tk2", [128, 8], I32)
    tksb = [place(f"tksb{i}", [128, 6], I32) for i in range(2)]
    ixc = [[place(f"ixc{i}_{j}", [128, 1], I32, align=32) for j in range(3)]
           for i in range(2)]
    Ll = place("Lls", [128, 128], BF16)
    ones1 = place("ones1s", [1, 128], F32)
    tid = place("tids", [128, 8], I32)
    eoffC = place("eoffs", [128, 16], F32)
    wdr = Ring([place(f"wd{i}", [128, 6 * 2048], BF16) for i in range(2)])
    gur = Ring([place(f"gur{i}", [128, 2048], BF16) for i in range(3)])
    xn2c = place("xn2c", [128, 2048], BF16)
    sgbuf = place("sgbuf", [128, 1024], F32)
    scores = place("scores", [128, 8 * 16], F32)
    combine = place("combine", [128, 8 * 16], F32)
    ft = place("ft", [128, 96], F32)
    ebias_s = place("ebias_s", [128, 16], F32)
    rwT = place("rwT", [128, 256], BF16)
    m2 = place("m2", [128, 8], F32)
    r2 = place("r2", [128, 8], F32)
    ffn_end = cursor[0]
    assert ffn_end <= ARENA, f"ffn SBUF over: {ffn_end}"

    elu1, elu2, elu3 = sq[:, 0:512], sq[:, 512:1024], sq[:, 1024:1536]

    ln_state = {"prev_se": None, "prev_rd": None}

    def ln_stats(xc_ap, mcol_ap, rcol_ap, deps):
        t_sq = P.emit("ve", TT_(sq[:, :], xc_ap, xc_ap, ALU.mult),
                      waits=list(deps) + [ln_state["prev_se"],
                                          ln_state["prev_rd"]], inc="ve")
        t0 = P.emit("ve", RED(tiny[:, 0:1], xc_ap, ALU.add), inc="ve")
        t1 = P.emit("ve", RED(tiny[:, 1:2], sq[:, :], ALU.add), waits=[t_sq],
                    inc="ve")
        ln_state["prev_rd"] = t1
        t2 = P.emit("ve", TS(tiny[:, 2:3], tiny[:, 0:1], 1.0 / D, None,
                             ALU.mult), waits=[t0], inc="ve")
        P.emit("ve", CP(mcol_ap, tiny[:, 2:3]), waits=[t2])
        t3 = P.emit("ve", TS(tiny[:, 3:4], tiny[:, 1:2], 1.0 / D, None,
                             ALU.mult), waits=[t1], inc="ve")
        t4 = P.emit("ve", TT_(tiny[:, 4:5], tiny[:, 2:3], tiny[:, 2:3],
                              ALU.mult), waits=[t2], inc="ve")
        tv = P.emit("ve", TT_(tiny[:, 5:6], tiny[:, 3:4], tiny[:, 4:5],
                              ALU.subtract), waits=[t3, t4], inc="ve")
        tl = P.emit("se", ACT(tiny[:, 6:7], tiny[:, 5:6], AF.Ln,
                              bias=epsc[:, 0:1]), waits=[tv], inc="se")
        ts = P.emit("se", ACT(rcol_ap, tiny[:, 6:7], AF.Exp, scale=-0.5),
                    waits=[tl], inc="se")
        ln_state["prev_se"] = ts
        return ts

    def transpose_to_T(src_chunk, dstT, dst_col, tok_w, deps):
        last = None
        for half in range(2):
            b = ptb.next()
            tok = None
            for j in range(8):
                k = half * 8 + j
                w = (list(deps) + [b["free"]]) if j == 0 else []
                tok = P.emit("te", TR(b["o"][:, j * 128:(j + 1) * 128],
                                      src_chunk[:, k * 128:(k + 1) * 128],
                                      idn16[:, :]),
                             waits=w, inc="te" if j == 7 else None)
            src = b["o"][:, 0:1024].rearrange("p (k t) -> p k t", t=128)
            dst = dstT[:, :].rearrange("p (k t) -> p k t", t=tok_w)
            last = P.emit("ve", CP(dst[:, half * 8:half * 8 + 8,
                                       dst_col:dst_col + 128], src),
                          waits=[tok], inc="ve")
            b["free"] = last
        return last

    def barrier():
        base = P.ctr["ph"]
        qtot = [(k, v) for k, v in P.ctr.items() if k.startswith("q")]
        P.emit("sy", NOP(), waits=qtot, inc="ph")
        P.emit("gp", NOP(), waits=qtot, inc="ph")
        for eng in ("te", "ve", "se"):
            P.emit(eng, NOP(), inc="ph")
        for eng in ENGINES:
            P.emit(eng, NOP(), waits=[("ph", base + 5)])

    P.emit("sy", DMA(idn16[:, :], idn16_d.ap()), inc="qcs", inc_n=16)
    P.emit("sy", DMA(idn32[:, :], idn32_d.ap()), inc="qcs", inc_n=16)
    P.emit("sy", DMA(small[:, 0:1], mw_d.ap()), inc="qcs", inc_n=16)
    for m in range(4):
        P.emit("gp", DMA(wbuf[:, m * 2048:(m + 1) * 2048], wk_d.ap()[m]),
               inc="qcg", inc_n=16)
    P.emit("gp", DMA(wbuf[:, 8192:16384], wv_d.ap()), inc="qcg", inc_n=16)
    for m in range(4):
        P.emit("gp", DMA(memB[:, m * 512:(m + 1) * 512], mem_d.ap()[m]),
               inc="qcg", inc_n=16)
    P.emit("gp", DMA(memn4[:, :], memn_d.ap()), inc="qcg", inc_n=16)
    consts_done = [("qcs", P.ctr["qcs"]), ("qcg", P.ctr["qcg"])]
    t_idn16 = t_idn32 = t_mw = t_wk = t_wv = t_mem = t_memn = consts_done

    P.emit("ve", MEMSET(epsc[:, 0:1], EPS))
    t1 = P.emit("se", ACT(small[:, 1:2], small[:, 0:1], AF.Exp, scale=-1.0),
                waits=[t_mw], inc="se")
    t2 = P.emit("ve", TS(small[:, 2:3], small[:, 1:2], 1.0, None, ALU.add),
                waits=[t1], inc="ve")
    t3 = P.emit("ve", RECIP(wsig[:, 0:1], small[:, 2:3]), waits=[t2], inc="ve")
    P.emit("ve", TS(wsig[:, 1:2], wsig[:, 0:1], -1.0, 1.0, ALU.mult, ALU.add),
           waits=[t3], inc="ve")

    P.emit("ve", MEMSET(v_ext[:, 16 * 520:16 * 520 + 128], 0.0))
    ve_view = v_ext[:, 0:16 * 520].rearrange("p (t c) -> p t c", c=520)
    for h in range(NH):
        P.emit("ve", MEMSET(ve_view[:, :, h * 65 + 64:h * 65 + 65], 1.0))
    t_ones = P.emit("ve", VNOP(), inc="ve")

    def dbg_finish(pairs):
        for r0, nr, src in pairs:
            c = nr // 128
            dst = y_d.ap()[r0:r0 + nr, :].rearrange("(c p) f -> p c f", c=c)
            sview = src.rearrange("p (c f) -> p c f", c=c)
            P.emit("gp", DMA(dst, sview), waits=[("te", P.ctr["te"]),
                                                 ("ve", P.ctr["ve"]),
                                                 ("se", P.ctr["se"])],
                   inc="qdbg", inc_n=16)
        P.emit("gp", NOP(), waits=[("qdbg", P.ctr.get("qdbg", 0))])
        _realize(nc, P)
        return nc

    if dbg == "const":
        return dbg_finish([])

    for tile in range(4):
        drain_toks = []
        for lc in range(4):
            ci = tile * 4 + lc
            xs = xring.next()
            xsem = f"qx{ci % 2}"
            t_x = P.emit("sy", DMA(xs["o"][:, :],
                                   x_d.ap()[ci * 128:(ci + 1) * 128, :]),
                         waits=[xs["free"]], inc=xsem, inc_n=16)
            ts = ln_stats(xs["o"][:, :], lnm[:, ci:ci + 1], lnr[:, ci:ci + 1],
                          [t_x])
            t_n = P.emit("ve", TS(xn1c[:, :], xs["o"][:, :], lnm[:, ci:ci + 1],
                                  lnr[:, ci:ci + 1], ALU.subtract, ALU.mult),
                         waits=[ts], inc="ve")
            xs["free"] = t_n
            drain_toks.append(
                transpose_to_T(xn1c[:, :], xn1T, lc * 128, 512, [t_n]))
        for m in range(4):
            b = pool.next()
            tok = None
            for ki in range(16):
                w = ([t_wk, b["free"]] + drain_toks) if ki == 0 else []
                tok = P.emit("te", MM(
                    b["o"][:, :],
                    wbuf[:, m * 2048 + ki * 128:m * 2048 + ki * 128 + 128],
                    xn1T[:, ki * 512:(ki + 1) * 512], ki == 0, ki == 15),
                    waits=w, inc="te" if ki == 15 else None)
            b["free"] = P.emit("ve", CP(
                kT[:, m * 2048 + tile * 512:m * 2048 + tile * 512 + 512],
                b["o"][:, :]), waits=[tok], inc="ve")
        for mt in range(4):
            b = pool.next()
            tc = tile * 4 + mt
            tok = None
            for ki in range(16):
                w = ([t_wv, b["free"]] + drain_toks) if ki == 0 else []
                tok = P.emit("te", MM(
                    b["o"][:, :],
                    xn1T[:, ki * 512 + mt * 128:ki * 512 + mt * 128 + 128],
                    wbuf[:, 8192 + ki * 512:8192 + (ki + 1) * 512],
                    ki == 0, ki == 15),
                    waits=w, inc="te" if ki == 15 else None)
            lastd = None
            for h in range(NH):
                lastd = P.emit("ve", CP(
                    v_ext[:, tc * 520 + h * 65:tc * 520 + h * 65 + 64],
                    b["o"][:, h * 64:(h + 1) * 64]),
                    waits=[tok, t_ones] if h == 0 else [],
                    inc="ve" if h == NH - 1 else None)
            b["free"] = lastd

    barrier()

    if dbg == "pass1":
        for src, r0, c0, w_ in ((lnm, 1024, 0, 16), (lnr, 1024, 16, 16)):
            P.emit("gp", DMA(y_d.ap()[r0:r0 + 128, c0:c0 + w_], src[:, 0:w_]),
                   waits=[("te", P.ctr["te"]), ("ve", P.ctr["ve"]),
                          ("se", P.ctr["se"])], inc="qdbg", inc_n=16)
        P.emit("gp", DMA(y_d.ap()[1152:1280, :], xn1c[:, :]),
               inc="qdbg", inc_n=16)
        return dbg_finish([(0, 512, kT[:, :]), (512, 512, xn1T[:, :])])

    wq_ring = Ring([i * 2048 for i in range(4)])
    prev_te_end = None
    acc_war = None
    for st in range(NSt):
        acc_ready = []
        for lc in range(4):
            ci = st * 4 + lc
            acc_ready.append(P.emit("sy", DMA(
                acc[:, lc * 2048:(lc + 1) * 2048],
                x_d.ap()[ci * 128:(ci + 1) * 128, :]),
                waits=[acc_war], inc=f"qa{lc}", inc_n=16))
        xn1_toks = []
        for lc in range(4):
            ci = st * 4 + lc
            t_n = P.emit("ve", TS(xn1c[:, :], acc[:, lc * 2048:(lc + 1) * 2048],
                                  lnm[:, ci:ci + 1], lnr[:, ci:ci + 1],
                                  ALU.subtract, ALU.mult),
                         waits=[acc_ready[lc], prev_te_end], inc="ve")
            xn1_toks.append(transpose_to_T(xn1c[:, :], xn1T, lc * 128, 512, [t_n]))
        prev_elu_se = None
        prev_elu_ve = None
        for m in range(16):
            slot = wq_ring.next()
            col = slot["o"]
            t_w = P.emit("gp", DMA(wbuf[:, col:col + 2048], wq_d.ap()[m]),
                         waits=[slot["free"]], inc=f"qw{m % 4}", inc_n=16)
            b = pool.next()
            tok = None
            for ki in range(16):
                w = ([t_w, b["free"]] + xn1_toks) if ki == 0 else []
                tok = P.emit("te", MM(
                    b["o"][:, :],
                    wbuf[:, col + ki * 128:col + ki * 128 + 128],
                    xn1T[:, ki * 512:(ki + 1) * 512], ki == 0, ki == 15),
                    waits=w, inc="te" if ki == 15 else None)
            slot["free"] = tok
            P.emit("ve", CP(qT[:, m * 512:(m + 1) * 512], b["o"][:, :]),
                   waits=[tok])
            t_mn = P.emit("ve", TS(elu1, b["o"][:, :], 0.0, None, ALU.min),
                          waits=[prev_elu_se], inc="ve")
            t_e = P.emit("se", ACT(elu2, elu1, AF.Exp),
                         waits=[t_mn, prev_elu_ve], inc="se")
            prev_elu_se = t_e
            t_mx = P.emit("ve", TS(elu3, b["o"][:, :], 0.0, None, ALU.max),
                          waits=[prev_elu_ve], inc="ve")
            t_s = P.emit("ve", STT(sgT[:, m * 512:(m + 1) * 512], elu2, -1.0,
                                   elu3, ALU.add, ALU.add),
                         waits=[t_e, t_mx], inc="ve")
            prev_elu_ve = t_s
            b["free"] = t_s
        q_done = P.emit("ve", VNOP(), inc="ve")

        stpt = [{"o": stp0, "free": [pool.slots[0]["free"], pool.slots[1]["free"]]},
                {"o": stp1, "free": [pool.slots[2]["free"], pool.slots[3]["free"]]}]
        pvt = [{"o": pvp0, "free": [pool.slots[4]["free"]]},
               {"o": pvp1, "free": [pool.slots[5]["free"]]}]
        first_heads = [q_done]
        ep_state = {"prev_tt": None}
        for qg in range(NQ):
            for hp in range(4):
                h0, h1 = 2 * hp, 2 * hp + 1
                mcol = (qg * 4 + hp) * 512
                pend = None

                def emit_pv(pd, h0=h0, h1=h1):
                    j, pc, tex = pd
                    t0_ = P.emit("te", MM(
                        pvt[0]["o"][:, :],
                        v_ext[:, j * 520 + h0 * 65:j * 520 + h0 * 65 + 128],
                        pc["o"][:, 0:512], j == 0, j == 15),
                        waits=[tex] + (pvt[0]["free"] if j == 0 else []),
                        inc="te" if j == 15 else None)
                    t1_ = P.emit("te", MM(
                        pvt[1]["o"][:, :],
                        v_ext[:, j * 520 + h1 * 65:j * 520 + h1 * 65 + 128],
                        pc["o"][:, 512:1024], j == 0, j == 15),
                        waits=(pvt[1]["free"] if j == 0 else []),
                        inc="te" if j == 15 else None)
                    pc["free"] = t1_ if j == 15 else t0_
                    return t0_, t1_

                for i in range(16):
                    sp = stpt[i % 2]
                    w = [sp["free"]]
                    if qg == 0 and hp == 0 and i < 2:
                        w = w + first_heads
                    P.emit("te", MM(
                        sp["o"][:, 0:512],
                        kT[0:64, hp * 2048 + i * 128:hp * 2048 + i * 128 + 128],
                        qT[0:64, mcol:mcol + 512], True, True, tp=(0, 0)),
                        waits=w)
                    tqk1 = P.emit("te", MM(
                        sp["o"][:, 512:1024],
                        kT[64:128, hp * 2048 + i * 128:hp * 2048 + i * 128 + 128],
                        qT[64:128, mcol:mcol + 512], True, True, tp=(64, 0)),
                        inc="te")
                    pc = ptc.next()
                    tex = P.emit("se", ACT(pc["o"][:, 0:1024], sp["o"][:, 0:1024],
                                           AF.Exp, scale=0.125),
                                 waits=[tqk1, pc["free"]], inc="se")
                    sp["free"] = [tex]
                    if pend is not None:
                        emit_pv(pend)
                    pend = (i, pc, tex)
                tpv0, tpv1 = emit_pv(pend)

                td0 = P.emit("ve", CP(oT_sb[0:65, 0:512], pvt[0]["o"][0:65, :]),
                             waits=[tpv0, ep_state["prev_tt"]], inc="ve")
                td1 = P.emit("ve", CP(oT_sb[0:65, 512:1024], pvt[1]["o"][0:65, :]),
                             waits=[tpv1], inc="ve")
                ta0 = ta1 = None
                for sc in range(4):
                    btA = ptb.next()
                    tt0 = P.emit("te", TR(btA["o"][:, 0:65],
                                          oT_sb[0:65, sc * 128:sc * 128 + 128],
                                          idn16[0:65, 0:65]),
                                 waits=[td0, btA["free"], t_idn16], inc="te")
                    btB = ptb.next()
                    tt1 = P.emit("te", TR(
                        btB["o"][:, 0:65],
                        oT_sb[0:65, 512 + sc * 128:512 + sc * 128 + 128],
                        idn16[0:65, 0:65]),
                        waits=[td1, btB["free"]], inc="te")
                    if sc == 3:
                        ep_state["prev_tt"] = tt1
                    tr = P.emit("ve", RECIP(small[:, 4:5], btA["o"][:, 64:65]),
                                waits=[tt0], inc="ve")
                    tw_ = P.emit("ve", TT_(small[:, 5:6], small[:, 4:5],
                                           wsig[:, 1:2], ALU.mult),
                                 waits=[tr], inc="ve")
                    c0 = sc * 2048 + qg * 512 + h0 * 64
                    ta0 = P.emit("ve", STT(acc[:, c0:c0 + 64],
                                           btA["o"][:, 0:64],
                                           small[:, 5:6], acc[:, c0:c0 + 64],
                                           ALU.mult, ALU.add),
                                 waits=[tw_], inc="ve")
                    btA["free"] = ta0
                    tr1 = P.emit("ve", RECIP(small[:, 6:7], btB["o"][:, 64:65]),
                                 waits=[tt1], inc="ve")
                    tw1 = P.emit("ve", TT_(small[:, 7:8], small[:, 6:7],
                                           wsig[:, 1:2], ALU.mult),
                                 waits=[tr1], inc="ve")
                    c1 = sc * 2048 + qg * 512 + h1 * 64
                    ta1 = P.emit("ve", STT(acc[:, c1:c1 + 64],
                                           btB["o"][:, 0:64],
                                           small[:, 7:8], acc[:, c1:c1 + 64],
                                           ALU.mult, ALU.add),
                                 waits=[tw1], inc="ve")
                    btB["free"] = ta1
                pvt[0]["free"] = [td0]
                pvt[1]["free"] = [td1]

        pool.slots[0]["free"] = stpt[0]["free"][0]
        pool.slots[1]["free"] = stpt[0]["free"][0]
        pool.slots[2]["free"] = stpt[1]["free"][0]
        pool.slots[3]["free"] = stpt[1]["free"][0]
        pool.slots[4]["free"] = pvt[0]["free"][0]
        pool.slots[5]["free"] = pvt[1]["free"][0]
        pool.i = 0

        for qg in range(NQ):
            bD = pool.next()
            tokD = None
            for ii in range(4):
                w = [q_done, t_memn, bD["free"]] if ii == 0 else []
                tokD = P.emit("te", MM(
                    bD["o"][0:1, :], memn4[:, ii:ii + 1],
                    sgT[:, (qg * 4 + ii) * 512:(qg * 4 + ii + 1) * 512],
                    ii == 0, ii == 3),
                    waits=w, inc="te" if ii == 3 else None)
            jtoks = []
            for jj in range(4):
                b = pool.next()
                tok = None
                for ii in range(4):
                    w = [t_mem, b["free"]] if ii == 0 else []
                    tok = P.emit("te", MM(
                        b["o"][:, :],
                        memB[:, jj * 512 + ii * 128:jj * 512 + ii * 128 + 128],
                        sgT[:, (qg * 4 + ii) * 512:(qg * 4 + ii + 1) * 512],
                        ii == 0, ii == 3),
                        waits=w, inc="te" if ii == 3 else None)
                t_am = P.emit("ve", CP(amT_sb[:, jj * 512:(jj + 1) * 512],
                                       b["o"][:, :]), waits=[tok], inc="ve")
                b["free"] = t_am
                jtoks.append(t_am)
            t_dn = P.emit("ve", CP(amden[0:1, :], bD["o"][0:1, :]),
                          waits=[tokD], inc="ve")
            bD["free"] = t_dn
            bDn = pool.next()
            prev_dr = None
            for sc in range(4):
                w = [t_dn] + ([bDn["free"]] if sc == 0 else [prev_dr])
                tsp = P.emit("te", TR(bDn["o"][:, 0:1],
                                      amden[0:1, sc * 128:(sc + 1) * 128],
                                      idn32[0:1, 0:1]),
                             waits=w, inc="te")
                prev_dr = P.emit("ve", CP(tiny[:, 10 + sc:11 + sc],
                                          bDn["o"][:, 0:1]),
                                 waits=[tsp], inc="ve")
            tdn = prev_dr
            last_sc = None
            for sc in range(4):
                trc = P.emit("ve", RECIP(tiny[:, 8:9], tiny[:, 10 + sc:11 + sc]),
                             waits=[tdn], inc="ve")
                tww = P.emit("ve", TT_(tiny[:, 9:10], tiny[:, 8:9], wsig[:, 0:1],
                                       ALU.mult), waits=[trc], inc="ve")
                for jj in range(4):
                    bA2 = pool.next()
                    ttj = P.emit("te", TR(
                        bA2["o"][:, 0:128],
                        amT_sb[:, jj * 512 + sc * 128:jj * 512 + sc * 128 + 128],
                        idn32[:, :]),
                        waits=[jtoks[jj], bA2["free"]], inc="te")
                    cc = sc * 2048 + qg * 512 + jj * 128
                    last_sc = P.emit("ve", STT(
                        acc[:, cc:cc + 128], bA2["o"][:, 0:128], tiny[:, 9:10],
                        acc[:, cc:cc + 128], ALU.mult, ALU.add),
                        waits=[ttj, tww], inc="ve")
                    bA2["free"] = last_sc
            bDn["free"] = tdn

        t_all = P.emit("ve", VNOP(), inc="ve")
        for lc in range(4):
            ci = st * 4 + lc
            P.emit("sy", DMA(attn_out_d.ap()[ci * 128:(ci + 1) * 128, :],
                             acc[:, lc * 2048:(lc + 1) * 2048]),
                   waits=[t_all], inc="qs", inc_n=16)
        acc_war = ("qs", P.ctr["qs"])
        prev_te_end = P.emit("te", NOP(), inc="te")

    attn_store_ctr = P.ctr["qs"]

    if attn_only:
        P.emit("sy", NOP(), waits=[("qs", P.ctr["qs"])])
        _realize(nc, P)
        return nc

    barrier()

    P.emit("sy", DMA(ebias_s[:, :], ebias_d.ap()), inc="qc2s", inc_n=16)
    P.emit("gp", DMA(rwT[:, :], rw_d.ap()), inc="qc2g", inc_n=16)
    P.emit("sy", DMA(Ll[:, :], ll_d.ap()), inc="qc2s", inc_n=16)
    P.emit("sy", DMA(ones1[0:1, :], ones1_d.ap()), inc="qc2s", inc_n=16)
    P.emit("sy", DMA(tid[:, :], tid_d.ap()), inc="qc2s", inc_n=16)
    P.emit("sy", DMA(eoffC[:, :], eoff_d.ap()), inc="qc2s", inc_n=16)
    t_eb = t_rw = [("qc2s", P.ctr["qc2s"]), ("qc2g", P.ctr["qc2g"])]
    t_onc = P.emit("ve", MEMSET(onesc[:, :], 1.0), inc="ve")
    oob_view = y_sp[:, 0:96].bitcast(I32)

    prev_y_store = ("qs", attn_store_ctr)
    prev_te_end2 = None
    prev_idx_load = None
    prev_act_te = None
    prev_sg = [None, None]
    tk_ring = Ring([0, 1, 2, 3])
    tkb_war = [None, None]
    ysp_ring = Ring([0, 1])

    dense_state = {"last_gu": None}

    def dense_branch(e, xn2_toks):
        nonlocal prev_act_te
        wi = wdr.i % 2
        wd_slot = wdr.next()
        t_wd = P.emit("gp", DMA(wd_slot["o"][:, :], dw_d.ap()[e]),
                      waits=[wd_slot["free"]], inc=f"qd{wi}", inc_n=16)
        act_toks = []
        for i in range(6):
            pair = []
            for m in (i, i + 6):
                gi = gur.i % 3
                slot = gur.next()
                t_w = P.emit("gp", DMA(slot["o"][:, :], gu_d.ap()[e, m]),
                             waits=[slot["free"]], inc=f"qgu{gi}", inc_n=16)
                bh = [pool.next(), pool.next()]
                tok = None
                for ki in range(16):
                    for half in range(2):
                        w = []
                        if ki == 0:
                            w = [t_w, bh[half]["free"]]
                            if half == 0:
                                w += xn2_toks
                        tok = P.emit("te", MM(
                            bh[half]["o"][:, :],
                            slot["o"][:, ki * 128:(ki + 1) * 128],
                            xn2T[:, ki * 1024 + half * 512:ki * 1024 + half * 512 + 512],
                            ki == 0, ki == 15),
                            waits=w,
                            inc="te" if (ki == 15 and half == 1) else None)
                slot["free"] = tok
                pair.append((bh, tok))
            dense_state["last_gu"] = pair[1][1]
            (bg, tokg), (bu, toku) = pair
            for half in range(2):
                t_sg = P.emit("se", ACT(sgbuf[:, half * 512:(half + 1) * 512],
                                        bg[half]["o"][:, :], AF.Silu),
                              waits=[tokg, prev_sg[half]], inc="se")
                bg[half]["free"] = t_sg
                t_act = P.emit("ve", TT_(
                    actB[:, i * 1024 + half * 512:i * 1024 + half * 512 + 512],
                    sgbuf[:, half * 512:(half + 1) * 512],
                    bu[half]["o"][:, :], ALU.mult),
                    waits=[t_sg, toku, prev_act_te], inc="ve")
                bu[half]["free"] = t_act
                prev_sg[half] = t_act
                act_toks.append(t_act)
        tok_dn = None
        for tc in range(8):
            for ncol in range(4):
                b = pool.next()
                for ki in range(6):
                    w = ([t_wd, b["free"]] + act_toks) if ki == 0 else []
                    tok_dn = P.emit("te", MM(
                        b["o"][:, :],
                        actB[:, ki * 1024 + tc * 128:ki * 1024 + tc * 128 + 128],
                        wd_slot["o"][:, ki * 2048 + ncol * 512:ki * 2048 + ncol * 512 + 512],
                        ki == 0, ki == 5),
                        waits=w, inc="te" if ki == 5 else None)
                dst = y_acc[:, tc * 2048 + ncol * 512:tc * 2048 + ncol * 512 + 512]
                t_dr = P.emit("ve", STT(dst, b["o"][:, :], 1.0, dst,
                                        ALU.mult, ALU.add),
                              waits=[tok_dn], inc="ve")
                b["free"] = t_dr
        wd_slot["free"] = tok_dn
        prev_act_te = tok_dn

    prev_ys = []
    for p in range(2):
        t_ms = P.emit("ve", MEMSET(oob_view, 1 << 20),
                      waits=prev_ys, inc="ve")
        t_fill = P.emit("sy", DMA(
            idxcw_d.ap().rearrange("(p f) j -> p (f j)", p=128),
            oob_view), waits=[prev_idx_load, t_ms], inc="qif", inc_n=16)
        fill_tok = ("qif", P.ctr["qif"])
        for _sl in ysp_ring.slots:
            _sl["free"] = fill_tok

        xn2_toks = []
        for c in range(8):
            ci = p * 8 + c
            t_x = P.emit("sy", DMA(y_acc[:, c * 2048:(c + 1) * 2048],
                                   x1_d.ap()[ci * 128:(ci + 1) * 128, :]),
                         waits=[prev_y_store] if c == 0 else [],
                         inc=f"qy{c}", inc_n=16)
            ts2 = ln_stats(y_acc[:, c * 2048:(c + 1) * 2048],
                           m2[:, c:c + 1], r2[:, c:c + 1], [t_x, prev_te_end2])
            t_n = P.emit("ve", TS(xn2c[:, :], y_acc[:, c * 2048:(c + 1) * 2048],
                                  m2[:, c:c + 1], r2[:, c:c + 1],
                                  ALU.subtract, ALU.mult),
                         waits=[ts2], inc="ve")
            P.emit("sy", DMA(
                xn2n_d.ap()[p * 1024 + c * 128:p * 1024 + (c + 1) * 128, :],
                xn2c[:, :]), waits=[t_n], inc="qxn", inc_n=16)
            xn2_toks.append(transpose_to_T(xn2c[:, :], xn2T, c * 128, 1024,
                                           [t_n]))
        xn_ready = ("qxn", P.ctr["qxn"])

        rt_toks = []
        for c in range(8):
            b = pool.next()
            tok = None
            for ki in range(16):
                w = ([t_rw, b["free"]] + xn2_toks) if ki == 0 else []
                tok = P.emit("te", MM(
                    b["o"][:, 0:16],
                    xn2T[:, ki * 1024 + c * 128:ki * 1024 + c * 128 + 128],
                    rwT[:, ki * 16:(ki + 1) * 16], ki == 0, ki == 15),
                    waits=w, inc="te" if ki == 15 else None)
            rt_toks.append((b, tok))

        topk_toks = []
        prev_ft_read = None
        for c in range(8):
            b, tok = rt_toks[c]
            t_ex = P.emit("se", ACT(ft[:, 0:16], b["o"][:, 0:16], AF.Exp,
                                    scale=-1.0), waits=[tok, prev_ft_read],
                          inc="se")
            b["free"] = t_ex
            u = P.emit("ve", TS(ft[:, 16:32], ft[:, 0:16], 1.0, None, ALU.add),
                       waits=[t_ex], inc="ve")
            prev_ft_read = u
            u = P.emit("ve", RECIP(scores[:, c * 16:(c + 1) * 16],
                                   ft[:, 16:32]), waits=[u], inc="ve")
            u = P.emit("ve", TT_(ft[:, 16:32], scores[:, c * 16:(c + 1) * 16],
                                 ebias_s[:, :], ALU.add), waits=[t_eb, u],
                       inc="ve")
            u2 = P.emit("ve", MEMSET(combine[:, c * 16:(c + 1) * 16], 0.0),
                        inc="ve")
            for k in range(4):
                u = P.emit("ve", RED(ft[:, 32:33], ft[:, 16:32], ALU.max),
                           waits=[u], inc="ve")
                u = P.emit("ve", TS(ft[:, 48:64], ft[:, 16:32], ft[:, 32:33],
                                    None, ALU.is_equal), waits=[u], inc="ve")
                u = P.emit("ve", TT_(ft[:, 64:80], ft[:, 48:64],
                                     scores[:, c * 16:(c + 1) * 16], ALU.mult),
                           waits=[u], inc="ve")
                u = P.emit("ve", TT_(combine[:, c * 16:(c + 1) * 16],
                                     combine[:, c * 16:(c + 1) * 16],
                                     ft[:, 64:80], ALU.add),
                           waits=[u, u2], inc="ve")
                u = P.emit("ve", STT(ft[:, 16:32], ft[:, 48:64], NEG_BIG,
                                     ft[:, 16:32], ALU.mult, ALU.add),
                           waits=[u], inc="ve")
            u = P.emit("ve", RED(ft[:, 33:34], combine[:, c * 16:(c + 1) * 16],
                                 ALU.add), waits=[u], inc="ve")
            u = P.emit("ve", TS(ft[:, 34:35], ft[:, 33:34], 1e-20, None,
                                ALU.add), waits=[u], inc="ve")
            u = P.emit("ve", RECIP(ft[:, 35:36], ft[:, 34:35]), waits=[u],
                       inc="ve")
            tk = P.emit("ve", TS(combine[:, c * 16:(c + 1) * 16],
                                 combine[:, c * 16:(c + 1) * 16],
                                 ft[:, 35:36], None, ALU.mult), waits=[u],
                        inc="ve")
            topk_toks.append(tk)

        dense_branch(16, xn2_toks)
        t_shared_done = P.emit("ve", VNOP(), inc="ve")

        for c in range(8):
            ci = p * 8 + c
            P.emit("sy", DMA(y_d.ap()[ci * 128:(ci + 1) * 128, :],
                             y_acc[:, c * 2048:(c + 1) * 2048]),
                   waits=[t_shared_done], inc="qs", inc_n=16)
        y_base_tok = ("qs", P.ctr["qs"])
        prev_y_store = y_base_tok
        t_m = P.emit("ve", TS(m_bf[:, :], combine[:, :], 0.0, None, ALU.is_gt),
                     waits=topk_toks + [t_onc], inc="ve")
        bS = pool.next()
        tS = P.emit("te", MM(bS["o"][0:1, 0:128], onesc[:, 0:1], m_bf[:, :],
                             True, True), waits=[t_m, bS["free"]], inc="te")
        t_r1 = P.emit("ve", CP(trow0[0:1, :], bS["o"][0:1, 0:128]),
                      waits=[tS], inc="ve")
        bS["free"] = t_r1
        u = P.emit("ve", MEMSET(trow1[0:1, :], 0.0), waits=[t_r1], inc="ve")
        u = P.emit("ve", CP(trow1[0:1, 16:128], trow0[0:1, 0:112]),
                   waits=[u], inc="ve")
        bufs = [trow1, trow0]
        for step, sh in enumerate((16, 32, 64)):
            src = bufs[step % 2]
            dst = bufs[(step + 1) % 2]
            u = P.emit("ve", CP(dst[0:1, 0:sh], src[0:1, 0:sh]), waits=[u],
                       inc="ve")
            u = P.emit("ve", TT_(dst[0:1, sh:128], src[0:1, sh:128],
                                 src[0:1, 0:128 - sh], ALU.add), waits=[u],
                       inc="ve")
        pref_buf = trow0
        t_pref = u
        bP = pool.next()
        P.emit("te", MM(bP["o"][:, 0:128], Ll[:, :], m_bf[:, :], True, False),
               waits=[t_m, bP["free"], t_eb], inc=None)
        tP = P.emit("te", MM(bP["o"][:, 0:128], ones1[0:1, :],
                             pref_buf[0:1, :], False, True),
                    waits=[t_pref], inc="te")
        t_Pn = P.emit("ve", CP(Pn[:, :], bP["o"][:, 0:128]), waits=[tP],
                      inc="ve")
        bP["free"] = t_Pn

        for c in range(8):
            cs = slice(c * 16, (c + 1) * 16)
            u = P.emit("ve", TT_(ft[:, 0:16], eoffC[:, :], Pn[:, cs],
                                 ALU.add), waits=[t_Pn], inc="ve")
            u = P.emit("ve", TS(ft[:, 80:96], Pn[:, cs], CEXP - 0.5, None,
                                ALU.is_ge), waits=[u], inc="ve")
            u = P.emit("ve", STT(ft[:, 0:16], ft[:, 80:96], 1.0e6, ft[:, 0:16],
                                 ALU.mult, ALU.add), waits=[u], inc="ve")
            u = P.emit("ve", TT_(ft[:, 16:32], scores[:, cs],
                                 ebias_s[:, :], ALU.add), waits=[u], inc="ve")
            for k in range(4):
                u = P.emit("ve", RED(ft[:, 32:33], ft[:, 16:32], ALU.max),
                           waits=[u], inc="ve")
                u = P.emit("ve", TS(ft[:, 48:64], ft[:, 16:32], ft[:, 32:33],
                                    None, ALU.is_equal), waits=[u], inc="ve")
                si = tk_ring.next()
                sidx = si["o"]
                u = P.emit("ve", TT_(ft[:, 64:80], ft[:, 48:64], ft[:, 0:16],
                                     ALU.mult), waits=[u, si["free"]],
                           inc="ve")
                u = P.emit("ve", RED(posf[:, sidx:sidx + 1], ft[:, 64:80],
                                     ALU.add), waits=[u], inc="ve")
                u = P.emit("ve", TT_(ft[:, 64:80], ft[:, 48:64],
                                     combine[:, cs], ALU.mult),
                           waits=[u], inc="ve")
                u = P.emit("ve", RED(cwf[:, sidx:sidx + 1], ft[:, 64:80],
                                     ALU.add), waits=[u], inc="ve")
                u = P.emit("ve", CP(posi[:, sidx:sidx + 1],
                                    posf[:, sidx:sidx + 1]), waits=[u],
                           inc="ve")
                u = P.emit("ve", TS(tk2[:, 2 * sidx:2 * sidx + 1],
                                    tid[:, c:c + 1], float(p * 1024), None,
                                    ALU.add), waits=[u], inc="ve")
                u = P.emit("ve", CP(
                    tk2[:, 2 * sidx + 1:2 * sidx + 2],
                    cwf[:, sidx:sidx + 1].bitcast(I32)), waits=[u], inc="ve")
                o_ap = idxcw_d.ap()
                off_ap = posi[:, sidx:sidx + 1]
                in_ap = tk2[:, 2 * sidx:2 * sidx + 2]
                tsc = P.emit("gp", (lambda oa=o_ap, fa=off_ap, ia=in_ap:
                             (lambda g: g.indirect_dma_start(
                                 oa, IndirectOffsetOnAxis(fa, 0), ia, None)))(),
                             waits=[u, fill_tok], inc=f"qsc{sidx}", inc_n=16)
                si["free"] = tsc
                u = P.emit("ve", STT(ft[:, 16:32], ft[:, 48:64], NEG_BIG,
                                     ft[:, 16:32], ALU.mult, ALU.add),
                           waits=[u], inc="ve")
        scatters_done = [(f"qsc{i}", P.ctr.get(f"qsc{i}", 0))
                         for i in range(4)]

        prev_gu_read = None
        for e in range(16):
            par = e % 2
            tkb_cur = tksb[par]
            for cc3 in range(3):
                P.emit("sy", DMA(
                    tkb_cur[:, 2 * cc3:2 * cc3 + 2],
                    idxcw_d.ap()[e * CEXP + cc3 * 128:e * CEXP + (cc3 + 1) * 128, :]),
                    waits=(scatters_done + [tkb_war[par]]) if cc3 == 0 else [],
                    inc=f"qi{par}", inc_n=16)
            tkb_tok = (f"qi{par}", P.ctr[f"qi{par}"])
            tcl = None
            for sc in range(3):
                tcl = P.emit("ve", TS(ixc[par][sc][:, 0:1],
                                      tkb_cur[:, 2 * sc:2 * sc + 1],
                                      float(p * 1024 + 1023), None, ALU.min),
                             waits=[tkb_tok] if sc == 0 else [], inc="ve")
            g_toks = []
            for sc in range(3):
                o_ap = xe_nat[:, sc * 2048:(sc + 1) * 2048]
                in_ap = xn2n_d.ap()
                off_ap = ixc[par][sc][:, 0:1]
                tg = P.emit("gp", (lambda oa=o_ap, ia=in_ap, fa=off_ap:
                            (lambda g: g.indirect_dma_start(
                                oa, None, ia, IndirectOffsetOnAxis(fa, 0))))(),
                            waits=[tcl, xn_ready, prev_gu_read,
                                   dense_state["last_gu"] if e == 0 else None],
                            inc=f"qga{sc}", inc_n=16)
                g_toks.append((f"qga{sc}", P.ctr[f"qga{sc}"]))
            tr_last = None
            for sc in range(3):
                for kk in range(16):
                    bt = ptb.next()
                    tr_ = P.emit("te", TR(
                        bt["o"][:, 0:128],
                        xe_nat[:, sc * 2048 + kk * 128:sc * 2048 + kk * 128 + 128],
                        idn16[:, :]),
                        waits=[g_toks[sc], bt["free"]], inc="te")
                    dcol = kk * CEXP + sc * 128
                    o_ap = xeT[:, dcol:dcol + 128]
                    i_ap = bt["o"][:, 0:128]
                    tr_last = P.emit("se", (lambda oa=o_ap, ia=i_ap:
                                     (lambda s: s.copy(oa, ia)))(),
                                     waits=[tr_], inc="se")
                    bt["free"] = tr_last
            xeT_ready = tr_last
            last_tr_te = tr_

            wi = wdr.i % 2
            wd_slot = wdr.next()
            t_wd = P.emit("gp", DMA(wd_slot["o"][:, :], dw_d.ap()[e]),
                          waits=[wd_slot["free"]], inc=f"qd{wi}", inc_n=16)
            act_toks = []
            tok = None
            for i in range(6):
                pair = []
                for m in (i, i + 6):
                    gi = gur.i % 3
                    slot = gur.next()
                    t_w = P.emit("gp", DMA(slot["o"][:, :], gu_d.ap()[e, m]),
                                 waits=[slot["free"]], inc=f"qgu{gi}",
                                 inc_n=16)
                    b = pool.next()
                    for ki in range(16):
                        w = ([t_w, b["free"], xeT_ready] if ki == 0 else [])
                        tok = P.emit("te", MM(
                            b["o"][:, 0:CEXP],
                            slot["o"][:, ki * 128:(ki + 1) * 128],
                            xeT[:, ki * CEXP:(ki + 1) * CEXP],
                            ki == 0, ki == 15),
                            waits=w, inc="te" if ki == 15 else None)
                    slot["free"] = tok
                    pair.append((b, tok))
                (bg, tokg), (bu, toku) = pair
                t_sg = P.emit("se", ACT(sgbuf[:, 0:CEXP], bg["o"][:, 0:CEXP],
                                        AF.Silu),
                              waits=[tokg, prev_sg[0]], inc="se")
                bg["free"] = t_sg
                t_act = P.emit("ve", TT_(
                    actB[:, i * CEXP:(i + 1) * CEXP], sgbuf[:, 0:CEXP],
                    bu["o"][:, 0:CEXP], ALU.mult),
                    waits=[t_sg, toku, prev_act_te], inc="ve")
                bu["free"] = t_act
                prev_sg[0] = t_act
                act_toks.append(t_act)
            prev_gu_read = tok

            tok_dn = None
            last_ys = None
            for sc in range(3):
                ysl = ysp_ring.next()
                ypar = ysl["o"]
                drains = []
                for ncol in range(4):
                    b = pool.next()
                    for ki in range(6):
                        w = ([t_wd, b["free"]] + act_toks) if ki == 0 else []
                        tok_dn = P.emit("te", MM(
                            b["o"][:, :],
                            actB[:, ki * CEXP + sc * 128:ki * CEXP + sc * 128 + 128],
                            wd_slot["o"][:, ki * 2048 + ncol * 512:ki * 2048 + ncol * 512 + 512],
                            ki == 0, ki == 5),
                            waits=w, inc="te" if ki == 5 else None)
                    t_dr = P.emit("ve", TS(
                        y_sp[:, ypar * 2048 + ncol * 512:ypar * 2048 + ncol * 512 + 512],
                        b["o"][:, :],
                        tkb_cur[:, 2 * sc + 1:2 * sc + 2].bitcast(F32),
                        None, ALU.mult),
                        waits=[tok_dn, ysl["free"]], inc="ve")
                    drains.append(t_dr)
                    b["free"] = t_dr
                o_ap = y_d.ap()
                off_ap = tkb_cur[:, 2 * sc:2 * sc + 1]
                i_ap = y_sp[:, ypar * 2048:(ypar + 1) * 2048]
                last_ys = P.emit("gp", (lambda oa=o_ap, fa=off_ap, ia=i_ap:
                                 (lambda g: g.indirect_dma_start(
                                     oa, IndirectOffsetOnAxis(fa, 0), ia, None,
                                     compute_op=ALU.add)))(),
                                 waits=drains + [y_base_tok],
                                 inc=f"qys{ypar}", inc_n=16)
                ysl["free"] = last_ys
            wd_slot["free"] = tok_dn
            prev_act_te = tok_dn
            tkb_war[par] = last_ys

        prev_idx_load = [(f"qi{i}", P.ctr.get(f"qi{i}", 0)) for i in range(2)]
        prev_ys = [(f"qys{i}", P.ctr.get(f"qys{i}", 0)) for i in range(2)]
        prev_te_end2 = prev_act_te

    fin = [(k, v) for k, v in P.ctr.items() if k.startswith("q")]
    P.emit("sy", NOP(), waits=fin)
    P.emit("gp", NOP(), waits=[t for t in fin if t[0].startswith("qys")])
    _realize(nc, P)
    return nc


def _realize(nc, P):
    import contextlib
    with contextlib.ExitStack() as ctx:
        sems = {s: ctx.enter_context(nc.semaphore(s))
                for s in sorted(P.ctr.keys())}
        block = ctx.enter_context(nc.Block())

        def make(eng_name):
            ops = P.ops[eng_name]
            seen = {}

            def body(eng):
                for fn, waits, inc in ops:
                    for s, v in waits:
                        if seen.get(s, -1) < v:
                            eng.wait_ge(sems[s], v)
                            seen[s] = v
                    ins = fn(eng)
                    if inc is not None:
                        ins.then_inc(sems[inc[0]], inc[1])
            return body

        for eng_name, attr in (("te", "tensor"), ("ve", "vector"),
                               ("se", "scalar"), ("sy", "sync"),
                               ("gp", "gpsimd")):
            getattr(block, attr)(make(eng_name))


def prep_inputs(L0_attn_Wq, L0_attn_Wk, L0_attn_Wv, L0_attn_memory,
                L0_attn_memory_norm, L0_attn_memory_weight, L0_ffn_router_w,
                L0_ffn_e_bias, L0_ffn_gate_up_proj, L0_ffn_down_proj,
                L0_ffn_shared_gate_w, L0_ffn_shared_up_w, L0_ffn_shared_down_w):
    f = np.float32
    wq = _lhsT_blocks(np.asarray(L0_attn_Wq, f))
    wk = _lhsT_blocks(np.asarray(L0_attn_Wk, f))
    wv = _rhs_layout(np.asarray(L0_attn_Wv, f))
    mem = _lhsT_blocks(np.asarray(L0_attn_memory, f).T)
    memn = np.ascontiguousarray(
        np.asarray(L0_attn_memory_norm, f).reshape(4, 128, 1)
        .transpose(1, 0, 2).reshape(128, 4))
    mw = np.ascontiguousarray(
        np.broadcast_to(np.asarray(L0_attn_memory_weight, f).reshape(1, 1),
                        (128, 1)))
    ebias = np.ascontiguousarray(
        np.broadcast_to(np.asarray(L0_ffn_e_bias, f).reshape(1, 16), (128, 16)))
    rw = _rhs_layout(np.asarray(L0_ffn_router_w, f))
    gup = np.asarray(L0_ffn_gate_up_proj, f)
    dp = np.asarray(L0_ffn_down_proj, f)
    sh_gu = np.concatenate([np.asarray(L0_ffn_shared_gate_w, f),
                            np.asarray(L0_ffn_shared_up_w, f)], axis=0)
    gu = np.stack([_lhsT_blocks(gup[e]) for e in range(16)]
                  + [_lhsT_blocks(sh_gu)])
    dw = np.stack([_rhs_layout(dp[e]) for e in range(16)]
                  + [_rhs_layout(np.asarray(L0_ffn_shared_down_w, f))])
    idn16 = np.eye(128, dtype=ml_dtypes.bfloat16)
    idn32 = np.eye(128, dtype=np.float32)
    ll = np.triu(np.ones((128, 128), np.float32), 1).astype(ml_dtypes.bfloat16)
    ones1 = np.ones((1, 128), np.float32)
    tid = np.ascontiguousarray(
        (np.arange(8)[None, :] * 128 + np.arange(128)[:, None]).astype(np.int32))
    eoff = np.ascontiguousarray(np.broadcast_to(
        (np.arange(16) * CEXP).astype(np.float32)[None, :], (128, 16)))
    return {"wq": wq, "wk": wk, "wv": wv, "mem": mem, "memn": memn, "mw": mw,
            "ebias": ebias, "rw": rw, "gu": gu, "dw": dw,
            "idn16": idn16, "idn32": idn32, "ll": ll, "ones1": ones1,
            "tid": tid, "eoff": eoff}


def kernel(L0_attn_x, **w):
    common = prep_inputs(**w)
    x = np.ascontiguousarray(np.asarray(L0_attn_x, np.float32))
    nc = build_nc(attn_only=False)
    in_maps = [dict(common, x=np.ascontiguousarray(x[c])) for c in range(8)]
    res = run_bass_kernel_spmd(nc, in_maps, core_ids=list(range(8)))
    return np.stack([np.asarray(r["y"], np.float32) for r in res.results])


# version 15
# speedup vs baseline: 1.1901x; 1.0091x over previous
import numpy as np
import ml_dtypes
import concourse.bass as bass
import concourse.mybir as mybir
from concourse.bass_utils import run_bass_kernel_spmd
from concourse.bass_core_rust import IndirectOffsetOnAxis

F32 = mybir.dt.float32
BF16 = mybir.dt.bfloat16
AF = mybir.ActivationFunctionType
ALU = mybir.AluOpType
AX = mybir.AxisListType

S, D = 2048, 2048
NQ, NH = 4, 8
NE = 17
STILE = 512
NSt = S // STILE
CEXP = 384
I32 = mybir.dt.int32
EPS = 1e-5
NEG_BIG = -1e30
ENGINES = ("te", "ve", "se", "sy", "gp")
_LAST_PROG = [None]


class Ring:
    def __init__(self, slots):
        self.slots = [{"o": s, "free": None} for s in slots]
        self.i = 0

    def next(self):
        s = self.slots[self.i % len(self.slots)]
        self.i += 1
        return s


class Prog:
    def __init__(self):
        self.ops = {e: [] for e in ENGINES}
        self.ctr = {"te": 0, "ve": 0, "se": 0, "ph": 0}

    def emit(self, eng, fn, waits=(), inc=None, inc_n=1):
        flat = []
        for t in waits:
            if t is None:
                continue
            if isinstance(t, list):
                flat.extend(x for x in t if x is not None)
            else:
                flat.append(t)
        w = tuple(flat)
        tok = None
        if inc is not None:
            self.ctr.setdefault(inc, 0)
            self.ctr[inc] += inc_n
            tok = (inc, self.ctr[inc])
        self.ops[eng].append((fn, w, (inc, inc_n) if inc else None))
        return tok


def MM(out_ap, l_ap, r_ap, start, stop, tp=None):
    if tp is None:
        return lambda t: t.matmul(out_ap, l_ap, r_ap, start=start, stop=stop)
    return lambda t: t.matmul(out_ap, l_ap, r_ap, start=start, stop=stop,
                              tile_position=tp)


def TR(out_ap, in_ap, ident_ap):
    return lambda t: t.transpose(out_ap, in_ap, ident_ap)


def CP(out_ap, in_ap):
    return lambda v: v.tensor_copy(out_ap, in_ap)


def TT_(out_ap, a_ap, b_ap, op):
    return lambda v: v.tensor_tensor(out_ap, a_ap, b_ap, op)


def TS(out_ap, in_ap, s1, s2, op0, op1=None):
    if op1 is None:
        return lambda v: v.tensor_scalar(out_ap, in_ap, s1, None, op0)
    return lambda v: v.tensor_scalar(out_ap, in_ap, s1, s2, op0, op1)


def STT(out_ap, in0_ap, sc, in1_ap, op0, op1):
    return lambda v: v.scalar_tensor_tensor(out_ap, in0_ap, sc, in1_ap, op0, op1)


def ACT(out_ap, in_ap, func, bias=0.0, scale=1.0):
    return lambda s: s.activation(out_ap, in_ap, func, bias=bias, scale=scale)


def DMA(out_ap, in_ap):
    return lambda e: e.dma_start(out=out_ap, in_=in_ap)


def RED(out_ap, in_ap, op):
    return lambda v: v.tensor_reduce(out_ap, in_ap, AX.X, op)


def RECIP(out_ap, in_ap):
    return lambda v: v.reciprocal(out_ap, in_ap)


def MEMSET(ap, val):
    return lambda v: v.memset(ap, val)


def NOP():
    return lambda e: e.nop()


def VNOP():
    return lambda v: v.engine_nop()


def _lhsT_blocks(W):
    dout, din = W.shape
    Ki, nm = din // 128, dout // 128
    A = np.ascontiguousarray(W.T).reshape(Ki, 128, nm, 128)
    A = np.ascontiguousarray(A.transpose(2, 1, 0, 3))
    return A.reshape(nm, 128, Ki * 128)


def _rhs_layout(W):
    dout, din = W.shape
    Ki = din // 128
    B = np.ascontiguousarray(W.T).reshape(Ki, 128, dout).transpose(1, 0, 2)
    return np.ascontiguousarray(B).reshape(128, Ki * dout)


def build_nc(attn_only=False, dbg=None):
    nc = bass.Bass("TRN2", target_bir_lowering=False)
    P = Prog()
    _LAST_PROG[0] = P

    x_d = nc.dram_tensor("x", [S, D], F32, kind="ExternalInput")
    wq_d = nc.dram_tensor("wq", [16, 128, 2048], F32, kind="ExternalInput")
    wk_d = nc.dram_tensor("wk", [4, 128, 2048], F32, kind="ExternalInput")
    wv_d = nc.dram_tensor("wv", [128, 8192], F32, kind="ExternalInput")
    mem_d = nc.dram_tensor("mem", [4, 128, 512], F32, kind="ExternalInput")
    memn_d = nc.dram_tensor("memn", [128, 4], F32, kind="ExternalInput")
    mw_d = nc.dram_tensor("mw", [128, 1], F32, kind="ExternalInput")
    ebias_d = nc.dram_tensor("ebias", [128, 16], F32, kind="ExternalInput")
    rw_d = nc.dram_tensor("rw", [128, 256], F32, kind="ExternalInput")
    gu_d = nc.dram_tensor("gu", [NE, 12, 128, 2048], F32, kind="ExternalInput")
    dw_d = nc.dram_tensor("dw", [NE, 128, 12288], F32, kind="ExternalInput")
    idn16_d = nc.dram_tensor("idn16", [128, 128], BF16, kind="ExternalInput")
    idn32_d = nc.dram_tensor("idn32", [128, 128], F32, kind="ExternalInput")
    x1_d = nc.dram_tensor("x1buf", [S, D], F32, kind="Internal")
    xn2n_d = nc.dram_tensor("xn2nat", [S, D], BF16, kind="Internal")
    idxcw_d = nc.dram_tensor("idxcw", [16 * CEXP, 2], I32, kind="Internal")
    ll_d = nc.dram_tensor("ll", [128, 128], BF16, kind="ExternalInput")
    ones1_d = nc.dram_tensor("ones1", [1, 128], F32, kind="ExternalInput")
    tid_d = nc.dram_tensor("tid", [128, 8], I32, kind="ExternalInput")
    eoff_d = nc.dram_tensor("eoff", [128, 16], F32, kind="ExternalInput")
    y_d = nc.dram_tensor("y", [S, D], F32, kind="ExternalOutput")
    attn_out_d = y_d if attn_only else x1_d

    stp0 = nc.alloc_psum_tensor("stp0", [128, 1024], F32)
    stp1 = nc.alloc_psum_tensor("stp1", [128, 1024], F32)
    pvp0 = nc.alloc_psum_tensor("pvp0", [128, 512], F32)
    pvp1 = nc.alloc_psum_tensor("pvp1", [128, 512], F32)
    pool = Ring([stp0[:, 0:512], stp0[:, 512:1024],
                 stp1[:, 0:512], stp1[:, 512:1024],
                 pvp0[:, :], pvp1[:, :]])
    ptb = Ring([nc.alloc_psum_tensor(f"pt{i}", [128, 1024], BF16) for i in range(2)])

    ARENA = 212736
    arena = nc.alloc_sbuf_tensor("arena", [128, ARENA], mybir.dt.uint8)
    base = nc.lookup_mloc(arena).addr
    cursor = [0]

    def place(name, shape, dtype, align=128):
        nbytes = int(np.prod(shape[1:])) * mybir.dt.size(dtype)
        off = cursor[0]
        cursor[0] = (off + nbytes + align - 1) & ~(align - 1)
        return nc.alloc_sbuf_tensor_at(name, list(shape), dtype,
                                       offset=base + off)

    idn16 = place("idn16s", [128, 128], BF16)
    idn32 = place("idn32s", [128, 128], F32)
    sq = place("sq", [128, 2048], F32)
    tiny = place("tiny", [128, 16], F32)
    small = place("small", [128, 16], F32)
    wsig = place("wsig", [128, 2], F32)
    epsc = place("epsc", [128, 1], F32)
    region0 = cursor[0]

    wbuf = place("wbuf", [128, 16384], BF16)
    memB = place("memB", [128, 2048], BF16)
    memn4 = place("memn4", [128, 4], BF16)
    kT = place("kT", [128, 4 * 2048], BF16)
    v_ext = place("v_ext", [128, 16 * 520 + 128], BF16)
    xn1T = place("xn1T", [128, 16 * 512], BF16)
    qT = place("qT", [128, 16 * 512], BF16)
    sgT = place("sgT", [128, 16 * 512], BF16)
    acc = place("acc", [128, 4 * 2048], F32)
    xn1c = place("xn1c", [128, 2048], BF16)
    oT_sb = place("oT_sb", [128, 1024], F32)
    amT_sb = place("amT_sb", [128, 4 * 512], F32)
    amden = place("amden", [1, 512], F32)
    lnm = place("lnm", [128, 16], F32)
    lnr = place("lnr", [128, 16], F32)
    ptc = Ring([place(f"ptc{i}", [128, 1024], BF16) for i in range(3)])
    xring = Ring([place(f"xr{i}", [128, 2048], F32) for i in range(2)])
    attn_end = cursor[0]
    assert attn_end <= ARENA, f"attn SBUF over: {attn_end}"

    cursor[0] = region0
    y_acc = place("y_acc", [128, 8 * 2048], F32)
    xn2T_off = cursor[0]
    xn2T = place("xn2T", [128, 16 * 1024], BF16)
    xe_nat = nc.alloc_sbuf_tensor_at("xe_nat", [128, 3 * 2048], BF16,
                                     offset=base + xn2T_off)
    xeT = nc.alloc_sbuf_tensor_at("xeT", [128, 16 * CEXP], BF16,
                                  offset=base + xn2T_off + 3 * 2048 * 2)
    actB = place("actB", [128, 6 * 1024], BF16)
    y_sp = place("y_sp", [128, 2 * 2048], F32)
    m_bf = place("m_bf", [128, 128], BF16)
    Pn = place("Pn", [128, 128], F32)
    trow0 = place("trow0", [1, 128], F32)
    trow1 = place("trow1", [1, 128], F32)
    onesc = place("onesc", [128, 1], BF16)
    posf = place("posf", [128, 4], F32)
    cwf = place("cwf", [128, 4], F32)
    posi = place("posi", [128, 4], I32)
    tk2 = place("tk2", [128, 8], I32)
    tksb = [place(f"tksb{i}", [128, 6], I32) for i in range(2)]
    ixc = [[place(f"ixc{i}_{j}", [128, 1], I32, align=32) for j in range(3)]
           for i in range(2)]
    Ll = place("Lls", [128, 128], BF16)
    ones1 = place("ones1s", [1, 128], F32)
    tid = place("tids", [128, 8], I32)
    eoffC = place("eoffs", [128, 16], F32)
    wdr = Ring([place(f"wd{i}", [128, 6 * 2048], BF16) for i in range(2)])
    gur = Ring([place(f"gur{i}", [128, 2048], BF16) for i in range(3)])
    xn2c = place("xn2c", [128, 2048], BF16)
    sgbuf = place("sgbuf", [128, 1024], F32)
    scores = place("scores", [128, 8 * 16], F32)
    combine = place("combine", [128, 8 * 16], F32)
    ft = place("ft", [128, 96], F32)
    ebias_s = place("ebias_s", [128, 16], F32)
    rwT = place("rwT", [128, 256], BF16)
    m2 = place("m2", [128, 8], F32)
    r2 = place("r2", [128, 8], F32)
    ffn_end = cursor[0]
    assert ffn_end <= ARENA, f"ffn SBUF over: {ffn_end}"

    elu1, elu2, elu3 = sq[:, 0:512], sq[:, 512:1024], sq[:, 1024:1536]

    ln_state = {"prev_se": None, "prev_rd": None}

    def ln_stats(xc_ap, mcol_ap, rcol_ap, deps):
        t_sq = P.emit("ve", TT_(sq[:, :], xc_ap, xc_ap, ALU.mult),
                      waits=list(deps) + [ln_state["prev_se"],
                                          ln_state["prev_rd"]], inc="ve")
        t0 = P.emit("ve", RED(tiny[:, 0:1], xc_ap, ALU.add), inc="ve")
        t1 = P.emit("ve", RED(tiny[:, 1:2], sq[:, :], ALU.add), waits=[t_sq],
                    inc="ve")
        ln_state["prev_rd"] = t1
        t2 = P.emit("ve", TS(tiny[:, 2:3], tiny[:, 0:1], 1.0 / D, None,
                             ALU.mult), waits=[t0], inc="ve")
        P.emit("ve", CP(mcol_ap, tiny[:, 2:3]), waits=[t2])
        t3 = P.emit("ve", TS(tiny[:, 3:4], tiny[:, 1:2], 1.0 / D, None,
                             ALU.mult), waits=[t1], inc="ve")
        t4 = P.emit("ve", TT_(tiny[:, 4:5], tiny[:, 2:3], tiny[:, 2:3],
                              ALU.mult), waits=[t2], inc="ve")
        tv = P.emit("ve", TT_(tiny[:, 5:6], tiny[:, 3:4], tiny[:, 4:5],
                              ALU.subtract), waits=[t3, t4], inc="ve")
        tl = P.emit("se", ACT(tiny[:, 6:7], tiny[:, 5:6], AF.Ln,
                              bias=epsc[:, 0:1]), waits=[tv], inc="se")
        ts = P.emit("se", ACT(rcol_ap, tiny[:, 6:7], AF.Exp, scale=-0.5),
                    waits=[tl], inc="se")
        ln_state["prev_se"] = ts
        return ts

    def transpose_to_T(src_chunk, dstT, dst_col, tok_w, deps):
        last = None
        for half in range(2):
            b = ptb.next()
            tok = None
            for j in range(8):
                k = half * 8 + j
                w = (list(deps) + [b["free"]]) if j == 0 else []
                tok = P.emit("te", TR(b["o"][:, j * 128:(j + 1) * 128],
                                      src_chunk[:, k * 128:(k + 1) * 128],
                                      idn16[:, :]),
                             waits=w, inc="te" if j == 7 else None)
            src = b["o"][:, 0:1024].rearrange("p (k t) -> p k t", t=128)
            dst = dstT[:, :].rearrange("p (k t) -> p k t", t=tok_w)
            last = P.emit("ve", CP(dst[:, half * 8:half * 8 + 8,
                                       dst_col:dst_col + 128], src),
                          waits=[tok], inc="ve")
            b["free"] = last
        return last

    def barrier():
        base = P.ctr["ph"]
        qtot = [(k, v) for k, v in P.ctr.items() if k.startswith("q")]
        P.emit("sy", NOP(), waits=qtot, inc="ph")
        P.emit("gp", NOP(), waits=qtot, inc="ph")
        for eng in ("te", "ve", "se"):
            P.emit(eng, NOP(), inc="ph")
        for eng in ENGINES:
            P.emit(eng, NOP(), waits=[("ph", base + 5)])

    P.emit("sy", DMA(idn16[:, :], idn16_d.ap()), inc="qcs", inc_n=16)
    P.emit("sy", DMA(idn32[:, :], idn32_d.ap()), inc="qcs", inc_n=16)
    P.emit("sy", DMA(small[:, 0:1], mw_d.ap()), inc="qcs", inc_n=16)
    for m in range(4):
        P.emit("gp", DMA(wbuf[:, m * 2048:(m + 1) * 2048], wk_d.ap()[m]),
               inc="qcg", inc_n=16)
    P.emit("gp", DMA(wbuf[:, 8192:16384], wv_d.ap()), inc="qcg", inc_n=16)
    for m in range(4):
        P.emit("gp", DMA(memB[:, m * 512:(m + 1) * 512], mem_d.ap()[m]),
               inc="qcg", inc_n=16)
    P.emit("gp", DMA(memn4[:, :], memn_d.ap()), inc="qcg", inc_n=16)
    consts_done = [("qcs", P.ctr["qcs"]), ("qcg", P.ctr["qcg"])]
    t_idn16 = t_idn32 = t_mw = t_wk = t_wv = t_mem = t_memn = consts_done

    P.emit("ve", MEMSET(epsc[:, 0:1], EPS))
    t1 = P.emit("se", ACT(small[:, 1:2], small[:, 0:1], AF.Exp, scale=-1.0),
                waits=[t_mw], inc="se")
    t2 = P.emit("ve", TS(small[:, 2:3], small[:, 1:2], 1.0, None, ALU.add),
                waits=[t1], inc="ve")
    t3 = P.emit("ve", RECIP(wsig[:, 0:1], small[:, 2:3]), waits=[t2], inc="ve")
    P.emit("ve", TS(wsig[:, 1:2], wsig[:, 0:1], -1.0, 1.0, ALU.mult, ALU.add),
           waits=[t3], inc="ve")

    P.emit("ve", MEMSET(v_ext[:, 16 * 520:16 * 520 + 128], 0.0))
    ve_view = v_ext[:, 0:16 * 520].rearrange("p (t c) -> p t c", c=520)
    for h in range(NH):
        P.emit("ve", MEMSET(ve_view[:, :, h * 65 + 64:h * 65 + 65], 1.0))
    t_ones = P.emit("ve", VNOP(), inc="ve")

    def dbg_finish(pairs):
        for r0, nr, src in pairs:
            c = nr // 128
            dst = y_d.ap()[r0:r0 + nr, :].rearrange("(c p) f -> p c f", c=c)
            sview = src.rearrange("p (c f) -> p c f", c=c)
            P.emit("gp", DMA(dst, sview), waits=[("te", P.ctr["te"]),
                                                 ("ve", P.ctr["ve"]),
                                                 ("se", P.ctr["se"])],
                   inc="qdbg", inc_n=16)
        P.emit("gp", NOP(), waits=[("qdbg", P.ctr.get("qdbg", 0))])
        _realize(nc, P)
        return nc

    if dbg == "const":
        return dbg_finish([])

    for tile in range(4):
        drain_toks = []
        for lc in range(4):
            ci = tile * 4 + lc
            xs = xring.next()
            xsem = f"qx{ci % 2}"
            t_x = P.emit("sy", DMA(xs["o"][:, :],
                                   x_d.ap()[ci * 128:(ci + 1) * 128, :]),
                         waits=[xs["free"]], inc=xsem, inc_n=16)
            ts = ln_stats(xs["o"][:, :], lnm[:, ci:ci + 1], lnr[:, ci:ci + 1],
                          [t_x])
            t_n = P.emit("ve", TS(xn1c[:, :], xs["o"][:, :], lnm[:, ci:ci + 1],
                                  lnr[:, ci:ci + 1], ALU.subtract, ALU.mult),
                         waits=[ts], inc="ve")
            xs["free"] = t_n
            drain_toks.append(
                transpose_to_T(xn1c[:, :], xn1T, lc * 128, 512, [t_n]))
        for m in range(4):
            b = pool.next()
            tok = None
            for ki in range(16):
                w = ([t_wk, b["free"]] + drain_toks) if ki == 0 else []
                tok = P.emit("te", MM(
                    b["o"][:, :],
                    wbuf[:, m * 2048 + ki * 128:m * 2048 + ki * 128 + 128],
                    xn1T[:, ki * 512:(ki + 1) * 512], ki == 0, ki == 15),
                    waits=w, inc="te" if ki == 15 else None)
            b["free"] = P.emit("ve", CP(
                kT[:, m * 2048 + tile * 512:m * 2048 + tile * 512 + 512],
                b["o"][:, :]), waits=[tok], inc="ve")
        for mt in range(4):
            b = pool.next()
            tc = tile * 4 + mt
            tok = None
            for ki in range(16):
                w = ([t_wv, b["free"]] + drain_toks) if ki == 0 else []
                tok = P.emit("te", MM(
                    b["o"][:, :],
                    xn1T[:, ki * 512 + mt * 128:ki * 512 + mt * 128 + 128],
                    wbuf[:, 8192 + ki * 512:8192 + (ki + 1) * 512],
                    ki == 0, ki == 15),
                    waits=w, inc="te" if ki == 15 else None)
            lastd = None
            for h in range(NH):
                lastd = P.emit("ve", CP(
                    v_ext[:, tc * 520 + h * 65:tc * 520 + h * 65 + 64],
                    b["o"][:, h * 64:(h + 1) * 64]),
                    waits=[tok, t_ones] if h == 0 else [],
                    inc="ve" if h == NH - 1 else None)
            b["free"] = lastd

    barrier()

    if dbg == "pass1":
        for src, r0, c0, w_ in ((lnm, 1024, 0, 16), (lnr, 1024, 16, 16)):
            P.emit("gp", DMA(y_d.ap()[r0:r0 + 128, c0:c0 + w_], src[:, 0:w_]),
                   waits=[("te", P.ctr["te"]), ("ve", P.ctr["ve"]),
                          ("se", P.ctr["se"])], inc="qdbg", inc_n=16)
        P.emit("gp", DMA(y_d.ap()[1152:1280, :], xn1c[:, :]),
               inc="qdbg", inc_n=16)
        return dbg_finish([(0, 512, kT[:, :]), (512, 512, xn1T[:, :])])

    wq_ring = Ring([i * 2048 for i in range(4)])
    prev_te_end = None
    acc_war = None
    for st in range(NSt):
        acc_ready = []
        for lc in range(4):
            ci = st * 4 + lc
            acc_ready.append(P.emit("sy", DMA(
                acc[:, lc * 2048:(lc + 1) * 2048],
                x_d.ap()[ci * 128:(ci + 1) * 128, :]),
                waits=[acc_war], inc=f"qa{lc}", inc_n=16))
        xn1_toks = []
        for lc in range(4):
            ci = st * 4 + lc
            t_n = P.emit("ve", TS(xn1c[:, :], acc[:, lc * 2048:(lc + 1) * 2048],
                                  lnm[:, ci:ci + 1], lnr[:, ci:ci + 1],
                                  ALU.subtract, ALU.mult),
                         waits=[acc_ready[lc], prev_te_end], inc="ve")
            xn1_toks.append(transpose_to_T(xn1c[:, :], xn1T, lc * 128, 512, [t_n]))
        prev_elu_se = None
        prev_elu_ve = None
        for m in range(16):
            slot = wq_ring.next()
            col = slot["o"]
            t_w = P.emit("gp", DMA(wbuf[:, col:col + 2048], wq_d.ap()[m]),
                         waits=[slot["free"]], inc=f"qw{m % 4}", inc_n=16)
            b = pool.next()
            tok = None
            for ki in range(16):
                w = ([t_w, b["free"]] + xn1_toks) if ki == 0 else []
                tok = P.emit("te", MM(
                    b["o"][:, :],
                    wbuf[:, col + ki * 128:col + ki * 128 + 128],
                    xn1T[:, ki * 512:(ki + 1) * 512], ki == 0, ki == 15),
                    waits=w, inc="te" if ki == 15 else None)
            slot["free"] = tok
            P.emit("ve", CP(qT[:, m * 512:(m + 1) * 512], b["o"][:, :]),
                   waits=[tok])
            t_mn = P.emit("ve", TS(elu1, b["o"][:, :], 0.0, None, ALU.min),
                          waits=[prev_elu_se], inc="ve")
            t_e = P.emit("se", ACT(elu2, elu1, AF.Exp),
                         waits=[t_mn, prev_elu_ve], inc="se")
            prev_elu_se = t_e
            t_mx = P.emit("ve", TS(elu3, b["o"][:, :], 0.0, None, ALU.max),
                          waits=[prev_elu_ve], inc="ve")
            t_s = P.emit("ve", STT(sgT[:, m * 512:(m + 1) * 512], elu2, -1.0,
                                   elu3, ALU.add, ALU.add),
                         waits=[t_e, t_mx], inc="ve")
            prev_elu_ve = t_s
            b["free"] = t_s
        q_done = P.emit("ve", VNOP(), inc="ve")

        stpt = [{"o": stp0, "free": [pool.slots[0]["free"], pool.slots[1]["free"]]},
                {"o": stp1, "free": [pool.slots[2]["free"], pool.slots[3]["free"]]}]
        pvt = [{"o": pvp0, "free": [pool.slots[4]["free"]]},
               {"o": pvp1, "free": [pool.slots[5]["free"]]}]
        first_heads = [q_done]
        for qg in range(NQ):
            for hp in range(4):
                h0, h1 = 2 * hp, 2 * hp + 1
                mcol = (qg * 4 + hp) * 512
                pend = None

                def emit_pv(pd, h0=h0, h1=h1):
                    j, pc, tex = pd
                    t0_ = P.emit("te", MM(
                        pvt[0]["o"][:, :],
                        v_ext[:, j * 520 + h0 * 65:j * 520 + h0 * 65 + 128],
                        pc["o"][:, 0:512], j == 0, j == 15),
                        waits=[tex] + (pvt[0]["free"] if j == 0 else []),
                        inc="te" if j == 15 else None)
                    t1_ = P.emit("te", MM(
                        pvt[1]["o"][:, :],
                        v_ext[:, j * 520 + h1 * 65:j * 520 + h1 * 65 + 128],
                        pc["o"][:, 512:1024], j == 0, j == 15),
                        waits=(pvt[1]["free"] if j == 0 else []),
                        inc="te" if j == 15 else None)
                    pc["free"] = t1_ if j == 15 else t0_
                    return t0_, t1_

                for i in range(16):
                    sp = stpt[i % 2]
                    w = [sp["free"]]
                    if qg == 0 and hp == 0 and i < 2:
                        w = w + first_heads
                    P.emit("te", MM(
                        sp["o"][:, 0:512],
                        kT[0:64, hp * 2048 + i * 128:hp * 2048 + i * 128 + 128],
                        qT[0:64, mcol:mcol + 512], True, True, tp=(0, 0)),
                        waits=w)
                    tqk1 = P.emit("te", MM(
                        sp["o"][:, 512:1024],
                        kT[64:128, hp * 2048 + i * 128:hp * 2048 + i * 128 + 128],
                        qT[64:128, mcol:mcol + 512], True, True, tp=(64, 0)),
                        inc="te")
                    pc = ptc.next()
                    tex = P.emit("se", ACT(pc["o"][:, 0:1024], sp["o"][:, 0:1024],
                                           AF.Exp, scale=0.125),
                                 waits=[tqk1, pc["free"]], inc="se")
                    sp["free"] = [tex]
                    if pend is not None:
                        emit_pv(pend)
                    pend = (i, pc, tex)
                tpv0, tpv1 = emit_pv(pend)

                td0 = P.emit("ve", CP(oT_sb[0:65, 0:512], pvt[0]["o"][0:65, :]),
                             waits=[tpv0], inc="ve")
                td1 = P.emit("ve", CP(oT_sb[0:65, 512:1024], pvt[1]["o"][0:65, :]),
                             waits=[tpv1], inc="ve")
                ta0 = ta1 = None
                for sc in range(4):
                    tt0 = P.emit("te", TR(pvt[0]["o"][:, 0:65],
                                          oT_sb[0:65, sc * 128:sc * 128 + 128],
                                          idn32[0:65, 0:65]),
                                 waits=[td0, t_idn32] + ([ta0] if ta0 else []),
                                 inc="te")
                    tt1 = P.emit("te", TR(
                        pvt[1]["o"][:, 0:65],
                        oT_sb[0:65, 512 + sc * 128:512 + sc * 128 + 128],
                        idn32[0:65, 0:65]),
                        waits=[td1] + ([ta1] if ta1 else []), inc="te")
                    tr = P.emit("ve", RECIP(small[:, 4:5], pvt[0]["o"][:, 64:65]),
                                waits=[tt0], inc="ve")
                    tw_ = P.emit("ve", TT_(small[:, 5:6], small[:, 4:5],
                                           wsig[:, 1:2], ALU.mult),
                                 waits=[tr], inc="ve")
                    c0 = sc * 2048 + qg * 512 + h0 * 64
                    ta0 = P.emit("ve", STT(acc[:, c0:c0 + 64],
                                           pvt[0]["o"][:, 0:64],
                                           small[:, 5:6], acc[:, c0:c0 + 64],
                                           ALU.mult, ALU.add),
                                 waits=[tw_], inc="ve")
                    tr1 = P.emit("ve", RECIP(small[:, 6:7], pvt[1]["o"][:, 64:65]),
                                 waits=[tt1], inc="ve")
                    tw1 = P.emit("ve", TT_(small[:, 7:8], small[:, 6:7],
                                           wsig[:, 1:2], ALU.mult),
                                 waits=[tr1], inc="ve")
                    c1 = sc * 2048 + qg * 512 + h1 * 64
                    ta1 = P.emit("ve", STT(acc[:, c1:c1 + 64],
                                           pvt[1]["o"][:, 0:64],
                                           small[:, 7:8], acc[:, c1:c1 + 64],
                                           ALU.mult, ALU.add),
                                 waits=[tw1], inc="ve")
                pvt[0]["free"] = [ta0]
                pvt[1]["free"] = [ta1]

        pool.slots[0]["free"] = stpt[0]["free"][0]
        pool.slots[1]["free"] = stpt[0]["free"][0]
        pool.slots[2]["free"] = stpt[1]["free"][0]
        pool.slots[3]["free"] = stpt[1]["free"][0]
        pool.slots[4]["free"] = pvt[0]["free"][0]
        pool.slots[5]["free"] = pvt[1]["free"][0]
        pool.i = 0

        for qg in range(NQ):
            bD = pool.next()
            tokD = None
            for ii in range(4):
                w = [q_done, t_memn, bD["free"]] if ii == 0 else []
                tokD = P.emit("te", MM(
                    bD["o"][0:1, :], memn4[:, ii:ii + 1],
                    sgT[:, (qg * 4 + ii) * 512:(qg * 4 + ii + 1) * 512],
                    ii == 0, ii == 3),
                    waits=w, inc="te" if ii == 3 else None)
            jtoks = []
            for jj in range(4):
                b = pool.next()
                tok = None
                for ii in range(4):
                    w = [t_mem, b["free"]] if ii == 0 else []
                    tok = P.emit("te", MM(
                        b["o"][:, :],
                        memB[:, jj * 512 + ii * 128:jj * 512 + ii * 128 + 128],
                        sgT[:, (qg * 4 + ii) * 512:(qg * 4 + ii + 1) * 512],
                        ii == 0, ii == 3),
                        waits=w, inc="te" if ii == 3 else None)
                t_am = P.emit("ve", CP(amT_sb[:, jj * 512:(jj + 1) * 512],
                                       b["o"][:, :]), waits=[tok], inc="ve")
                b["free"] = t_am
                jtoks.append(t_am)
            t_dn = P.emit("ve", CP(amden[0:1, :], bD["o"][0:1, :]),
                          waits=[tokD], inc="ve")
            bD["free"] = t_dn
            bDn = pool.next()
            prev_dr = None
            for sc in range(4):
                w = [t_dn] + ([bDn["free"]] if sc == 0 else [prev_dr])
                tsp = P.emit("te", TR(bDn["o"][:, 0:1],
                                      amden[0:1, sc * 128:(sc + 1) * 128],
                                      idn32[0:1, 0:1]),
                             waits=w, inc="te")
                prev_dr = P.emit("ve", CP(tiny[:, 10 + sc:11 + sc],
                                          bDn["o"][:, 0:1]),
                                 waits=[tsp], inc="ve")
            tdn = prev_dr
            last_sc = None
            for sc in range(4):
                trc = P.emit("ve", RECIP(tiny[:, 8:9], tiny[:, 10 + sc:11 + sc]),
                             waits=[tdn], inc="ve")
                tww = P.emit("ve", TT_(tiny[:, 9:10], tiny[:, 8:9], wsig[:, 0:1],
                                       ALU.mult), waits=[trc], inc="ve")
                for jj in range(4):
                    bA2 = pool.next()
                    ttj = P.emit("te", TR(
                        bA2["o"][:, 0:128],
                        amT_sb[:, jj * 512 + sc * 128:jj * 512 + sc * 128 + 128],
                        idn32[:, :]),
                        waits=[jtoks[jj], bA2["free"]], inc="te")
                    cc = sc * 2048 + qg * 512 + jj * 128
                    last_sc = P.emit("ve", STT(
                        acc[:, cc:cc + 128], bA2["o"][:, 0:128], tiny[:, 9:10],
                        acc[:, cc:cc + 128], ALU.mult, ALU.add),
                        waits=[ttj, tww], inc="ve")
                    bA2["free"] = last_sc
            bDn["free"] = tdn

        t_all = P.emit("ve", VNOP(), inc="ve")
        for lc in range(4):
            ci = st * 4 + lc
            P.emit("sy", DMA(attn_out_d.ap()[ci * 128:(ci + 1) * 128, :],
                             acc[:, lc * 2048:(lc + 1) * 2048]),
                   waits=[t_all], inc="qs", inc_n=16)
        acc_war = ("qs", P.ctr["qs"])
        prev_te_end = P.emit("te", NOP(), inc="te")

    attn_store_ctr = P.ctr["qs"]

    if attn_only:
        P.emit("sy", NOP(), waits=[("qs", P.ctr["qs"])])
        _realize(nc, P)
        return nc

    barrier()

    P.emit("sy", DMA(ebias_s[:, :], ebias_d.ap()), inc="qc2s", inc_n=16)
    P.emit("gp", DMA(rwT[:, :], rw_d.ap()), inc="qc2g", inc_n=16)
    P.emit("sy", DMA(Ll[:, :], ll_d.ap()), inc="qc2s", inc_n=16)
    P.emit("sy", DMA(ones1[0:1, :], ones1_d.ap()), inc="qc2s", inc_n=16)
    P.emit("sy", DMA(tid[:, :], tid_d.ap()), inc="qc2s", inc_n=16)
    P.emit("sy", DMA(eoffC[:, :], eoff_d.ap()), inc="qc2s", inc_n=16)
    t_eb = t_rw = [("qc2s", P.ctr["qc2s"]), ("qc2g", P.ctr["qc2g"])]
    t_onc = P.emit("ve", MEMSET(onesc[:, :], 1.0), inc="ve")
    oob_view = y_sp[:, 0:96].bitcast(I32)

    prev_y_store = ("qs", attn_store_ctr)
    prev_te_end2 = None
    prev_idx_load = None
    prev_act_te = None
    prev_sg = [None, None]
    tk_ring = Ring([0, 1, 2, 3])
    tkb_war = [None, None]
    ysp_ring = Ring([0, 1])

    dense_state = {"last_gu": None}

    def dense_branch(e, xn2_toks):
        nonlocal prev_act_te
        wi = wdr.i % 2
        wd_slot = wdr.next()
        t_wd = P.emit("gp", DMA(wd_slot["o"][:, :], dw_d.ap()[e]),
                      waits=[wd_slot["free"]], inc=f"qd{wi}", inc_n=16)
        act_toks = []
        for i in range(6):
            pair = []
            for m in (i, i + 6):
                gi = gur.i % 3
                slot = gur.next()
                t_w = P.emit("gp", DMA(slot["o"][:, :], gu_d.ap()[e, m]),
                             waits=[slot["free"]], inc=f"qgu{gi}", inc_n=16)
                bh = [pool.next(), pool.next()]
                tok = None
                for ki in range(16):
                    for half in range(2):
                        w = []
                        if ki == 0:
                            w = [t_w, bh[half]["free"]]
                            w += xn2_toks[half * 4:half * 4 + 4]
                        tok = P.emit("te", MM(
                            bh[half]["o"][:, :],
                            slot["o"][:, ki * 128:(ki + 1) * 128],
                            xn2T[:, ki * 1024 + half * 512:ki * 1024 + half * 512 + 512],
                            ki == 0, ki == 15),
                            waits=w,
                            inc="te" if (ki == 15 and half == 1) else None)
                slot["free"] = tok
                pair.append((bh, tok))
            dense_state["last_gu"] = pair[1][1]
            (bg, tokg), (bu, toku) = pair
            for half in range(2):
                t_sg = P.emit("se", ACT(sgbuf[:, half * 512:(half + 1) * 512],
                                        bg[half]["o"][:, :], AF.Silu),
                              waits=[tokg, prev_sg[half]], inc="se")
                bg[half]["free"] = t_sg
                t_act = P.emit("ve", TT_(
                    actB[:, i * 1024 + half * 512:i * 1024 + half * 512 + 512],
                    sgbuf[:, half * 512:(half + 1) * 512],
                    bu[half]["o"][:, :], ALU.mult),
                    waits=[t_sg, toku, prev_act_te], inc="ve")
                bu[half]["free"] = t_act
                prev_sg[half] = t_act
                act_toks.append(t_act)
        tok_dn = None
        for tc in range(8):
            for ncol in range(4):
                b = pool.next()
                for ki in range(6):
                    w = ([t_wd, b["free"]] + act_toks) if ki == 0 else []
                    tok_dn = P.emit("te", MM(
                        b["o"][:, :],
                        actB[:, ki * 1024 + tc * 128:ki * 1024 + tc * 128 + 128],
                        wd_slot["o"][:, ki * 2048 + ncol * 512:ki * 2048 + ncol * 512 + 512],
                        ki == 0, ki == 5),
                        waits=w, inc="te" if ki == 5 else None)
                dst = y_acc[:, tc * 2048 + ncol * 512:tc * 2048 + ncol * 512 + 512]
                t_dr = P.emit("ve", STT(dst, b["o"][:, :], 1.0, dst,
                                        ALU.mult, ALU.add),
                              waits=[tok_dn], inc="ve")
                b["free"] = t_dr
        wd_slot["free"] = tok_dn
        prev_act_te = tok_dn

    prev_ys = []
    for p in range(2):
        t_ms = P.emit("ve", MEMSET(oob_view, 1 << 20),
                      waits=prev_ys, inc="ve")
        t_fill = P.emit("sy", DMA(
            idxcw_d.ap().rearrange("(p f) j -> p (f j)", p=128),
            oob_view), waits=[prev_idx_load, t_ms], inc="qif", inc_n=16)
        fill_tok = ("qif", P.ctr["qif"])
        for _sl in ysp_ring.slots:
            _sl["free"] = fill_tok

        xn2_toks = []
        for c in range(8):
            ci = p * 8 + c
            t_x = P.emit("sy" if c % 2 == 0 else "se",
                         DMA(y_acc[:, c * 2048:(c + 1) * 2048],
                             x1_d.ap()[ci * 128:(ci + 1) * 128, :]),
                         waits=[prev_y_store] if c < 2 else [],
                         inc=f"qy{c}", inc_n=16)
            ts2 = ln_stats(y_acc[:, c * 2048:(c + 1) * 2048],
                           m2[:, c:c + 1], r2[:, c:c + 1], [t_x, prev_te_end2])
            t_n = P.emit("ve", TS(xn2c[:, :], y_acc[:, c * 2048:(c + 1) * 2048],
                                  m2[:, c:c + 1], r2[:, c:c + 1],
                                  ALU.subtract, ALU.mult),
                         waits=[ts2], inc="ve")
            P.emit("sy", DMA(
                xn2n_d.ap()[p * 1024 + c * 128:p * 1024 + (c + 1) * 128, :],
                xn2c[:, :]), waits=[t_n], inc="qxn", inc_n=16)
            xn2_toks.append(transpose_to_T(xn2c[:, :], xn2T, c * 128, 1024,
                                           [t_n]))
        xn_ready = ("qxn", P.ctr["qxn"])

        rt_toks = []
        for c in range(8):
            b = pool.next()
            tok = None
            for ki in range(16):
                w = ([t_rw, b["free"], xn2_toks[c]]) if ki == 0 else []
                tok = P.emit("te", MM(
                    b["o"][:, 0:16],
                    xn2T[:, ki * 1024 + c * 128:ki * 1024 + c * 128 + 128],
                    rwT[:, ki * 16:(ki + 1) * 16], ki == 0, ki == 15),
                    waits=w, inc="te" if ki == 15 else None)
            rt_toks.append((b, tok))

        topk_toks = []
        prev_ft_read = None
        for c in range(8):
            b, tok = rt_toks[c]
            t_ex = P.emit("se", ACT(ft[:, 0:16], b["o"][:, 0:16], AF.Exp,
                                    scale=-1.0), waits=[tok, prev_ft_read],
                          inc="se")
            b["free"] = t_ex
            u = P.emit("ve", TS(ft[:, 16:32], ft[:, 0:16], 1.0, None, ALU.add),
                       waits=[t_ex], inc="ve")
            prev_ft_read = u
            u = P.emit("ve", RECIP(scores[:, c * 16:(c + 1) * 16],
                                   ft[:, 16:32]), waits=[u], inc="ve")
            u = P.emit("ve", TT_(ft[:, 16:32], scores[:, c * 16:(c + 1) * 16],
                                 ebias_s[:, :], ALU.add), waits=[t_eb, u],
                       inc="ve")
            u2 = P.emit("ve", MEMSET(combine[:, c * 16:(c + 1) * 16], 0.0),
                        inc="ve")
            for k in range(4):
                u = P.emit("ve", RED(ft[:, 32:33], ft[:, 16:32], ALU.max),
                           waits=[u], inc="ve")
                u = P.emit("ve", TS(ft[:, 48:64], ft[:, 16:32], ft[:, 32:33],
                                    None, ALU.is_equal), waits=[u], inc="ve")
                u = P.emit("ve", TT_(ft[:, 64:80], ft[:, 48:64],
                                     scores[:, c * 16:(c + 1) * 16], ALU.mult),
                           waits=[u], inc="ve")
                u = P.emit("ve", TT_(combine[:, c * 16:(c + 1) * 16],
                                     combine[:, c * 16:(c + 1) * 16],
                                     ft[:, 64:80], ALU.add),
                           waits=[u, u2], inc="ve")
                u = P.emit("ve", STT(ft[:, 16:32], ft[:, 48:64], NEG_BIG,
                                     ft[:, 16:32], ALU.mult, ALU.add),
                           waits=[u], inc="ve")
            u = P.emit("ve", RED(ft[:, 33:34], combine[:, c * 16:(c + 1) * 16],
                                 ALU.add), waits=[u], inc="ve")
            u = P.emit("ve", TS(ft[:, 34:35], ft[:, 33:34], 1e-20, None,
                                ALU.add), waits=[u], inc="ve")
            u = P.emit("ve", RECIP(ft[:, 35:36], ft[:, 34:35]), waits=[u],
                       inc="ve")
            tk = P.emit("ve", TS(combine[:, c * 16:(c + 1) * 16],
                                 combine[:, c * 16:(c + 1) * 16],
                                 ft[:, 35:36], None, ALU.mult), waits=[u],
                        inc="ve")
            topk_toks.append(tk)

        dense_branch(16, xn2_toks)
        t_shared_done = P.emit("ve", VNOP(), inc="ve")

        for c in range(8):
            ci = p * 8 + c
            P.emit("sy", DMA(y_d.ap()[ci * 128:(ci + 1) * 128, :],
                             y_acc[:, c * 2048:(c + 1) * 2048]),
                   waits=[t_shared_done], inc="qs", inc_n=16)
        y_base_tok = ("qs", P.ctr["qs"])
        prev_y_store = y_base_tok
        t_m = P.emit("ve", TS(m_bf[:, :], combine[:, :], 0.0, None, ALU.is_gt),
                     waits=topk_toks + [t_onc], inc="ve")
        bS = pool.next()
        tS = P.emit("te", MM(bS["o"][0:1, 0:128], onesc[:, 0:1], m_bf[:, :],
                             True, True), waits=[t_m, bS["free"]], inc="te")
        t_r1 = P.emit("ve", CP(trow0[0:1, :], bS["o"][0:1, 0:128]),
                      waits=[tS], inc="ve")
        bS["free"] = t_r1
        u = P.emit("ve", MEMSET(trow1[0:1, :], 0.0), waits=[t_r1], inc="ve")
        u = P.emit("ve", CP(trow1[0:1, 16:128], trow0[0:1, 0:112]),
                   waits=[u], inc="ve")
        bufs = [trow1, trow0]
        for step, sh in enumerate((16, 32, 64)):
            src = bufs[step % 2]
            dst = bufs[(step + 1) % 2]
            u = P.emit("ve", CP(dst[0:1, 0:sh], src[0:1, 0:sh]), waits=[u],
                       inc="ve")
            u = P.emit("ve", TT_(dst[0:1, sh:128], src[0:1, sh:128],
                                 src[0:1, 0:128 - sh], ALU.add), waits=[u],
                       inc="ve")
        pref_buf = trow0
        t_pref = u
        bP = pool.next()
        P.emit("te", MM(bP["o"][:, 0:128], Ll[:, :], m_bf[:, :], True, False),
               waits=[t_m, bP["free"], t_eb], inc=None)
        tP = P.emit("te", MM(bP["o"][:, 0:128], ones1[0:1, :],
                             pref_buf[0:1, :], False, True),
                    waits=[t_pref], inc="te")
        t_Pn = P.emit("ve", CP(Pn[:, :], bP["o"][:, 0:128]), waits=[tP],
                      inc="ve")
        bP["free"] = t_Pn

        for c in range(8):
            cs = slice(c * 16, (c + 1) * 16)
            u = P.emit("ve", TT_(ft[:, 0:16], eoffC[:, :], Pn[:, cs],
                                 ALU.add), waits=[t_Pn], inc="ve")
            u = P.emit("ve", TS(ft[:, 80:96], Pn[:, cs], CEXP - 0.5, None,
                                ALU.is_ge), waits=[u], inc="ve")
            u = P.emit("ve", STT(ft[:, 0:16], ft[:, 80:96], 1.0e6, ft[:, 0:16],
                                 ALU.mult, ALU.add), waits=[u], inc="ve")
            u = P.emit("ve", TT_(ft[:, 16:32], scores[:, cs],
                                 ebias_s[:, :], ALU.add), waits=[u], inc="ve")
            for k in range(4):
                u = P.emit("ve", RED(ft[:, 32:33], ft[:, 16:32], ALU.max),
                           waits=[u], inc="ve")
                u = P.emit("ve", TS(ft[:, 48:64], ft[:, 16:32], ft[:, 32:33],
                                    None, ALU.is_equal), waits=[u], inc="ve")
                si = tk_ring.next()
                sidx = si["o"]
                u = P.emit("ve", TT_(ft[:, 64:80], ft[:, 48:64], ft[:, 0:16],
                                     ALU.mult), waits=[u, si["free"]],
                           inc="ve")
                u = P.emit("ve", RED(posf[:, sidx:sidx + 1], ft[:, 64:80],
                                     ALU.add), waits=[u], inc="ve")
                u = P.emit("ve", TT_(ft[:, 64:80], ft[:, 48:64],
                                     combine[:, cs], ALU.mult),
                           waits=[u], inc="ve")
                u = P.emit("ve", RED(cwf[:, sidx:sidx + 1], ft[:, 64:80],
                                     ALU.add), waits=[u], inc="ve")
                u = P.emit("ve", CP(posi[:, sidx:sidx + 1],
                                    posf[:, sidx:sidx + 1]), waits=[u],
                           inc="ve")
                u = P.emit("ve", TS(tk2[:, 2 * sidx:2 * sidx + 1],
                                    tid[:, c:c + 1], float(p * 1024), None,
                                    ALU.add), waits=[u], inc="ve")
                u = P.emit("ve", CP(
                    tk2[:, 2 * sidx + 1:2 * sidx + 2],
                    cwf[:, sidx:sidx + 1].bitcast(I32)), waits=[u], inc="ve")
                o_ap = idxcw_d.ap()
                off_ap = posi[:, sidx:sidx + 1]
                in_ap = tk2[:, 2 * sidx:2 * sidx + 2]
                tsc = P.emit("gp", (lambda oa=o_ap, fa=off_ap, ia=in_ap:
                             (lambda g: g.indirect_dma_start(
                                 oa, IndirectOffsetOnAxis(fa, 0), ia, None)))(),
                             waits=[u, fill_tok], inc=f"qsc{sidx}", inc_n=16)
                si["free"] = tsc
                u = P.emit("ve", STT(ft[:, 16:32], ft[:, 48:64], NEG_BIG,
                                     ft[:, 16:32], ALU.mult, ALU.add),
                           waits=[u], inc="ve")
        scatters_done = [(f"qsc{i}", P.ctr.get(f"qsc{i}", 0))
                         for i in range(4)]

        prev_gu_read = None
        for e in range(16):
            par = e % 2
            tkb_cur = tksb[par]
            for cc3 in range(3):
                P.emit("sy", DMA(
                    tkb_cur[:, 2 * cc3:2 * cc3 + 2],
                    idxcw_d.ap()[e * CEXP + cc3 * 128:e * CEXP + (cc3 + 1) * 128, :]),
                    waits=(scatters_done + [tkb_war[par]]) if cc3 == 0 else [],
                    inc=f"qi{par}", inc_n=16)
            tkb_tok = (f"qi{par}", P.ctr[f"qi{par}"])
            tcl = None
            for sc in range(3):
                tcl = P.emit("ve", TS(ixc[par][sc][:, 0:1],
                                      tkb_cur[:, 2 * sc:2 * sc + 1],
                                      float(p * 1024 + 1023), None, ALU.min),
                             waits=[tkb_tok] if sc == 0 else [], inc="ve")
            g_toks = []
            for sc in range(3):
                o_ap = xe_nat[:, sc * 2048:(sc + 1) * 2048]
                in_ap = xn2n_d.ap()
                off_ap = ixc[par][sc][:, 0:1]
                tg = P.emit("gp", (lambda oa=o_ap, ia=in_ap, fa=off_ap:
                            (lambda g: g.indirect_dma_start(
                                oa, None, ia, IndirectOffsetOnAxis(fa, 0))))(),
                            waits=[tcl, xn_ready, prev_gu_read,
                                   dense_state["last_gu"] if e == 0 else None],
                            inc=f"qga{sc}", inc_n=16)
                g_toks.append((f"qga{sc}", P.ctr[f"qga{sc}"]))
            tr_last = None
            for sc in range(3):
                for kk in range(16):
                    bt = ptb.next()
                    tr_ = P.emit("te", TR(
                        bt["o"][:, 0:128],
                        xe_nat[:, sc * 2048 + kk * 128:sc * 2048 + kk * 128 + 128],
                        idn16[:, :]),
                        waits=[g_toks[sc], bt["free"]], inc="te")
                    dcol = kk * CEXP + sc * 128
                    o_ap = xeT[:, dcol:dcol + 128]
                    i_ap = bt["o"][:, 0:128]
                    tr_last = P.emit("se", (lambda oa=o_ap, ia=i_ap:
                                     (lambda s: s.copy(oa, ia)))(),
                                     waits=[tr_], inc="se")
                    bt["free"] = tr_last
            xeT_ready = tr_last
            last_tr_te = tr_

            wi = wdr.i % 2
            wd_slot = wdr.next()
            t_wd = P.emit("gp", DMA(wd_slot["o"][:, :], dw_d.ap()[e]),
                          waits=[wd_slot["free"]], inc=f"qd{wi}", inc_n=16)
            act_toks = []
            tok = None
            for i in range(6):
                pair = []
                for m in (i, i + 6):
                    gi = gur.i % 3
                    slot = gur.next()
                    t_w = P.emit("gp", DMA(slot["o"][:, :], gu_d.ap()[e, m]),
                                 waits=[slot["free"]], inc=f"qgu{gi}",
                                 inc_n=16)
                    b = pool.next()
                    for ki in range(16):
                        w = ([t_w, b["free"], xeT_ready] if ki == 0 else [])
                        tok = P.emit("te", MM(
                            b["o"][:, 0:CEXP],
                            slot["o"][:, ki * 128:(ki + 1) * 128],
                            xeT[:, ki * CEXP:(ki + 1) * CEXP],
                            ki == 0, ki == 15),
                            waits=w, inc="te" if ki == 15 else None)
                    slot["free"] = tok
                    pair.append((b, tok))
                (bg, tokg), (bu, toku) = pair
                t_sg = P.emit("se", ACT(sgbuf[:, 0:CEXP], bg["o"][:, 0:CEXP],
                                        AF.Silu),
                              waits=[tokg, prev_sg[0]], inc="se")
                bg["free"] = t_sg
                t_act = P.emit("ve", TT_(
                    actB[:, i * CEXP:(i + 1) * CEXP], sgbuf[:, 0:CEXP],
                    bu["o"][:, 0:CEXP], ALU.mult),
                    waits=[t_sg, toku, prev_act_te], inc="ve")
                bu["free"] = t_act
                prev_sg[0] = t_act
                act_toks.append(t_act)
            prev_gu_read = tok

            tok_dn = None
            last_ys = None
            for sc in range(3):
                ysl = ysp_ring.next()
                ypar = ysl["o"]
                drains = []
                for ncol in range(4):
                    b = pool.next()
                    for ki in range(6):
                        w = ([t_wd, b["free"]] + act_toks) if ki == 0 else []
                        tok_dn = P.emit("te", MM(
                            b["o"][:, :],
                            actB[:, ki * CEXP + sc * 128:ki * CEXP + sc * 128 + 128],
                            wd_slot["o"][:, ki * 2048 + ncol * 512:ki * 2048 + ncol * 512 + 512],
                            ki == 0, ki == 5),
                            waits=w, inc="te" if ki == 5 else None)
                    t_dr = P.emit("ve", TS(
                        y_sp[:, ypar * 2048 + ncol * 512:ypar * 2048 + ncol * 512 + 512],
                        b["o"][:, :],
                        tkb_cur[:, 2 * sc + 1:2 * sc + 2].bitcast(F32),
                        None, ALU.mult),
                        waits=[tok_dn, ysl["free"]], inc="ve")
                    drains.append(t_dr)
                    b["free"] = t_dr
                o_ap = y_d.ap()
                off_ap = tkb_cur[:, 2 * sc:2 * sc + 1]
                i_ap = y_sp[:, ypar * 2048:(ypar + 1) * 2048]
                last_ys = P.emit("gp", (lambda oa=o_ap, fa=off_ap, ia=i_ap:
                                 (lambda g: g.indirect_dma_start(
                                     oa, IndirectOffsetOnAxis(fa, 0), ia, None,
                                     compute_op=ALU.add)))(),
                                 waits=drains + [y_base_tok],
                                 inc=f"qys{ypar}", inc_n=16)
                ysl["free"] = last_ys
            wd_slot["free"] = tok_dn
            prev_act_te = tok_dn
            tkb_war[par] = last_ys

        prev_idx_load = [(f"qi{i}", P.ctr.get(f"qi{i}", 0)) for i in range(2)]
        prev_ys = [(f"qys{i}", P.ctr.get(f"qys{i}", 0)) for i in range(2)]
        prev_te_end2 = prev_act_te

    fin = [(k, v) for k, v in P.ctr.items() if k.startswith("q")]
    P.emit("sy", NOP(), waits=fin)
    P.emit("gp", NOP(), waits=[t for t in fin if t[0].startswith("qys")])
    _realize(nc, P)
    return nc


def _realize(nc, P):
    import contextlib
    with contextlib.ExitStack() as ctx:
        sems = {s: ctx.enter_context(nc.semaphore(s))
                for s in sorted(P.ctr.keys())}
        block = ctx.enter_context(nc.Block())

        def make(eng_name):
            ops = P.ops[eng_name]
            seen = {}

            def body(eng):
                for fn, waits, inc in ops:
                    for s, v in waits:
                        if seen.get(s, -1) < v:
                            eng.wait_ge(sems[s], v)
                            seen[s] = v
                    ins = fn(eng)
                    if inc is not None:
                        ins.then_inc(sems[inc[0]], inc[1])
            return body

        for eng_name, attr in (("te", "tensor"), ("ve", "vector"),
                               ("se", "scalar"), ("sy", "sync"),
                               ("gp", "gpsimd")):
            getattr(block, attr)(make(eng_name))


def prep_inputs(L0_attn_Wq, L0_attn_Wk, L0_attn_Wv, L0_attn_memory,
                L0_attn_memory_norm, L0_attn_memory_weight, L0_ffn_router_w,
                L0_ffn_e_bias, L0_ffn_gate_up_proj, L0_ffn_down_proj,
                L0_ffn_shared_gate_w, L0_ffn_shared_up_w, L0_ffn_shared_down_w):
    f = np.float32
    wq = _lhsT_blocks(np.asarray(L0_attn_Wq, f))
    wk = _lhsT_blocks(np.asarray(L0_attn_Wk, f))
    wv = _rhs_layout(np.asarray(L0_attn_Wv, f))
    mem = _lhsT_blocks(np.asarray(L0_attn_memory, f).T)
    memn = np.ascontiguousarray(
        np.asarray(L0_attn_memory_norm, f).reshape(4, 128, 1)
        .transpose(1, 0, 2).reshape(128, 4))
    mw = np.ascontiguousarray(
        np.broadcast_to(np.asarray(L0_attn_memory_weight, f).reshape(1, 1),
                        (128, 1)))
    ebias = np.ascontiguousarray(
        np.broadcast_to(np.asarray(L0_ffn_e_bias, f).reshape(1, 16), (128, 16)))
    rw = _rhs_layout(np.asarray(L0_ffn_router_w, f))
    gup = np.asarray(L0_ffn_gate_up_proj, f)
    dp = np.asarray(L0_ffn_down_proj, f)
    sh_gu = np.concatenate([np.asarray(L0_ffn_shared_gate_w, f),
                            np.asarray(L0_ffn_shared_up_w, f)], axis=0)
    gu = np.stack([_lhsT_blocks(gup[e]) for e in range(16)]
                  + [_lhsT_blocks(sh_gu)])
    dw = np.stack([_rhs_layout(dp[e]) for e in range(16)]
                  + [_rhs_layout(np.asarray(L0_ffn_shared_down_w, f))])
    idn16 = np.eye(128, dtype=ml_dtypes.bfloat16)
    idn32 = np.eye(128, dtype=np.float32)
    ll = np.triu(np.ones((128, 128), np.float32), 1).astype(ml_dtypes.bfloat16)
    ones1 = np.ones((1, 128), np.float32)
    tid = np.ascontiguousarray(
        (np.arange(8)[None, :] * 128 + np.arange(128)[:, None]).astype(np.int32))
    eoff = np.ascontiguousarray(np.broadcast_to(
        (np.arange(16) * CEXP).astype(np.float32)[None, :], (128, 16)))
    return {"wq": wq, "wk": wk, "wv": wv, "mem": mem, "memn": memn, "mw": mw,
            "ebias": ebias, "rw": rw, "gu": gu, "dw": dw,
            "idn16": idn16, "idn32": idn32, "ll": ll, "ones1": ones1,
            "tid": tid, "eoff": eoff}


def kernel(L0_attn_x, **w):
    common = prep_inputs(**w)
    x = np.ascontiguousarray(np.asarray(L0_attn_x, np.float32))
    nc = build_nc(attn_only=False)
    in_maps = [dict(common, x=np.ascontiguousarray(x[c])) for c in range(8)]
    res = run_bass_kernel_spmd(nc, in_maps, core_ids=list(range(8)))
    return np.stack([np.asarray(r["y"], np.float32) for r in res.results])


# version 16
# speedup vs baseline: 1.2344x; 1.0372x over previous
import numpy as np
import ml_dtypes
import concourse.bass as bass
import concourse.mybir as mybir
from concourse.bass_utils import run_bass_kernel_spmd
from concourse.bass_core_rust import IndirectOffsetOnAxis

F32 = mybir.dt.float32
BF16 = mybir.dt.bfloat16
AF = mybir.ActivationFunctionType
ALU = mybir.AluOpType
AX = mybir.AxisListType

S, D = 2048, 2048
NQ, NH = 4, 8
NE = 17
STILE = 512
NSt = S // STILE
CEXP = 384
I32 = mybir.dt.int32
EPS = 1e-5
NEG_BIG = -1e30
ENGINES = ("te", "ve", "se", "sy", "gp")
_LAST_PROG = [None]


class Ring:
    def __init__(self, slots):
        self.slots = [{"o": s, "free": None} for s in slots]
        self.i = 0

    def next(self):
        s = self.slots[self.i % len(self.slots)]
        self.i += 1
        return s


class Prog:
    def __init__(self):
        self.ops = {e: [] for e in ENGINES}
        self.ctr = {"te": 0, "ve": 0, "se": 0, "ph": 0}

    def emit(self, eng, fn, waits=(), inc=None, inc_n=1):
        flat = []
        for t in waits:
            if t is None:
                continue
            if isinstance(t, list):
                flat.extend(x for x in t if x is not None)
            else:
                flat.append(t)
        w = tuple(flat)
        tok = None
        if inc is not None:
            self.ctr.setdefault(inc, 0)
            self.ctr[inc] += inc_n
            tok = (inc, self.ctr[inc])
        self.ops[eng].append((fn, w, (inc, inc_n) if inc else None))
        return tok


def MM(out_ap, l_ap, r_ap, start, stop, tp=None):
    if tp is None:
        return lambda t: t.matmul(out_ap, l_ap, r_ap, start=start, stop=stop)
    return lambda t: t.matmul(out_ap, l_ap, r_ap, start=start, stop=stop,
                              tile_position=tp)


def TR(out_ap, in_ap, ident_ap):
    return lambda t: t.transpose(out_ap, in_ap, ident_ap)


def CP(out_ap, in_ap):
    return lambda v: v.tensor_copy(out_ap, in_ap)


def TT_(out_ap, a_ap, b_ap, op):
    return lambda v: v.tensor_tensor(out_ap, a_ap, b_ap, op)


def TS(out_ap, in_ap, s1, s2, op0, op1=None):
    if op1 is None:
        return lambda v: v.tensor_scalar(out_ap, in_ap, s1, None, op0)
    return lambda v: v.tensor_scalar(out_ap, in_ap, s1, s2, op0, op1)


def STT(out_ap, in0_ap, sc, in1_ap, op0, op1):
    return lambda v: v.scalar_tensor_tensor(out_ap, in0_ap, sc, in1_ap, op0, op1)


def ACT(out_ap, in_ap, func, bias=0.0, scale=1.0):
    return lambda s: s.activation(out_ap, in_ap, func, bias=bias, scale=scale)


def DMA(out_ap, in_ap):
    return lambda e: e.dma_start(out=out_ap, in_=in_ap)


def RED(out_ap, in_ap, op):
    return lambda v: v.tensor_reduce(out_ap, in_ap, AX.X, op)


def RECIP(out_ap, in_ap):
    return lambda v: v.reciprocal(out_ap, in_ap)


def MEMSET(ap, val):
    return lambda v: v.memset(ap, val)


def NOP():
    return lambda e: e.nop()


def VNOP():
    return lambda v: v.engine_nop()


def _lhsT_blocks(W):
    dout, din = W.shape
    Ki, nm = din // 128, dout // 128
    A = np.ascontiguousarray(W.T).reshape(Ki, 128, nm, 128)
    A = np.ascontiguousarray(A.transpose(2, 1, 0, 3))
    return A.reshape(nm, 128, Ki * 128)


def _rhs_layout(W):
    dout, din = W.shape
    Ki = din // 128
    B = np.ascontiguousarray(W.T).reshape(Ki, 128, dout).transpose(1, 0, 2)
    return np.ascontiguousarray(B).reshape(128, Ki * dout)


def build_nc(attn_only=False, dbg=None):
    nc = bass.Bass("TRN2", target_bir_lowering=False)
    P = Prog()
    _LAST_PROG[0] = P

    x_d = nc.dram_tensor("x", [S, D], F32, kind="ExternalInput")
    wq_d = nc.dram_tensor("wq", [16, 128, 2048], F32, kind="ExternalInput")
    wk_d = nc.dram_tensor("wk", [4, 128, 2048], F32, kind="ExternalInput")
    wv_d = nc.dram_tensor("wv", [128, 8192], F32, kind="ExternalInput")
    mem_d = nc.dram_tensor("mem", [4, 128, 512], F32, kind="ExternalInput")
    memn_d = nc.dram_tensor("memn", [128, 4], F32, kind="ExternalInput")
    mw_d = nc.dram_tensor("mw", [128, 1], F32, kind="ExternalInput")
    ebias_d = nc.dram_tensor("ebias", [128, 16], F32, kind="ExternalInput")
    rw_d = nc.dram_tensor("rw", [128, 256], F32, kind="ExternalInput")
    gu_d = nc.dram_tensor("gu", [NE, 12, 128, 2048], F32, kind="ExternalInput")
    dw_d = nc.dram_tensor("dw", [NE, 128, 12288], F32, kind="ExternalInput")
    idn16_d = nc.dram_tensor("idn16", [128, 128], BF16, kind="ExternalInput")
    idn32_d = nc.dram_tensor("idn32", [128, 128], F32, kind="ExternalInput")
    x1_d = nc.dram_tensor("x1buf", [S, D], F32, kind="Internal")
    xn2n_d = nc.dram_tensor("xn2nat", [S, D], BF16, kind="Internal")
    idxcw_d = nc.dram_tensor("idxcw", [16 * CEXP, 2], I32, kind="Internal")
    ll_d = nc.dram_tensor("ll", [128, 128], BF16, kind="ExternalInput")
    ones1_d = nc.dram_tensor("ones1", [1, 128], F32, kind="ExternalInput")
    tid_d = nc.dram_tensor("tid", [128, 8], I32, kind="ExternalInput")
    eoff_d = nc.dram_tensor("eoff", [128, 16], F32, kind="ExternalInput")
    y_d = nc.dram_tensor("y", [S, D], F32, kind="ExternalOutput")
    attn_out_d = y_d if attn_only else x1_d

    stp0 = nc.alloc_psum_tensor("stp0", [128, 1024], F32)
    stp1 = nc.alloc_psum_tensor("stp1", [128, 1024], F32)
    pvp0 = nc.alloc_psum_tensor("pvp0", [128, 512], F32)
    pvp1 = nc.alloc_psum_tensor("pvp1", [128, 512], F32)
    pool = Ring([stp0[:, 0:512], stp0[:, 512:1024],
                 stp1[:, 0:512], stp1[:, 512:1024],
                 pvp0[:, :], pvp1[:, :]])
    ptb = Ring([nc.alloc_psum_tensor(f"pt{i}", [128, 1024], BF16) for i in range(2)])

    ARENA = 212736
    arena = nc.alloc_sbuf_tensor("arena", [128, ARENA], mybir.dt.uint8)
    base = nc.lookup_mloc(arena).addr
    cursor = [0]

    def place(name, shape, dtype, align=128):
        nbytes = int(np.prod(shape[1:])) * mybir.dt.size(dtype)
        off = cursor[0]
        cursor[0] = (off + nbytes + align - 1) & ~(align - 1)
        return nc.alloc_sbuf_tensor_at(name, list(shape), dtype,
                                       offset=base + off)

    idn16 = place("idn16s", [128, 128], BF16)
    idn32 = place("idn32s", [128, 128], F32)
    sq = place("sq", [128, 2048], F32)
    tiny = place("tiny", [128, 16], F32)
    small = place("small", [128, 16], F32)
    wsig = place("wsig", [128, 2], F32)
    epsc = place("epsc", [128, 1], F32)
    region0 = cursor[0]

    wbuf = place("wbuf", [128, 16384], BF16)
    memB = place("memB", [128, 2048], BF16)
    memn4 = place("memn4", [128, 4], BF16)
    kT = place("kT", [128, 4 * 2048], BF16)
    v_ext = place("v_ext", [128, 16 * 520 + 128], BF16)
    xn1T = place("xn1T", [128, 16 * 512], BF16)
    qT = place("qT", [128, 16 * 512], BF16)
    sgT = place("sgT", [128, 16 * 512], BF16)
    acc = place("acc", [128, 4 * 2048], F32)
    xn1c = place("xn1c", [128, 2048], BF16)
    oT_sb = place("oT_sb", [128, 1024], F32)
    amT_sb = place("amT_sb", [128, 4 * 512], F32)
    amden = place("amden", [1, 512], F32)
    lnm = place("lnm", [128, 16], F32)
    lnr = place("lnr", [128, 16], F32)
    ptc = Ring([place(f"ptc{i}", [128, 1024], BF16) for i in range(3)])
    xring = Ring([place(f"xr{i}", [128, 2048], F32) for i in range(2)])
    attn_end = cursor[0]
    assert attn_end <= ARENA, f"attn SBUF over: {attn_end}"

    cursor[0] = region0
    y_acc = place("y_acc", [128, 8 * 2048], F32)
    xn2T_off = cursor[0]
    xn2T = place("xn2T", [128, 16 * 1024], BF16)
    xe_nat = nc.alloc_sbuf_tensor_at("xe_nat", [128, 3 * 2048], BF16,
                                     offset=base + xn2T_off)
    xeT = nc.alloc_sbuf_tensor_at("xeT", [128, 16 * CEXP], BF16,
                                  offset=base + xn2T_off + 3 * 2048 * 2)
    actB = place("actB", [128, 6 * 1024], BF16)
    y_sp = place("y_sp", [128, 2 * 2048], F32)
    m_bf = place("m_bf", [128, 128], BF16)
    Pn = place("Pn", [128, 128], F32)
    trow0 = place("trow0", [1, 128], F32)
    trow1 = place("trow1", [1, 128], F32)
    onesc = place("onesc", [128, 1], BF16)
    posf = place("posf", [128, 4], F32)
    cwf = place("cwf", [128, 4], F32)
    posi = place("posi", [128, 4], I32)
    tk2 = place("tk2", [128, 8], I32)
    tksb = [place(f"tksb{i}", [128, 6], I32) for i in range(2)]
    ixc = [[place(f"ixc{i}_{j}", [128, 1], I32, align=32) for j in range(3)]
           for i in range(2)]
    Ll = place("Lls", [128, 128], BF16)
    ones1 = place("ones1s", [1, 128], F32)
    tid = place("tids", [128, 8], I32)
    eoffC = place("eoffs", [128, 16], F32)
    wdr = Ring([place(f"wd{i}", [128, 6 * 2048], BF16) for i in range(2)])
    gur = Ring([place(f"gur{i}", [128, 2048], BF16) for i in range(3)])
    xn2c = place("xn2c", [128, 2048], BF16)
    sgbuf = place("sgbuf", [128, 1024], F32)
    scores = place("scores", [128, 8 * 16], F32)
    combine = place("combine", [128, 8 * 16], F32)
    ft = place("ft", [128, 96], F32)
    ebias_s = place("ebias_s", [128, 16], F32)
    rwT = place("rwT", [128, 256], BF16)
    m2 = place("m2", [128, 8], F32)
    r2 = place("r2", [128, 8], F32)
    ffn_end = cursor[0]
    assert ffn_end <= ARENA, f"ffn SBUF over: {ffn_end}"

    elu1, elu2, elu3 = sq[:, 0:512], sq[:, 512:1024], sq[:, 1024:1536]

    ln_state = {"prev_se": None, "prev_rd": None}

    def ln_stats(xc_ap, mcol_ap, rcol_ap, deps):
        t_sq = P.emit("ve", TT_(sq[:, :], xc_ap, xc_ap, ALU.mult),
                      waits=list(deps) + [ln_state["prev_se"],
                                          ln_state["prev_rd"]], inc="ve")
        t0 = P.emit("ve", RED(tiny[:, 0:1], xc_ap, ALU.add), inc="ve")
        t1 = P.emit("ve", RED(tiny[:, 1:2], sq[:, :], ALU.add), waits=[t_sq],
                    inc="ve")
        ln_state["prev_rd"] = t1
        t2 = P.emit("ve", TS(tiny[:, 2:3], tiny[:, 0:1], 1.0 / D, None,
                             ALU.mult), waits=[t0], inc="ve")
        P.emit("ve", CP(mcol_ap, tiny[:, 2:3]), waits=[t2])
        t3 = P.emit("ve", TS(tiny[:, 3:4], tiny[:, 1:2], 1.0 / D, None,
                             ALU.mult), waits=[t1], inc="ve")
        t4 = P.emit("ve", TT_(tiny[:, 4:5], tiny[:, 2:3], tiny[:, 2:3],
                              ALU.mult), waits=[t2], inc="ve")
        tv = P.emit("ve", TT_(tiny[:, 5:6], tiny[:, 3:4], tiny[:, 4:5],
                              ALU.subtract), waits=[t3, t4], inc="ve")
        tl = P.emit("se", ACT(tiny[:, 6:7], tiny[:, 5:6], AF.Ln,
                              bias=epsc[:, 0:1]), waits=[tv], inc="se")
        ts = P.emit("se", ACT(rcol_ap, tiny[:, 6:7], AF.Exp, scale=-0.5),
                    waits=[tl], inc="se")
        ln_state["prev_se"] = ts
        return ts

    tr_state = {"last_te": None}

    def transpose_to_T(src_chunk, dstT, dst_col, tok_w, deps, drain_eng="ve"):
        last = tok = None
        for half in range(2):
            b = ptb.next()
            tok = None
            for j in range(8):
                k = half * 8 + j
                w = (list(deps) + [b["free"]]) if j == 0 else []
                tok = P.emit("te", TR(b["o"][:, j * 128:(j + 1) * 128],
                                      src_chunk[:, k * 128:(k + 1) * 128],
                                      idn16[:, :]),
                             waits=w, inc="te" if j == 7 else None)
            src = b["o"][:, 0:1024].rearrange("p (k t) -> p k t", t=128)
            dst = dstT[:, :].rearrange("p (k t) -> p k t", t=tok_w)
            dslice = dst[:, half * 8:half * 8 + 8, dst_col:dst_col + 128]
            if drain_eng == "ve":
                last = P.emit("ve", CP(dslice, src), waits=[tok], inc="ve")
            else:
                last = P.emit("se", (lambda oa=dslice, ia=src:
                              (lambda s_: s_.copy(oa, ia)))(),
                              waits=[tok], inc="se")
            b["free"] = last
        tr_state["last_te"] = tok
        return last

    def barrier():
        base = P.ctr["ph"]
        qtot = [(k, v) for k, v in P.ctr.items() if k.startswith("q")]
        P.emit("sy", NOP(), waits=qtot, inc="ph")
        P.emit("gp", NOP(), waits=qtot, inc="ph")
        for eng in ("te", "ve", "se"):
            P.emit(eng, NOP(), inc="ph")
        for eng in ENGINES:
            P.emit(eng, NOP(), waits=[("ph", base + 5)])

    P.emit("sy", DMA(idn16[:, :], idn16_d.ap()), inc="qcs", inc_n=16)
    P.emit("sy", DMA(idn32[:, :], idn32_d.ap()), inc="qcs", inc_n=16)
    P.emit("sy", DMA(small[:, 0:1], mw_d.ap()), inc="qcs", inc_n=16)
    for m in range(4):
        P.emit("gp", DMA(wbuf[:, m * 2048:(m + 1) * 2048], wk_d.ap()[m]),
               inc="qcg", inc_n=16)
    P.emit("gp", DMA(wbuf[:, 8192:16384], wv_d.ap()), inc="qcg", inc_n=16)
    for m in range(4):
        P.emit("gp", DMA(memB[:, m * 512:(m + 1) * 512], mem_d.ap()[m]),
               inc="qcg", inc_n=16)
    P.emit("gp", DMA(memn4[:, :], memn_d.ap()), inc="qcg", inc_n=16)
    consts_done = [("qcs", P.ctr["qcs"]), ("qcg", P.ctr["qcg"])]
    t_idn16 = t_idn32 = t_mw = t_wk = t_wv = t_mem = t_memn = consts_done

    P.emit("ve", MEMSET(epsc[:, 0:1], EPS))
    t1 = P.emit("se", ACT(small[:, 1:2], small[:, 0:1], AF.Exp, scale=-1.0),
                waits=[t_mw], inc="se")
    t2 = P.emit("ve", TS(small[:, 2:3], small[:, 1:2], 1.0, None, ALU.add),
                waits=[t1], inc="ve")
    t3 = P.emit("ve", RECIP(wsig[:, 0:1], small[:, 2:3]), waits=[t2], inc="ve")
    P.emit("ve", TS(wsig[:, 1:2], wsig[:, 0:1], -1.0, 1.0, ALU.mult, ALU.add),
           waits=[t3], inc="ve")

    P.emit("ve", MEMSET(v_ext[:, 16 * 520:16 * 520 + 128], 0.0))
    ve_view = v_ext[:, 0:16 * 520].rearrange("p (t c) -> p t c", c=520)
    for h in range(NH):
        P.emit("ve", MEMSET(ve_view[:, :, h * 65 + 64:h * 65 + 65], 1.0))
    t_ones = P.emit("ve", VNOP(), inc="ve")

    def dbg_finish(pairs):
        for r0, nr, src in pairs:
            c = nr // 128
            dst = y_d.ap()[r0:r0 + nr, :].rearrange("(c p) f -> p c f", c=c)
            sview = src.rearrange("p (c f) -> p c f", c=c)
            P.emit("gp", DMA(dst, sview), waits=[("te", P.ctr["te"]),
                                                 ("ve", P.ctr["ve"]),
                                                 ("se", P.ctr["se"])],
                   inc="qdbg", inc_n=16)
        P.emit("gp", NOP(), waits=[("qdbg", P.ctr.get("qdbg", 0))])
        _realize(nc, P)
        return nc

    if dbg == "const":
        return dbg_finish([])

    for tile in range(4):
        drain_toks = []
        for lc in range(4):
            ci = tile * 4 + lc
            xs = xring.next()
            xsem = f"qx{ci % 2}"
            t_x = P.emit("sy", DMA(xs["o"][:, :],
                                   x_d.ap()[ci * 128:(ci + 1) * 128, :]),
                         waits=[xs["free"]], inc=xsem, inc_n=16)
            ts = ln_stats(xs["o"][:, :], lnm[:, ci:ci + 1], lnr[:, ci:ci + 1],
                          [t_x])
            t_n = P.emit("ve", TS(xn1c[:, :], xs["o"][:, :], lnm[:, ci:ci + 1],
                                  lnr[:, ci:ci + 1], ALU.subtract, ALU.mult),
                         waits=[ts, tr_state["last_te"]], inc="ve")
            xs["free"] = t_n
            drain_toks.append(
                transpose_to_T(xn1c[:, :], xn1T, lc * 128, 512, [t_n],
                               drain_eng="se"))
        for m in range(4):
            b = pool.next()
            tok = None
            for ki in range(16):
                w = ([t_wk, b["free"]] + drain_toks) if ki == 0 else []
                tok = P.emit("te", MM(
                    b["o"][:, :],
                    wbuf[:, m * 2048 + ki * 128:m * 2048 + ki * 128 + 128],
                    xn1T[:, ki * 512:(ki + 1) * 512], ki == 0, ki == 15),
                    waits=w, inc="te" if ki == 15 else None)
            b["free"] = P.emit("ve", CP(
                kT[:, m * 2048 + tile * 512:m * 2048 + tile * 512 + 512],
                b["o"][:, :]), waits=[tok], inc="ve")
        for mt in range(4):
            b = pool.next()
            tc = tile * 4 + mt
            tok = None
            for ki in range(16):
                w = ([t_wv, b["free"]] + drain_toks) if ki == 0 else []
                tok = P.emit("te", MM(
                    b["o"][:, :],
                    xn1T[:, ki * 512 + mt * 128:ki * 512 + mt * 128 + 128],
                    wbuf[:, 8192 + ki * 512:8192 + (ki + 1) * 512],
                    ki == 0, ki == 15),
                    waits=w, inc="te" if ki == 15 else None)
            lastd = None
            for h in range(NH):
                lastd = P.emit("ve", CP(
                    v_ext[:, tc * 520 + h * 65:tc * 520 + h * 65 + 64],
                    b["o"][:, h * 64:(h + 1) * 64]),
                    waits=[tok, t_ones] if h == 0 else [],
                    inc="ve" if h == NH - 1 else None)
            b["free"] = lastd

    barrier()

    if dbg == "pass1":
        for src, r0, c0, w_ in ((lnm, 1024, 0, 16), (lnr, 1024, 16, 16)):
            P.emit("gp", DMA(y_d.ap()[r0:r0 + 128, c0:c0 + w_], src[:, 0:w_]),
                   waits=[("te", P.ctr["te"]), ("ve", P.ctr["ve"]),
                          ("se", P.ctr["se"])], inc="qdbg", inc_n=16)
        P.emit("gp", DMA(y_d.ap()[1152:1280, :], xn1c[:, :]),
               inc="qdbg", inc_n=16)
        return dbg_finish([(0, 512, kT[:, :]), (512, 512, xn1T[:, :])])

    wq_ring = Ring([i * 2048 for i in range(4)])
    prev_te_end = None
    acc_war = None
    for st in range(NSt):
        acc_ready = []
        for lc in range(4):
            ci = st * 4 + lc
            acc_ready.append(P.emit("sy", DMA(
                acc[:, lc * 2048:(lc + 1) * 2048],
                x_d.ap()[ci * 128:(ci + 1) * 128, :]),
                waits=[acc_war], inc=f"qa{lc}", inc_n=16))
        xn1_toks = []
        for lc in range(4):
            ci = st * 4 + lc
            t_n = P.emit("ve", TS(xn1c[:, :], acc[:, lc * 2048:(lc + 1) * 2048],
                                  lnm[:, ci:ci + 1], lnr[:, ci:ci + 1],
                                  ALU.subtract, ALU.mult),
                         waits=[acc_ready[lc], prev_te_end,
                                tr_state["last_te"]], inc="ve")
            xn1_toks.append(transpose_to_T(xn1c[:, :], xn1T, lc * 128, 512, [t_n]))
        prev_elu_se = None
        prev_elu_ve = None
        for m in range(16):
            slot = wq_ring.next()
            col = slot["o"]
            t_w = P.emit("gp", DMA(wbuf[:, col:col + 2048], wq_d.ap()[m]),
                         waits=[slot["free"]], inc=f"qw{m % 4}", inc_n=16)
            b = pool.next()
            tok = None
            for ki in range(16):
                w = ([t_w, b["free"]] + xn1_toks) if ki == 0 else []
                tok = P.emit("te", MM(
                    b["o"][:, :],
                    wbuf[:, col + ki * 128:col + ki * 128 + 128],
                    xn1T[:, ki * 512:(ki + 1) * 512], ki == 0, ki == 15),
                    waits=w, inc="te" if ki == 15 else None)
            slot["free"] = tok
            P.emit("ve", CP(qT[:, m * 512:(m + 1) * 512], b["o"][:, :]),
                   waits=[tok])
            t_mn = P.emit("ve", TS(elu1, b["o"][:, :], 0.0, None, ALU.min),
                          waits=[prev_elu_se], inc="ve")
            t_e = P.emit("se", ACT(elu2, elu1, AF.Exp),
                         waits=[t_mn, prev_elu_ve], inc="se")
            prev_elu_se = t_e
            t_mx = P.emit("ve", TS(elu3, b["o"][:, :], 0.0, None, ALU.max),
                          waits=[prev_elu_ve], inc="ve")
            t_s = P.emit("ve", STT(sgT[:, m * 512:(m + 1) * 512], elu2, -1.0,
                                   elu3, ALU.add, ALU.add),
                         waits=[t_e, t_mx], inc="ve")
            prev_elu_ve = t_s
            b["free"] = t_s
        q_done = P.emit("ve", VNOP(), inc="ve")

        stpt = [{"o": stp0, "free": [pool.slots[0]["free"], pool.slots[1]["free"]]},
                {"o": stp1, "free": [pool.slots[2]["free"], pool.slots[3]["free"]]}]
        pvt = [{"o": pvp0, "free": [pool.slots[4]["free"]]},
               {"o": pvp1, "free": [pool.slots[5]["free"]]}]
        first_heads = [q_done]
        for qg in range(NQ):
            for hp in range(4):
                h0, h1 = 2 * hp, 2 * hp + 1
                mcol = (qg * 4 + hp) * 512
                pend = None

                def emit_pv(pd, h0=h0, h1=h1):
                    j, pc, tex = pd
                    t0_ = P.emit("te", MM(
                        pvt[0]["o"][:, :],
                        v_ext[:, j * 520 + h0 * 65:j * 520 + h0 * 65 + 128],
                        pc["o"][:, 0:512], j == 0, j == 15),
                        waits=[tex] + (pvt[0]["free"] if j == 0 else []),
                        inc="te" if j == 15 else None)
                    t1_ = P.emit("te", MM(
                        pvt[1]["o"][:, :],
                        v_ext[:, j * 520 + h1 * 65:j * 520 + h1 * 65 + 128],
                        pc["o"][:, 512:1024], j == 0, j == 15),
                        waits=(pvt[1]["free"] if j == 0 else []),
                        inc="te" if j == 15 else None)
                    pc["free"] = t1_ if j == 15 else t0_
                    return t0_, t1_

                for i in range(16):
                    sp = stpt[i % 2]
                    w = [sp["free"]]
                    if qg == 0 and hp == 0 and i < 2:
                        w = w + first_heads
                    P.emit("te", MM(
                        sp["o"][:, 0:512],
                        kT[0:64, hp * 2048 + i * 128:hp * 2048 + i * 128 + 128],
                        qT[0:64, mcol:mcol + 512], True, True, tp=(0, 0)),
                        waits=w)
                    tqk1 = P.emit("te", MM(
                        sp["o"][:, 512:1024],
                        kT[64:128, hp * 2048 + i * 128:hp * 2048 + i * 128 + 128],
                        qT[64:128, mcol:mcol + 512], True, True, tp=(64, 0)),
                        inc="te")
                    pc = ptc.next()
                    tex = P.emit("se", ACT(pc["o"][:, 0:1024], sp["o"][:, 0:1024],
                                           AF.Exp, scale=0.125),
                                 waits=[tqk1, pc["free"]], inc="se")
                    sp["free"] = [tex]
                    if pend is not None:
                        emit_pv(pend)
                    pend = (i, pc, tex)
                tpv0, tpv1 = emit_pv(pend)

                td0 = P.emit("ve", CP(oT_sb[0:65, 0:512], pvt[0]["o"][0:65, :]),
                             waits=[tpv0], inc="ve")
                td1 = P.emit("ve", CP(oT_sb[0:65, 512:1024], pvt[1]["o"][0:65, :]),
                             waits=[tpv1], inc="ve")
                ta0 = ta1 = None
                for sc in range(4):
                    tt0 = P.emit("te", TR(pvt[0]["o"][:, 0:65],
                                          oT_sb[0:65, sc * 128:sc * 128 + 128],
                                          idn32[0:65, 0:65]),
                                 waits=[td0, t_idn32] + ([ta0] if ta0 else []),
                                 inc="te")
                    tt1 = P.emit("te", TR(
                        pvt[1]["o"][:, 0:65],
                        oT_sb[0:65, 512 + sc * 128:512 + sc * 128 + 128],
                        idn32[0:65, 0:65]),
                        waits=[td1] + ([ta1] if ta1 else []), inc="te")
                    tr = P.emit("ve", RECIP(small[:, 4:5], pvt[0]["o"][:, 64:65]),
                                waits=[tt0], inc="ve")
                    tw_ = P.emit("ve", TT_(small[:, 5:6], small[:, 4:5],
                                           wsig[:, 1:2], ALU.mult),
                                 waits=[tr], inc="ve")
                    c0 = sc * 2048 + qg * 512 + h0 * 64
                    ta0 = P.emit("ve", STT(acc[:, c0:c0 + 64],
                                           pvt[0]["o"][:, 0:64],
                                           small[:, 5:6], acc[:, c0:c0 + 64],
                                           ALU.mult, ALU.add),
                                 waits=[tw_], inc="ve")
                    tr1 = P.emit("ve", RECIP(small[:, 6:7], pvt[1]["o"][:, 64:65]),
                                 waits=[tt1], inc="ve")
                    tw1 = P.emit("ve", TT_(small[:, 7:8], small[:, 6:7],
                                           wsig[:, 1:2], ALU.mult),
                                 waits=[tr1], inc="ve")
                    c1 = sc * 2048 + qg * 512 + h1 * 64
                    ta1 = P.emit("ve", STT(acc[:, c1:c1 + 64],
                                           pvt[1]["o"][:, 0:64],
                                           small[:, 7:8], acc[:, c1:c1 + 64],
                                           ALU.mult, ALU.add),
                                 waits=[tw1], inc="ve")
                pvt[0]["free"] = [ta0]
                pvt[1]["free"] = [ta1]

        pool.slots[0]["free"] = stpt[0]["free"][0]
        pool.slots[1]["free"] = stpt[0]["free"][0]
        pool.slots[2]["free"] = stpt[1]["free"][0]
        pool.slots[3]["free"] = stpt[1]["free"][0]
        pool.slots[4]["free"] = pvt[0]["free"][0]
        pool.slots[5]["free"] = pvt[1]["free"][0]
        pool.i = 0

        for qg in range(NQ):
            bD = pool.next()
            tokD = None
            for ii in range(4):
                w = [q_done, t_memn, bD["free"]] if ii == 0 else []
                tokD = P.emit("te", MM(
                    bD["o"][0:1, :], memn4[:, ii:ii + 1],
                    sgT[:, (qg * 4 + ii) * 512:(qg * 4 + ii + 1) * 512],
                    ii == 0, ii == 3),
                    waits=w, inc="te" if ii == 3 else None)
            jtoks = []
            for jj in range(4):
                b = pool.next()
                tok = None
                for ii in range(4):
                    w = [t_mem, b["free"]] if ii == 0 else []
                    tok = P.emit("te", MM(
                        b["o"][:, :],
                        memB[:, jj * 512 + ii * 128:jj * 512 + ii * 128 + 128],
                        sgT[:, (qg * 4 + ii) * 512:(qg * 4 + ii + 1) * 512],
                        ii == 0, ii == 3),
                        waits=w, inc="te" if ii == 3 else None)
                t_am = P.emit("ve", CP(amT_sb[:, jj * 512:(jj + 1) * 512],
                                       b["o"][:, :]), waits=[tok], inc="ve")
                b["free"] = t_am
                jtoks.append(t_am)
            t_dn = P.emit("ve", CP(amden[0:1, :], bD["o"][0:1, :]),
                          waits=[tokD], inc="ve")
            bD["free"] = t_dn
            bDn = pool.next()
            prev_dr = None
            for sc in range(4):
                w = [t_dn] + ([bDn["free"]] if sc == 0 else [prev_dr])
                tsp = P.emit("te", TR(bDn["o"][:, 0:1],
                                      amden[0:1, sc * 128:(sc + 1) * 128],
                                      idn32[0:1, 0:1]),
                             waits=w, inc="te")
                prev_dr = P.emit("ve", CP(tiny[:, 10 + sc:11 + sc],
                                          bDn["o"][:, 0:1]),
                                 waits=[tsp], inc="ve")
            tdn = prev_dr
            last_sc = None
            for sc in range(4):
                trc = P.emit("ve", RECIP(tiny[:, 8:9], tiny[:, 10 + sc:11 + sc]),
                             waits=[tdn], inc="ve")
                tww = P.emit("ve", TT_(tiny[:, 9:10], tiny[:, 8:9], wsig[:, 0:1],
                                       ALU.mult), waits=[trc], inc="ve")
                for jj in range(4):
                    bA2 = pool.next()
                    ttj = P.emit("te", TR(
                        bA2["o"][:, 0:128],
                        amT_sb[:, jj * 512 + sc * 128:jj * 512 + sc * 128 + 128],
                        idn32[:, :]),
                        waits=[jtoks[jj], bA2["free"]], inc="te")
                    cc = sc * 2048 + qg * 512 + jj * 128
                    last_sc = P.emit("ve", STT(
                        acc[:, cc:cc + 128], bA2["o"][:, 0:128], tiny[:, 9:10],
                        acc[:, cc:cc + 128], ALU.mult, ALU.add),
                        waits=[ttj, tww], inc="ve")
                    bA2["free"] = last_sc
            bDn["free"] = tdn

        t_all = P.emit("ve", VNOP(), inc="ve")
        for lc in range(4):
            ci = st * 4 + lc
            P.emit("sy", DMA(attn_out_d.ap()[ci * 128:(ci + 1) * 128, :],
                             acc[:, lc * 2048:(lc + 1) * 2048]),
                   waits=[t_all], inc="qs", inc_n=16)
        acc_war = ("qs", P.ctr["qs"])
        prev_te_end = P.emit("te", NOP(), inc="te")

    attn_store_ctr = P.ctr["qs"]

    if attn_only:
        P.emit("sy", NOP(), waits=[("qs", P.ctr["qs"])])
        _realize(nc, P)
        return nc

    barrier()

    P.emit("sy", DMA(ebias_s[:, :], ebias_d.ap()), inc="qc2s", inc_n=16)
    P.emit("gp", DMA(rwT[:, :], rw_d.ap()), inc="qc2g", inc_n=16)
    P.emit("sy", DMA(Ll[:, :], ll_d.ap()), inc="qc2s", inc_n=16)
    P.emit("sy", DMA(ones1[0:1, :], ones1_d.ap()), inc="qc2s", inc_n=16)
    P.emit("sy", DMA(tid[:, :], tid_d.ap()), inc="qc2s", inc_n=16)
    P.emit("sy", DMA(eoffC[:, :], eoff_d.ap()), inc="qc2s", inc_n=16)
    t_eb = t_rw = [("qc2s", P.ctr["qc2s"]), ("qc2g", P.ctr["qc2g"])]
    t_onc = P.emit("ve", MEMSET(onesc[:, :], 1.0), inc="ve")
    oob_view = y_sp[:, 0:96].bitcast(I32)

    prev_y_store = ("qs", attn_store_ctr)
    prev_te_end2 = None
    prev_idx_load = None
    prev_act_te = None
    prev_sg = [None, None]
    tk_ring = Ring([0, 1, 2, 3])
    tkb_war = [None, None]
    ysp_ring = Ring([0, 1])

    dense_state = {"last_gu": None}

    def dense_branch(e, xn2_toks):
        nonlocal prev_act_te
        wi = wdr.i % 2
        wd_slot = wdr.next()
        t_wd = P.emit("gp", DMA(wd_slot["o"][:, :], dw_d.ap()[e]),
                      waits=[wd_slot["free"]], inc=f"qd{wi}", inc_n=16)
        act_toks = []
        for i in range(6):
            pair = []
            for m in (i, i + 6):
                gi = gur.i % 3
                slot = gur.next()
                t_w = P.emit("gp", DMA(slot["o"][:, :], gu_d.ap()[e, m]),
                             waits=[slot["free"]], inc=f"qgu{gi}", inc_n=16)
                bh = [pool.next(), pool.next()]
                tok = None
                for ki in range(16):
                    for half in range(2):
                        w = []
                        if ki == 0:
                            w = [t_w, bh[half]["free"]]
                            if half == 0:
                                w += xn2_toks
                        tok = P.emit("te", MM(
                            bh[half]["o"][:, :],
                            slot["o"][:, ki * 128:(ki + 1) * 128],
                            xn2T[:, ki * 1024 + half * 512:ki * 1024 + half * 512 + 512],
                            ki == 0, ki == 15),
                            waits=w,
                            inc="te" if (ki == 15 and half == 1) else None)
                slot["free"] = tok
                pair.append((bh, tok))
            dense_state["last_gu"] = pair[1][1]
            (bg, tokg), (bu, toku) = pair
            for half in range(2):
                t_sg = P.emit("se", ACT(sgbuf[:, half * 512:(half + 1) * 512],
                                        bg[half]["o"][:, :], AF.Silu),
                              waits=[tokg, prev_sg[half]], inc="se")
                bg[half]["free"] = t_sg
                t_act = P.emit("ve", TT_(
                    actB[:, i * 1024 + half * 512:i * 1024 + half * 512 + 512],
                    sgbuf[:, half * 512:(half + 1) * 512],
                    bu[half]["o"][:, :], ALU.mult),
                    waits=[t_sg, toku, prev_act_te], inc="ve")
                bu[half]["free"] = t_act
                prev_sg[half] = t_act
                act_toks.append(t_act)
        tok_dn = None
        for tc in range(8):
            for ncol in range(4):
                b = pool.next()
                for ki in range(6):
                    w = ([t_wd, b["free"]] + act_toks) if ki == 0 else []
                    tok_dn = P.emit("te", MM(
                        b["o"][:, :],
                        actB[:, ki * 1024 + tc * 128:ki * 1024 + tc * 128 + 128],
                        wd_slot["o"][:, ki * 2048 + ncol * 512:ki * 2048 + ncol * 512 + 512],
                        ki == 0, ki == 5),
                        waits=w, inc="te" if ki == 5 else None)
                dst = y_acc[:, tc * 2048 + ncol * 512:tc * 2048 + ncol * 512 + 512]
                t_dr = P.emit("ve", STT(dst, b["o"][:, :], 1.0, dst,
                                        ALU.mult, ALU.add),
                              waits=[tok_dn], inc="ve")
                b["free"] = t_dr
        wd_slot["free"] = tok_dn
        prev_act_te = tok_dn

    prev_ys = []
    for p in range(2):
        t_ms = P.emit("ve", MEMSET(oob_view, 1 << 20),
                      waits=prev_ys, inc="ve")
        t_fill = P.emit("sy", DMA(
            idxcw_d.ap().rearrange("(p f) j -> p (f j)", p=128),
            oob_view), waits=[prev_idx_load, t_ms], inc="qif", inc_n=16)
        fill_tok = ("qif", P.ctr["qif"])
        for _sl in ysp_ring.slots:
            _sl["free"] = fill_tok

        xn2_toks = []
        for c in range(8):
            ci = p * 8 + c
            t_x = P.emit("sy", DMA(y_acc[:, c * 2048:(c + 1) * 2048],
                                   x1_d.ap()[ci * 128:(ci + 1) * 128, :]),
                         waits=[prev_y_store] if c == 0 else [],
                         inc=f"qy{c}", inc_n=16)
            ts2 = ln_stats(y_acc[:, c * 2048:(c + 1) * 2048],
                           m2[:, c:c + 1], r2[:, c:c + 1], [t_x, prev_te_end2])
            t_n = P.emit("ve", TS(xn2c[:, :], y_acc[:, c * 2048:(c + 1) * 2048],
                                  m2[:, c:c + 1], r2[:, c:c + 1],
                                  ALU.subtract, ALU.mult),
                         waits=[ts2, tr_state["last_te"]], inc="ve")
            P.emit("sy", DMA(
                xn2n_d.ap()[p * 1024 + c * 128:p * 1024 + (c + 1) * 128, :],
                xn2c[:, :]), waits=[t_n], inc="qxn", inc_n=16)
            xn2_toks.append(transpose_to_T(xn2c[:, :], xn2T, c * 128, 1024,
                                           [t_n]))
        xn_ready = ("qxn", P.ctr["qxn"])

        rt_toks = []
        for c in range(8):
            b = pool.next()
            tok = None
            for ki in range(16):
                w = ([t_rw, b["free"]] + xn2_toks) if ki == 0 else []
                tok = P.emit("te", MM(
                    b["o"][:, 0:16],
                    xn2T[:, ki * 1024 + c * 128:ki * 1024 + c * 128 + 128],
                    rwT[:, ki * 16:(ki + 1) * 16], ki == 0, ki == 15),
                    waits=w, inc="te" if ki == 15 else None)
            rt_toks.append((b, tok))

        topk_toks = []
        prev_ft_read = None
        for c in range(8):
            b, tok = rt_toks[c]
            t_ex = P.emit("se", ACT(ft[:, 0:16], b["o"][:, 0:16], AF.Exp,
                                    scale=-1.0), waits=[tok, prev_ft_read],
                          inc="se")
            b["free"] = t_ex
            u = P.emit("ve", TS(ft[:, 16:32], ft[:, 0:16], 1.0, None, ALU.add),
                       waits=[t_ex], inc="ve")
            prev_ft_read = u
            u = P.emit("ve", RECIP(scores[:, c * 16:(c + 1) * 16],
                                   ft[:, 16:32]), waits=[u], inc="ve")
            u = P.emit("ve", TT_(ft[:, 16:32], scores[:, c * 16:(c + 1) * 16],
                                 ebias_s[:, :], ALU.add), waits=[t_eb, u],
                       inc="ve")
            u2 = P.emit("ve", MEMSET(combine[:, c * 16:(c + 1) * 16], 0.0),
                        inc="ve")
            for k in range(4):
                u = P.emit("ve", RED(ft[:, 32:33], ft[:, 16:32], ALU.max),
                           waits=[u], inc="ve")
                u = P.emit("ve", TS(ft[:, 48:64], ft[:, 16:32], ft[:, 32:33],
                                    None, ALU.is_equal), waits=[u], inc="ve")
                u = P.emit("ve", TT_(ft[:, 64:80], ft[:, 48:64],
                                     scores[:, c * 16:(c + 1) * 16], ALU.mult),
                           waits=[u], inc="ve")
                u = P.emit("ve", TT_(combine[:, c * 16:(c + 1) * 16],
                                     combine[:, c * 16:(c + 1) * 16],
                                     ft[:, 64:80], ALU.add),
                           waits=[u, u2], inc="ve")
                u = P.emit("ve", STT(ft[:, 16:32], ft[:, 48:64], NEG_BIG,
                                     ft[:, 16:32], ALU.mult, ALU.add),
                           waits=[u], inc="ve")
            u = P.emit("ve", RED(ft[:, 33:34], combine[:, c * 16:(c + 1) * 16],
                                 ALU.add), waits=[u], inc="ve")
            u = P.emit("ve", TS(ft[:, 34:35], ft[:, 33:34], 1e-20, None,
                                ALU.add), waits=[u], inc="ve")
            u = P.emit("ve", RECIP(ft[:, 35:36], ft[:, 34:35]), waits=[u],
                       inc="ve")
            tk = P.emit("ve", TS(combine[:, c * 16:(c + 1) * 16],
                                 combine[:, c * 16:(c + 1) * 16],
                                 ft[:, 35:36], None, ALU.mult), waits=[u],
                        inc="ve")
            topk_toks.append(tk)

        dense_branch(16, xn2_toks)
        t_shared_done = P.emit("ve", VNOP(), inc="ve")

        for c in range(8):
            ci = p * 8 + c
            P.emit("sy", DMA(y_d.ap()[ci * 128:(ci + 1) * 128, :],
                             y_acc[:, c * 2048:(c + 1) * 2048]),
                   waits=[t_shared_done], inc="qs", inc_n=16)
        y_base_tok = ("qs", P.ctr["qs"])
        prev_y_store = y_base_tok
        t_m = P.emit("ve", TS(m_bf[:, :], combine[:, :], 0.0, None, ALU.is_gt),
                     waits=topk_toks + [t_onc], inc="ve")
        bS = pool.next()
        tS = P.emit("te", MM(bS["o"][0:1, 0:128], onesc[:, 0:1], m_bf[:, :],
                             True, True), waits=[t_m, bS["free"]], inc="te")
        t_r1 = P.emit("ve", CP(trow0[0:1, :], bS["o"][0:1, 0:128]),
                      waits=[tS], inc="ve")
        bS["free"] = t_r1
        u = P.emit("ve", MEMSET(trow1[0:1, :], 0.0), waits=[t_r1], inc="ve")
        u = P.emit("ve", CP(trow1[0:1, 16:128], trow0[0:1, 0:112]),
                   waits=[u], inc="ve")
        bufs = [trow1, trow0]
        for step, sh in enumerate((16, 32, 64)):
            src = bufs[step % 2]
            dst = bufs[(step + 1) % 2]
            u = P.emit("ve", CP(dst[0:1, 0:sh], src[0:1, 0:sh]), waits=[u],
                       inc="ve")
            u = P.emit("ve", TT_(dst[0:1, sh:128], src[0:1, sh:128],
                                 src[0:1, 0:128 - sh], ALU.add), waits=[u],
                       inc="ve")
        pref_buf = trow0
        t_pref = u
        bP = pool.next()
        P.emit("te", MM(bP["o"][:, 0:128], Ll[:, :], m_bf[:, :], True, False),
               waits=[t_m, bP["free"], t_eb], inc=None)
        tP = P.emit("te", MM(bP["o"][:, 0:128], ones1[0:1, :],
                             pref_buf[0:1, :], False, True),
                    waits=[t_pref], inc="te")
        t_Pn = P.emit("ve", CP(Pn[:, :], bP["o"][:, 0:128]), waits=[tP],
                      inc="ve")
        bP["free"] = t_Pn

        for c in range(8):
            cs = slice(c * 16, (c + 1) * 16)
            u = P.emit("ve", TT_(ft[:, 0:16], eoffC[:, :], Pn[:, cs],
                                 ALU.add), waits=[t_Pn], inc="ve")
            u = P.emit("ve", TS(ft[:, 80:96], Pn[:, cs], CEXP - 0.5, None,
                                ALU.is_ge), waits=[u], inc="ve")
            u = P.emit("ve", STT(ft[:, 0:16], ft[:, 80:96], 1.0e6, ft[:, 0:16],
                                 ALU.mult, ALU.add), waits=[u], inc="ve")
            u = P.emit("ve", TT_(ft[:, 16:32], scores[:, cs],
                                 ebias_s[:, :], ALU.add), waits=[u], inc="ve")
            for k in range(4):
                u = P.emit("ve", RED(ft[:, 32:33], ft[:, 16:32], ALU.max),
                           waits=[u], inc="ve")
                u = P.emit("ve", TS(ft[:, 48:64], ft[:, 16:32], ft[:, 32:33],
                                    None, ALU.is_equal), waits=[u], inc="ve")
                si = tk_ring.next()
                sidx = si["o"]
                u = P.emit("ve", TT_(ft[:, 64:80], ft[:, 48:64], ft[:, 0:16],
                                     ALU.mult), waits=[u, si["free"]],
                           inc="ve")
                u = P.emit("ve", RED(posf[:, sidx:sidx + 1], ft[:, 64:80],
                                     ALU.add), waits=[u], inc="ve")
                u = P.emit("ve", TT_(ft[:, 64:80], ft[:, 48:64],
                                     combine[:, cs], ALU.mult),
                           waits=[u], inc="ve")
                u = P.emit("ve", RED(cwf[:, sidx:sidx + 1], ft[:, 64:80],
                                     ALU.add), waits=[u], inc="ve")
                u = P.emit("ve", CP(posi[:, sidx:sidx + 1],
                                    posf[:, sidx:sidx + 1]), waits=[u],
                           inc="ve")
                u = P.emit("ve", TS(tk2[:, 2 * sidx:2 * sidx + 1],
                                    tid[:, c:c + 1], float(p * 1024), None,
                                    ALU.add), waits=[u], inc="ve")
                u = P.emit("ve", CP(
                    tk2[:, 2 * sidx + 1:2 * sidx + 2],
                    cwf[:, sidx:sidx + 1].bitcast(I32)), waits=[u], inc="ve")
                o_ap = idxcw_d.ap()
                off_ap = posi[:, sidx:sidx + 1]
                in_ap = tk2[:, 2 * sidx:2 * sidx + 2]
                tsc = P.emit("gp", (lambda oa=o_ap, fa=off_ap, ia=in_ap:
                             (lambda g: g.indirect_dma_start(
                                 oa, IndirectOffsetOnAxis(fa, 0), ia, None)))(),
                             waits=[u, fill_tok], inc=f"qsc{sidx}", inc_n=16)
                si["free"] = tsc
                u = P.emit("ve", STT(ft[:, 16:32], ft[:, 48:64], NEG_BIG,
                                     ft[:, 16:32], ALU.mult, ALU.add),
                           waits=[u], inc="ve")
        scatters_done = [(f"qsc{i}", P.ctr.get(f"qsc{i}", 0))
                         for i in range(4)]

        prev_gu_read = None
        for e in range(16):
            par = e % 2
            tkb_cur = tksb[par]
            for cc3 in range(3):
                P.emit("sy", DMA(
                    tkb_cur[:, 2 * cc3:2 * cc3 + 2],
                    idxcw_d.ap()[e * CEXP + cc3 * 128:e * CEXP + (cc3 + 1) * 128, :]),
                    waits=(scatters_done + [tkb_war[par]]) if cc3 == 0 else [],
                    inc=f"qi{par}", inc_n=16)
            tkb_tok = (f"qi{par}", P.ctr[f"qi{par}"])
            tcl = None
            for sc in range(3):
                tcl = P.emit("ve", TS(ixc[par][sc][:, 0:1],
                                      tkb_cur[:, 2 * sc:2 * sc + 1],
                                      float(p * 1024 + 1023), None, ALU.min),
                             waits=[tkb_tok] if sc == 0 else [], inc="ve")
            g_toks = []
            for sc in range(3):
                o_ap = xe_nat[:, sc * 2048:(sc + 1) * 2048]
                in_ap = xn2n_d.ap()
                off_ap = ixc[par][sc][:, 0:1]
                tg = P.emit("gp", (lambda oa=o_ap, ia=in_ap, fa=off_ap:
                            (lambda g: g.indirect_dma_start(
                                oa, None, ia, IndirectOffsetOnAxis(fa, 0))))(),
                            waits=[tcl, xn_ready, prev_gu_read,
                                   dense_state["last_gu"] if e == 0 else None],
                            inc=f"qga{sc}", inc_n=16)
                g_toks.append((f"qga{sc}", P.ctr[f"qga{sc}"]))
            tr_last = None
            for sc in range(3):
                for kk in range(16):
                    bt = ptb.next()
                    tr_ = P.emit("te", TR(
                        bt["o"][:, 0:128],
                        xe_nat[:, sc * 2048 + kk * 128:sc * 2048 + kk * 128 + 128],
                        idn16[:, :]),
                        waits=[g_toks[sc], bt["free"]], inc="te")
                    dcol = kk * CEXP + sc * 128
                    o_ap = xeT[:, dcol:dcol + 128]
                    i_ap = bt["o"][:, 0:128]
                    tr_last = P.emit("se", (lambda oa=o_ap, ia=i_ap:
                                     (lambda s: s.copy(oa, ia)))(),
                                     waits=[tr_], inc="se")
                    bt["free"] = tr_last
            xeT_ready = tr_last
            last_tr_te = tr_

            wi = wdr.i % 2
            wd_slot = wdr.next()
            t_wd = P.emit("gp", DMA(wd_slot["o"][:, :], dw_d.ap()[e]),
                          waits=[wd_slot["free"]], inc=f"qd{wi}", inc_n=16)
            act_toks = []
            tok = None
            for i in range(6):
                pair = []
                for m in (i, i + 6):
                    gi = gur.i % 3
                    slot = gur.next()
                    t_w = P.emit("gp", DMA(slot["o"][:, :], gu_d.ap()[e, m]),
                                 waits=[slot["free"]], inc=f"qgu{gi}",
                                 inc_n=16)
                    b = pool.next()
                    for ki in range(16):
                        w = ([t_w, b["free"], xeT_ready] if ki == 0 else [])
                        tok = P.emit("te", MM(
                            b["o"][:, 0:CEXP],
                            slot["o"][:, ki * 128:(ki + 1) * 128],
                            xeT[:, ki * CEXP:(ki + 1) * CEXP],
                            ki == 0, ki == 15),
                            waits=w, inc="te" if ki == 15 else None)
                    slot["free"] = tok
                    pair.append((b, tok))
                (bg, tokg), (bu, toku) = pair
                t_sg = P.emit("se", ACT(sgbuf[:, 0:CEXP], bg["o"][:, 0:CEXP],
                                        AF.Silu),
                              waits=[tokg, prev_sg[0]], inc="se")
                bg["free"] = t_sg
                t_act = P.emit("ve", TT_(
                    actB[:, i * CEXP:(i + 1) * CEXP], sgbuf[:, 0:CEXP],
                    bu["o"][:, 0:CEXP], ALU.mult),
                    waits=[t_sg, toku, prev_act_te], inc="ve")
                bu["free"] = t_act
                prev_sg[0] = t_act
                act_toks.append(t_act)
            prev_gu_read = tok

            tok_dn = None
            last_ys = None
            for sc in range(3):
                ysl = ysp_ring.next()
                ypar = ysl["o"]
                drains = []
                for ncol in range(4):
                    b = pool.next()
                    for ki in range(6):
                        w = ([t_wd, b["free"]] + act_toks) if ki == 0 else []
                        tok_dn = P.emit("te", MM(
                            b["o"][:, :],
                            actB[:, ki * CEXP + sc * 128:ki * CEXP + sc * 128 + 128],
                            wd_slot["o"][:, ki * 2048 + ncol * 512:ki * 2048 + ncol * 512 + 512],
                            ki == 0, ki == 5),
                            waits=w, inc="te" if ki == 5 else None)
                    t_dr = P.emit("ve", TS(
                        y_sp[:, ypar * 2048 + ncol * 512:ypar * 2048 + ncol * 512 + 512],
                        b["o"][:, :],
                        tkb_cur[:, 2 * sc + 1:2 * sc + 2].bitcast(F32),
                        None, ALU.mult),
                        waits=[tok_dn, ysl["free"]], inc="ve")
                    drains.append(t_dr)
                    b["free"] = t_dr
                o_ap = y_d.ap()
                off_ap = tkb_cur[:, 2 * sc:2 * sc + 1]
                i_ap = y_sp[:, ypar * 2048:(ypar + 1) * 2048]
                last_ys = P.emit("gp", (lambda oa=o_ap, fa=off_ap, ia=i_ap:
                                 (lambda g: g.indirect_dma_start(
                                     oa, IndirectOffsetOnAxis(fa, 0), ia, None,
                                     compute_op=ALU.add)))(),
                                 waits=drains + [y_base_tok],
                                 inc=f"qys{ypar}", inc_n=16)
                ysl["free"] = last_ys
            wd_slot["free"] = tok_dn
            prev_act_te = tok_dn
            tkb_war[par] = last_ys

        prev_idx_load = [(f"qi{i}", P.ctr.get(f"qi{i}", 0)) for i in range(2)]
        prev_ys = [(f"qys{i}", P.ctr.get(f"qys{i}", 0)) for i in range(2)]
        prev_te_end2 = prev_act_te

    fin = [(k, v) for k, v in P.ctr.items() if k.startswith("q")]
    P.emit("sy", NOP(), waits=fin)
    P.emit("gp", NOP(), waits=[t for t in fin if t[0].startswith("qys")])
    _realize(nc, P)
    return nc


def _realize(nc, P):
    import contextlib
    with contextlib.ExitStack() as ctx:
        sems = {s: ctx.enter_context(nc.semaphore(s))
                for s in sorted(P.ctr.keys())}
        block = ctx.enter_context(nc.Block())

        def make(eng_name):
            ops = P.ops[eng_name]
            seen = {}

            def body(eng):
                for fn, waits, inc in ops:
                    for s, v in waits:
                        if seen.get(s, -1) < v:
                            eng.wait_ge(sems[s], v)
                            seen[s] = v
                    ins = fn(eng)
                    if inc is not None:
                        ins.then_inc(sems[inc[0]], inc[1])
            return body

        for eng_name, attr in (("te", "tensor"), ("ve", "vector"),
                               ("se", "scalar"), ("sy", "sync"),
                               ("gp", "gpsimd")):
            getattr(block, attr)(make(eng_name))


def prep_inputs(L0_attn_Wq, L0_attn_Wk, L0_attn_Wv, L0_attn_memory,
                L0_attn_memory_norm, L0_attn_memory_weight, L0_ffn_router_w,
                L0_ffn_e_bias, L0_ffn_gate_up_proj, L0_ffn_down_proj,
                L0_ffn_shared_gate_w, L0_ffn_shared_up_w, L0_ffn_shared_down_w):
    f = np.float32
    wq = _lhsT_blocks(np.asarray(L0_attn_Wq, f))
    wk = _lhsT_blocks(np.asarray(L0_attn_Wk, f))
    wv = _rhs_layout(np.asarray(L0_attn_Wv, f))
    mem = _lhsT_blocks(np.asarray(L0_attn_memory, f).T)
    memn = np.ascontiguousarray(
        np.asarray(L0_attn_memory_norm, f).reshape(4, 128, 1)
        .transpose(1, 0, 2).reshape(128, 4))
    mw = np.ascontiguousarray(
        np.broadcast_to(np.asarray(L0_attn_memory_weight, f).reshape(1, 1),
                        (128, 1)))
    ebias = np.ascontiguousarray(
        np.broadcast_to(np.asarray(L0_ffn_e_bias, f).reshape(1, 16), (128, 16)))
    rw = _rhs_layout(np.asarray(L0_ffn_router_w, f))
    gup = np.asarray(L0_ffn_gate_up_proj, f)
    dp = np.asarray(L0_ffn_down_proj, f)
    sh_gu = np.concatenate([np.asarray(L0_ffn_shared_gate_w, f),
                            np.asarray(L0_ffn_shared_up_w, f)], axis=0)
    gu = np.stack([_lhsT_blocks(gup[e]) for e in range(16)]
                  + [_lhsT_blocks(sh_gu)])
    dw = np.stack([_rhs_layout(dp[e]) for e in range(16)]
                  + [_rhs_layout(np.asarray(L0_ffn_shared_down_w, f))])
    idn16 = np.eye(128, dtype=ml_dtypes.bfloat16)
    idn32 = np.eye(128, dtype=np.float32)
    ll = np.triu(np.ones((128, 128), np.float32), 1).astype(ml_dtypes.bfloat16)
    ones1 = np.ones((1, 128), np.float32)
    tid = np.ascontiguousarray(
        (np.arange(8)[None, :] * 128 + np.arange(128)[:, None]).astype(np.int32))
    eoff = np.ascontiguousarray(np.broadcast_to(
        (np.arange(16) * CEXP).astype(np.float32)[None, :], (128, 16)))
    return {"wq": wq, "wk": wk, "wv": wv, "mem": mem, "memn": memn, "mw": mw,
            "ebias": ebias, "rw": rw, "gu": gu, "dw": dw,
            "idn16": idn16, "idn32": idn32, "ll": ll, "ones1": ones1,
            "tid": tid, "eoff": eoff}


def kernel(L0_attn_x, **w):
    common = prep_inputs(**w)
    x = np.ascontiguousarray(np.asarray(L0_attn_x, np.float32))
    nc = build_nc(attn_only=False)
    in_maps = [dict(common, x=np.ascontiguousarray(x[c])) for c in range(8)]
    res = run_bass_kernel_spmd(nc, in_maps, core_ids=list(range(8)))
    return np.stack([np.asarray(r["y"], np.float32) for r in res.results])
